# version 03
# baseline (speedup 1.0000x reference)
import numpy as np
import ml_dtypes
from contextlib import ExitStack

import concourse.bass as bass
import concourse.mybir as mybir
from concourse.bass_utils import run_bass_kernel_spmd

F32 = mybir.dt.float32
BF16 = mybir.dt.bfloat16
U8 = mybir.dt.uint8
AF = mybir.ActivationFunctionType
ALU = mybir.AluOpType
AX = mybir.AxisListType.X

B, S, D = 4, 4096, 1024
T = 2048
NI = D // 128
TC = 512
NTC_T = T // TC
NTC_S = S // TC
R2 = float(1.5 * 2.0 ** 23)
NEG = -30000.0

N_LAYERS = 4
NSUBS = 99
L2CUT = 31
F2CUT = 63
F2P1 = 15
TAPS = {}


def lhsT_tiles(w):
    K, M = w.shape
    return np.ascontiguousarray(
        w.reshape(K // 128, 128, M // 128, 128).transpose(2, 0, 1, 3))


def _stage_core(c, I):
    h = c % 2
    hs = slice(512 * h, 512 * h + 512)
    st = {}

    def bt(v):
        return np.ascontiguousarray(v.reshape(-1, 128).T)

    st["l0_wq"] = lhsT_tiles(I["L0_attn_Wq"].T[:, hs].copy())
    st["l0_wk"] = lhsT_tiles(I["L0_attn_Wk"].T[:, hs].copy())
    st["l0_wv"] = lhsT_tiles(I["L0_attn_Wv"].T[:, hs].copy())
    st["l0_wo"] = lhsT_tiles(np.ascontiguousarray(I["L0_attn_Wo"].T))
    st["l0_wfc_t"] = np.ascontiguousarray(I["L0_ffn_W_fc"].T)
    st["l0_bfc"] = bt(I["L0_ffn_b_fc"])
    st["l0_afc_t"] = np.ascontiguousarray(I["L0_ffn_A_fc"].T)
    st["l0_bfc_l"] = np.ascontiguousarray(I["L0_ffn_B_fc"].T)
    st["l0_wpr_t"] = np.ascontiguousarray(I["L0_ffn_W_proj"].T)
    st["l0_bpr"] = bt(I["L0_ffn_b_proj"])
    st["l0_apr_t"] = np.ascontiguousarray(I["L0_ffn_A_proj"].T)
    st["l0_bpr_l"] = np.ascontiguousarray(I["L0_ffn_B_proj"].T)

    st["l1_wq"] = lhsT_tiles(I["L1_attn_Wq"][:, hs].copy())
    st["l1_wk"] = lhsT_tiles(I["L1_attn_Wk"][:, hs].copy())
    st["l1_wv"] = lhsT_tiles(I["L1_attn_Wv"][:, hs].copy())
    st["l1_wo"] = lhsT_tiles(I["L1_attn_Wo"].copy())
    st["l1_bq"] = bt(I["L1_attn_bq"][hs])
    st["l1_bk"] = bt(I["L1_attn_bk"][hs])
    st["l1_bv_row"] = np.ascontiguousarray(I["L1_attn_bv"][None, hs])
    st["l1_bo"] = bt(I["L1_attn_bo"])
    st["l1_gw"] = np.ascontiguousarray(I["L1_ffn_gate_w"])
    st["l1_gb"] = np.ascontiguousarray(I["L1_ffn_gate_b"][None, :])
    st["l1_ew"] = np.ascontiguousarray(
        np.stack([lhsT_tiles(I["L1_ffn_expert_w"][e]) for e in range(8)]))
    st["l1_eb"] = np.ascontiguousarray(I["L1_ffn_expert_b"])

    W = I["L2_attn_W_qkv"]
    st["l2_wqkv"] = lhsT_tiles(np.ascontiguousarray(
        np.concatenate([W[:, 512 * h:512 * h + 512],
                        W[:, 1024 + 512 * h:1024 + 512 * h + 512],
                        W[:, 2048 + 512 * h:2048 + 512 * h + 512]], axis=1)))
    wf2 = np.zeros((128, 128), np.float32)
    wf2[0:64, 0:64] = I["L2_attn_W_f"]
    wf2[64:128, 64:128] = I["L2_attn_W_f"]
    st["l2_wf2"] = wf2
    bf2 = np.concatenate([I["L2_attn_b_f"], I["L2_attn_b_f"]])
    st["l2_bf2c"] = np.ascontiguousarray(bf2[:, None])
    st["l2_bf2r"] = np.ascontiguousarray(bf2[None, :])
    st["l2_wout"] = lhsT_tiles(I["L2_attn_W_out"].copy())
    st["l2_bout"] = bt(I["L2_attn_b_out"])
    st["l2_wg"] = np.ascontiguousarray(I["L2_ffn_w_gate"])
    st["l2_wu"] = np.ascontiguousarray(I["L2_ffn_w_up"])
    st["l2_wd"] = np.ascontiguousarray(I["L2_ffn_w_down"])
    st["l2_wg_t"] = np.ascontiguousarray(I["L2_ffn_w_gate"].T)
    st["l2_wu_t"] = np.ascontiguousarray(I["L2_ffn_w_up"].T)
    st["l2_wd_t"] = np.ascontiguousarray(I["L2_ffn_w_down"].T)

    st["l3_wqa"] = np.ascontiguousarray(I["L3_attn_Wqa"])
    st["l3_wqb"] = np.ascontiguousarray(I["L3_attn_Wqb"][:, hs])
    st["l3_wkva"] = np.ascontiguousarray(I["L3_attn_Wkva"])
    kvb = I["L3_attn_Wkvb"]
    cols_k = [((kh * 64 + d) * 2 + 0) for kh in (2 * h, 2 * h + 1) for d in range(64)]
    cols_v = [((kh * 64 + d) * 2 + 1) for kh in (2 * h, 2 * h + 1) for d in range(64)]
    st["l3_wkvb"] = np.ascontiguousarray(kvb[:, cols_k + cols_v])
    st["l3_wout"] = lhsT_tiles(I["L3_attn_Wout"].copy())
    st["l3_bout"] = bt(I["L3_attn_bout"])
    st["l3_cn_w1"] = lhsT_tiles(I["L3_ffn_cn_w1"].copy())
    st["l3_cn_w2"] = lhsT_tiles(I["L3_ffn_cn_w2"].copy())
    st["l3_cn_w3"] = lhsT_tiles(I["L3_ffn_cn_w3"].copy())
    st["l3_cn_b1"] = bt(I["L3_ffn_cn_b1"])
    st["l3_cn_b2"] = bt(I["L3_ffn_cn_b2"])
    st["l3_cn_b3"] = bt(I["L3_ffn_cn_b3"])
    st["l3_cn_w4"] = np.ascontiguousarray(I["L3_ffn_cn_w4"].reshape(2, 128).T)
    st["l3_cn_b4"] = np.ascontiguousarray(I["L3_ffn_cn_b4"][None, :])
    st["l3_gc_w1"] = lhsT_tiles(I["L3_ffn_gc_w1"].copy())
    st["l3_gc_w2"] = lhsT_tiles(I["L3_ffn_gc_w2"].copy())
    st["l3_gc_w3"] = np.ascontiguousarray(I["L3_ffn_gc_w3"].reshape(2, 128).T)
    st["l3_gc_b1"] = bt(I["L3_ffn_gc_b1"])
    st["l3_gc_b2"] = bt(I["L3_ffn_gc_b2"])
    st["l3_gc_b3"] = np.ascontiguousarray(I["L3_ffn_gc_b3"][None, :])
    st["l3_ar"] = np.ascontiguousarray(I["L3_ffn_adaptive_range"][None, :])

    b = c // 2
    st["x_in"] = np.ascontiguousarray(I["L0_attn_x"][b, 2048 * h:2048 * h + 2048, :].T)
    return st


def _constants():
    co = {}
    co["ones_bf"] = np.ones((128, 128), ml_dtypes.bfloat16)
    co["ones_f32"] = np.ones((1, 128), np.float32)
    co["ident_bf"] = np.eye(128, dtype=ml_dtypes.bfloat16)
    co["ident_f32"] = np.eye(128, dtype=np.float32)
    pos = np.arange(S, dtype=np.float64)
    j = (np.arange(128) % 64) // 2
    inv = 10000.0 ** (-2.0 * j / 64.0)
    ang = pos[None, :] * inv[:, None]
    co["rope_cos"] = np.cos(ang).astype(ml_dtypes.bfloat16)
    co["rope_sin"] = np.sin(ang).astype(ml_dtypes.bfloat16)
    Rm = np.zeros((128, 128), np.float32)
    for blk in range(2):
        for i in range(32):
            a = blk * 64 + 2 * i
            Rm[a, a + 1] = -1.0
            Rm[a + 1, a] = 1.0
    co["ropeR_t"] = np.ascontiguousarray(Rm.T).astype(ml_dtypes.bfloat16)
    cm = np.zeros((128, 4, 512), np.float32)
    for ridx, r in enumerate((0, 128, 256, 384)):
        kk = np.arange(128)[:, None] + r
        qq = np.arange(512)[None, :]
        cm[:, ridx, :][kk > qq] = NEG
    co["cmask"] = cm.astype(ml_dtypes.bfloat16)
    m = np.zeros((128, 128), np.float32)
    kb_ = np.arange(128)[:, None] // 64
    qb_ = np.arange(128)[None, :] // 64
    m[kb_ != qb_] = NEG
    co["l1_dmask"] = m.astype(ml_dtypes.bfloat16)
    gp = np.linspace(0, S - 1, 8).astype(np.int64)
    gm = np.zeros((8, S), np.float32)
    qblk = np.arange(S) // 64
    for gi, g in enumerate(gp):
        gm[gi, qblk == (g // 64)] = NEG
    co["l1_gmask"] = gm.astype(ml_dtypes.bfloat16)
    ind8 = np.zeros((8, 8, 128), np.float32)
    for e in range(8):
        ind8[e, e, :] = 1.0
    co["ind8"] = ind8.reshape(8, 8 * 128).astype(ml_dtypes.bfloat16)
    b4 = np.zeros((4, 128), np.float32)
    for k in range(4):
        b4[k, 32 * k:32 * k + 32] = 1.0
    co["blkind4"] = b4
    return co


L1_GP = np.linspace(0, S - 1, 8).astype(np.int64)
GP0, GPSTEP = int(L1_GP[0]), int(L1_GP[1] - L1_GP[0])


class KB:
    def __init__(self, nc, st_shapes, taps):
        self.nc = nc
        self.es = ExitStack()
        es = self.es
        self.count = {}
        self.sems = {}
        self.waited = {}
        self.insts = []
        self.H = {}
        self.active = None
        for name in ("gp", "sy", "te", "ve", "se", "cc"):
            self.sems[name] = es.enter_context(nc.semaphore(name))
            self.count[name] = 0

        self.din = {n: nc.dram_tensor(n, list(s), F32, kind="ExternalInput")
                    for n, s in st_shapes.items()}
        self.dout = nc.dram_tensor("out", [D, T], F32, kind="ExternalOutput")
        co = _constants()
        self.dco = {n: nc.inline_tensor(np.ascontiguousarray(a), f"co_{n}")
                    for n, a in co.items()}
        self.tapd = {n: nc.dram_tensor(f"tap_{n}", list(s), F32, kind="ExternalOutput")
                     for n, s in taps.items()}

        self.ag_in = nc.dram_tensor("ag_in", [D, T], BF16, kind="Internal")
        self.ag_out = nc.dram_tensor("ag_out", [2, D, T], BF16, kind="Internal")
        self.hx_in = nc.dram_tensor("hx_in", [512, S], BF16, kind="Internal")
        self.hx_out = nc.dram_tensor("hx_out", [2, 512, S], BF16, kind="Internal")
        self.armx_in = nc.dram_tensor("armx_in", [1, 8], F32, kind="Internal")
        self.armx_out = nc.dram_tensor("armx_out", [1, 8], F32, kind="Internal",
                                       addr_space="Shared")
        self.ctx_in = nc.dram_tensor("ctx_in", [1, D], F32, kind="Internal")
        self.ctx_out = nc.dram_tensor("ctx_out", [1, D], F32, kind="Internal")
        self.fdr = {
            "up": nc.dram_tensor("fdr_up", [24, 128, T], BF16, kind="Internal"),
            "gate": nc.dram_tensor("fdr_gate", [24, 128, T], BF16, kind="Internal"),
        }
        self.wqd = {
            "wu": nc.dram_tensor("wq_wu", [24, 8, 128, 128], BF16, kind="Internal"),
            "wg": nc.dram_tensor("wq_wg", [24, 8, 128, 128], BF16, kind="Internal"),
            "wd": nc.dram_tensor("wq_wd", [8, 24, 128, 128], BF16, kind="Internal"),
            "fc": nc.dram_tensor("wq_fc", [32, 8, 128, 128], BF16, kind="Internal"),
            "pr": nc.dram_tensor("wq_pr", [8, 32, 128, 128], BF16, kind="Internal"),
        }
        self.PAIRS = [[0, 1], [2, 3], [4, 5], [6, 7]]
        self.ALL8 = [[0, 1, 2, 3, 4, 5, 6, 7]]

        sbt = lambda nm, shp, dt: es.enter_context(nc.sbuf_tensor(nm, shp, dt))
        self.x = sbt("x", [128, NI, T], F32)
        self.cast_c = sbt("cast_c", [128, NI, TC], BF16)
        self.sq_c = sbt("sq_c", [128, NI, TC], BF16)
        self.tmp_c = sbt("tmp_c", [128, NI, TC], BF16)
        self.ln_c = sbt("ln_c", [128, NI, TC], BF16)
        self.rows = sbt("rows", [1, 2048], F32)
        self.rhs = sbt("rhs", [128, 2, NI, TC], BF16)
        self.wsb = sbt("wsb", [128, 2, 16, 128], BF16)
        self.onesb = sbt("onesb", [128, 128], BF16)
        self.onesf = sbt("onesf", [1, 128], F32)
        self.identb = sbt("identb", [128, 128], BF16)
        self.ropeRt = sbt("ropeRt", [128, 128], BF16)
        self.cmaskb = sbt("cmaskb", [128, 4, 512], BF16)
        self.dmaskb = sbt("dmaskb", [128, 128], BF16)
        self.oevac = sbt("oevac", [128, 512], F32)
        self.scl = sbt("scl", [128, 16], F32)
        self.colA = sbt("colA", [128, 32], F32)
        self.bias_sb = sbt("bias_sb", [128, 40], F32)
        self.rowbf = sbt("rowbf", [64, 4096], BF16)
        AR = 64 * 1024
        self.arena = sbt("arena", [128, AR], U8)
        a0 = nc.lookup_mloc(self.arena).addr

        def at(nm, shp, dt, off):
            return nc.alloc_sbuf_tensor_at(nm, shp, dt, offset=a0 + off)

        self.cosb = at("cosb", [128, S], BF16, 0)
        self.sinb = at("sinb", [128, S], BF16, 8192)
        self.gmaskb = at("gmaskb", [8, S], BF16, 0)
        self.bfc_l = at("bfc_l", [16, 4096], BF16, 0)
        self.bpr_l = at("bpr_l", [16, 1024], BF16, 8192)
        self.afc = at("afc", [128, 8, 16], BF16, 10240)
        self.apr = at("apr", [128, 32, 16], BF16, 10496)
        self.ebs = at("ebs", [8, 1024], BF16, 0)
        self.ind8 = at("ind8", [8, 1024], BF16, 2048)
        self.gwsb = at("gwsb", [128, 8, 8], BF16, 4096)
        self.gbrow = at("gbrow", [1, 8], BF16, 4352)
        self.w8 = at("w8", [8, 512], BF16, 4608)
        self.wnbf = at("wnbf", [128, 8], BF16, 5632)
        self.l2wf2 = at("l2wf2", [128, 128], BF16, 0)
        self.l2bfc = at("l2bfc", [128, 1], F32, 256)
        self.l2bfr = at("l2bfr", [1, 128], BF16, 384)
        self.blk4 = at("blk4", [4, 128], F32, 640)
        self.stage4 = at("stage4", [4, 512], F32, 1152)
        self.kvw = at("kvw", [32, 256], BF16, 256)
        self.wqa = at("wqa", [128, 8, 32], BF16, 768)
        self.wqa2 = at("wqa2", [128, 8, 64], BF16, 1280)
        self.wqb = at("wqb", [64, 512], BF16, 2304)
        self.cnw4 = at("cnw4", [128, 2], BF16, 3328)
        self.qscr = at("qscr", [128, 512], F32, 13312)
        self.l1bv = at("l1bv", [1, 512], BF16, 5888)
        self.kglob = at("kglob", [128, 8], BF16, 6912)
        self.vglob = at("vglob", [40, 66], BF16, 7168)
        self.qglob = at("qglob", [128, 8], BF16, 7424)
        self.gsm = at("gsm", [128, 8], F32, 13312)
        self.gm2 = at("gm2", [128, 8], F32, 13824)
        self.wnf = at("wnf", [128, 8], F32, 14336)
        self.gcw3 = at("gcw3", [128, 2], BF16, 3392)
        self.wt_a = at("wt_a", [128, 512], F32, 16384)
        self.wt_b = at("wt_b", [128, 512], F32, 18432)
        self.wt2 = at("wt2", [128, 512], F32, 20480)
        self.wdq = at("wdq", [128, 512], BF16, 22528)
        self.smx = at("smx", [128, 96], F32, 23552)
        self.sS = at("sS", [128, 96], F32, 24064)
        self.sT = at("sT", [96, 3072], F32, 43008)
        self.lnbuf = at("lnbuf", [128, NI, T], BF16, 16384)
        self.aq = at("aq", [64, S], BF16, 55552)
        self.cnw1 = at("cnw1", [128, 8, 8, 128], BF16, 16384)
        self.cnw2 = at("cnw2", [128, 4, 8, 128], BF16, 32768)
        self.h1 = at("h1", [128, 8, TC], BF16, 40960)
        self.cnw3 = at("cnw3", [128, 2, 4, 128], BF16, 4096)
        self.h2 = at("h2", [128, 4, TC], BF16, 6144)
        self.h3 = at("h3", [128, 2, TC], BF16, 10240)
        self.qf = at("qf", [128, S], BF16, 16384)
        self.kf = at("kf", [128, S], BF16, 24576)
        self.vtm = at("vtm", [128, 32, 2, 66], BF16, 32768)
        self.outTg = at("outTg", [128, S], BF16, 41216)
        self.kva = at("kva", [32, S], BF16, 41216)
        self.expS = at("expS", [128, 4, 512], BF16, 49408)
        self.scr1 = at("scr1", [128, 512], BF16, 53504)
        self.scr2 = at("scr2", [128, 512], BF16, 54528)
        self.hid = at("hid", [128, 32, TC], BF16, 16384)
        self.oc32 = at("oc32", [128, NI, TC], F32, 49152)
        self.hu = at("hu", [128, 12, TC], BF16, 16384)
        self.hg = at("hg", [128, 12, TC], BF16, 28672)
        self.hs_ = at("hs_", [128, 12, TC], BF16, 40960)

        self.ps = [es.enter_context(nc.psum_tensor(f"ps{i}", [128, 512], F32))
                   for i in range(7)]
        self.pbf = es.enter_context(nc.psum_tensor("pbf", [128, 256], BF16))
        self.bank = [dict(w=None, r=[]) for _ in range(8)]
        self.bufdep = {}
        self.blk = nc.Block()
        self.blk.__enter__()
        self.h2048 = None
        self.x_last = None

    def ENG(self, n):
        return self.active == n

    def wait_on(self, name, *evs):
        if not self.ENG(name):
            return
        for e in evs:
            if e is None:
                continue
            s, v = e
            if v <= 0:
                continue
            key = (name, s)
            if self.waited.get(key, 0) >= v:
                continue
            self.waited[key] = v
            self.H[name].wait_ge(self.sems[s], v)
            self.insts.append((name, [(s, v)], []))

    def dma(self, q, out, in_, deps=()):
        self.wait_on(q, *deps)
        self.count[q] += 16
        if self.ENG(q):
            o = out() if callable(out) else out
            i = in_() if callable(in_) else in_
            self.H[q].dma_start(out=o, in_=i).then_inc(self.sems[q], 16)
            self.insts.append((q, [], [(q, 16)]))
        return (q, self.count[q])

    def coll(self, kind, aop, groups, ins, outs, deps=()):
        self.wait_on("gp", *deps)
        self.count["cc"] += 1
        if self.ENG("gp"):
            self.H["gp"].collective_compute(
                kind, aop, replica_groups=groups, ins=ins, outs=outs
            ).then_inc(self.sems["cc"], 1)
            self.insts.append(("gp", [], [("cc", 1)]))
        return ("cc", self.count["cc"])

    def op(self, engname, f, deps=()):
        self.wait_on(engname, *deps)
        self.count[engname] += 1
        if self.ENG(engname):
            f(self.H[engname]).then_inc(self.sems[engname], 1)
            self.insts.append((engname, [], [(engname, 1)]))
        return (engname, self.count[engname])

    def ev(self, name):
        return (name, self.count[name])

    def mm(self, psum_ap, lhsT, rhs_, start, stop, deps=(), tp=None):
        self.wait_on("te", *deps)
        self.count["te"] += 1
        if self.ENG("te"):
            self.H["te"].matmul(psum_ap, lhsT, rhs_, start=start, stop=stop,
                                tile_position=tp).then_inc(self.sems["te"], 1)
            self.insts.append(("te", [], [("te", 1)]))
        return ("te", self.count["te"])

    def mmg(self, b, region, items, extra_deps=()):
        war = tuple(self.bank[b]["r"]) + tuple(extra_deps)
        self.bank[b]["r"] = []
        e = None
        n = len(items)
        for idx, (l, r, dp) in enumerate(items):
            e = self.mm(region, l, r, idx == 0, idx == n - 1,
                        deps=(war + tuple(dp)) if idx == 0 else tuple(dp))
        self.bank[b]["w"] = e
        return e

    def bank_read(self, b, ev_):
        self.bank[b]["r"].append(ev_)
        return ev_

    def wdeps(self, buf):
        d = tuple(self.bufdep.get(buf, ()))
        self.bufdep[buf] = []
        return d

    def rdep(self, buf, ev_):
        self.bufdep.setdefault(buf, []).append(ev_)

    def tap(self, name, ap, deps=()):
        if name in self.tapd:
            return self.dma("gp", self.tapd[name].ap(), ap, deps=deps)

    def bcast(self, b, row_ap, n, nparts=128, deps=()):
        war = tuple(self.bank[b]["r"])
        self.bank[b]["r"] = []
        e = self.mm(self.ps[b][0:nparts, 0:n], self.onesf[:, 0:nparts], row_ap, True, True,
                    deps=war + tuple(deps))
        self.bank[b]["w"] = e
        return e

    def bscal(self, slot, sc_ap, deps=(), scale=1.0):
        e = self.bcast(6, sc_ap, 1, deps=deps)
        ee = self.op("ve", lambda v: v.tensor_scalar(
            self.scl[:, slot:slot + 1], self.ps[6][:, 0:1], scale, None, ALU.mult),
            deps=(e,))
        self.bank_read(6, ee)
        return ee

    def _ln_stats(self, y_evs):
        its = [(self.onesb[:, 0:1], self.cast_c[:, i, :], tuple(y_evs)) for i in range(NI)]
        e1 = self.mmg(5, self.ps[5][0:1, 0:512], its)
        its = [(self.onesb[:, 0:1], self.sq_c[:, i, :], tuple(y_evs)) for i in range(NI)]
        e2 = self.mmg(4, self.ps[4][0:1, 0:512], its)
        self.rdep("cast_c", e1)
        self.rdep("sq_c", e2)
        return e1, e2

    def _ln_tail(self, e_sum, e_sumsq, dst_bf=None, dst_f32=None, sub_src=None,
                 resid_tck=None):
        rows = self.rows
        em = self.op("ve", lambda v: v.tensor_scalar(rows[:, 0:512], self.ps[5][0:1, :],
                                                     1.0 / D, None, ALU.mult),
                     deps=(e_sum,))
        self.bank_read(5, em)
        es_ = self.op("ve", lambda v: v.tensor_scalar(rows[:, 512:1024], self.ps[4][0:1, :],
                                                      1.0 / D, None, ALU.mult),
                      deps=(e_sumsq,))
        self.bank_read(4, es_)
        self.op("ve", lambda v: v.tensor_mul(rows[:, 1024:1536], rows[:, 0:512],
                                             rows[:, 0:512]))
        self.op("ve", lambda v: v.tensor_sub(rows[:, 1024:1536], rows[:, 512:1024],
                                             rows[:, 1024:1536]))
        ev3 = self.op("ve", lambda v: v.tensor_scalar(rows[:, 1024:1536], rows[:, 1024:1536],
                                                      1e-5, None, ALU.add))
        esq2 = self.op("se", lambda s: s.activation(rows[:, 1536:2048], rows[:, 1024:1536],
                                                    AF.Sqrt), deps=(ev3,))
        er = self.op("ve", lambda v: v.reciprocal(rows[:, 1024:1536], rows[:, 1536:2048]),
                     deps=(esq2,))
        ebm = self.bcast(5, rows[:, 0:512], 512, deps=(er,))
        ebr = self.bcast(4, rows[:, 1024:1536], 512, deps=(er,))
        sub_in = sub_src if sub_src is not None else self.cast_c[:]
        bm = self.ps[5][:, :].rearrange("p (u n) -> p u n", u=1).broadcast_to([128, NI, TC])
        br = self.ps[4][:, :].rearrange("p (u n) -> p u n", u=1).broadcast_to([128, NI, TC])
        en1 = self.op("ve", lambda v: v.tensor_sub(self.tmp_c[:], sub_in, bm),
                      deps=(ebm,) + self.wdeps("tmp_c"))
        self.bank_read(5, en1)
        if resid_tck is not None:
            en2 = self.op("ve", lambda v: v.tensor_mul(self.tmp_c[:], self.tmp_c[:], br),
                          deps=(ebr,))
            self.bank_read(4, en2)
            xs = self.x[:, :, resid_tck * TC:(resid_tck + 1) * TC]
            eadd = self.op("ve", lambda v: v.tensor_add(xs, xs, self.tmp_c[:]),
                           deps=(en2,))
            self.rdep("tmp_c", eadd)
            self.x_last = eadd
            return eadd
        outs = []
        if dst_bf is not None:
            en2 = self.op("ve", lambda v: v.tensor_mul(dst_bf, self.tmp_c[:], br),
                          deps=(ebr, en1))
            self.bank_read(4, en2)
            outs.append(en2)
        if dst_f32 is not None:
            en3 = self.op("ve", lambda v: v.tensor_mul(dst_f32, self.tmp_c[:], br),
                          deps=(ebr, en1))
            self.bank_read(4, en3)
            outs.append(en3)
        self.rdep("tmp_c", outs[-1])
        return outs[0] if len(outs) == 1 else tuple(outs)

    def ln_chunk(self, src_ap, dst_bf, src_evs, dst_f32=None):
        src_evs = tuple(src_evs) + ((self.x_last,) if self.x_last else ())
        ecast = self.op("ve", lambda v: v.tensor_copy(self.cast_c[:], src_ap),
                        deps=tuple(src_evs) + self.wdeps("cast_c"))
        esq = self.op("se", lambda s: s.activation(self.sq_c[:], src_ap, AF.Square),
                      deps=tuple(src_evs) + self.wdeps("sq_c"))
        e1, e2 = self._ln_stats((ecast, esq))
        return self._ln_tail(e1, e2, dst_bf=dst_bf, dst_f32=dst_f32, sub_src=src_ap)

    def lnout_residual(self, tck, y_evs):
        e1, e2 = self._ln_stats(tuple(y_evs))
        return self._ln_tail(e1, e2, resid_tck=tck)

    def load_w(self, dram_ap, buf, ni, deps=()):
        d = self.wdeps(f"wsb{buf}") + tuple(deps)
        return self.dma("gp", self.wsb[:, buf, 0:ni, :],
                        dram_ap.rearrange("i p o -> p i o"), deps=d)

    def load_rhs(self, buf, src_ap, deps=()):
        d = self.wdeps(f"rhs{buf}") + tuple(deps)
        return self.dma("sy", self.rhs[:, buf], src_ap, deps=d)

    def transpose(self, out_ap, in_ap, deps=(), ident=None):
        self.wait_on("te", *deps)
        self.count["te"] += 1
        if self.ENG("te"):
            iap = self.identb[:] if ident is None else ident
            self.H["te"].transpose(out_ap, in_ap, iap).then_inc(self.sems["te"], 1)
            self.insts.append(("te", [], [("te", 1)]))
        return ("te", self.count["te"])

    def fence(self, *engs):
        return tuple((e, self.count[e]) for e in engs)

    def deadcheck(self):
        streams = {}
        for (e, w, inc) in self.insts:
            streams.setdefault(e, []).append((w, inc))
        ptr = {e: 0 for e in streams}
        cnt = {s: 0 for s in self.sems}
        prog = True
        while prog:
            prog = False
            for e, lst in streams.items():
                while ptr[e] < len(lst):
                    w, inc = lst[ptr[e]]
                    if all(cnt[s] >= v for (s, v) in w):
                        for (s, a) in inc:
                            cnt[s] += a
                        ptr[e] += 1
                        prog = True
                    else:
                        break
        stuck = {e: lst[ptr[e]][0] for e, lst in streams.items() if ptr[e] < len(lst)}
        if stuck:
            raise RuntimeError(f"deadlock sim: stuck {stuck} counts {cnt} "
                               f"ptrs { {e: (ptr[e], len(l)) for e, l in streams.items()} }")


def build_program(kb, n_layers):
    self = kb
    din, dco, ps = kb.din, kb.dco, kb.ps

    def load_rope(deps=()):
        e1 = self.dma("gp", self.cosb[:], dco["rope_cos"].ap(), deps=deps)
        e2 = self.dma("gp", self.sinb[:], dco["rope_sin"].ap(), deps=deps)
        return (e1, e2)

    def stage_ln_to_ag(first_evs):
        stage = []
        for tck in range(NTC_T):
            src = self.x[:, :, tck * TC:(tck + 1) * TC]
            e_ln = self.ln_chunk(src, self.ln_c[:], tuple(first_evs) + self.wdeps("ln_c"))
            ed = self.dma("sy", self.ag_in.ap().rearrange("(i p) t -> p i t", p=128)
                          [:, :, tck * TC:(tck + 1) * TC], self.ln_c[:], deps=(e_ln,))
            self.rdep("ln_c", ed)
            stage.append(ed)
        return self.coll("AllGather", ALU.bypass, self.PAIRS,
                         [self.ag_in.ap()], [self.ag_out.ap()], deps=tuple(stage))

    def ag_rhs_ap(cc):
        half = (cc * TC) // T
        toff = (cc * TC) % T
        return self.ag_out.ap()[half].rearrange("(i p) t -> p i t", p=128)[:, :, toff:toff + TC]

    def rope_inplace(dst, cc, psbank, deps, nrows=128):
        sl = slice(cc * TC, (cc + 1) * TC)
        e1 = self.op("ve", lambda v: v.tensor_mul(self.scr1[0:nrows, :], dst[0:nrows, sl],
                                                  self.cosb[0:nrows, sl]),
                     deps=tuple(deps) + self.wdeps("scr1"))
        e2 = self.op("ve", lambda v: v.tensor_mul(self.scr2[0:nrows, :],
                                                  self.ps[psbank][0:nrows, :],
                                                  self.sinb[0:nrows, sl]),
                     deps=(self.bank[psbank]["w"],) + self.wdeps("scr2"))
        self.bank_read(psbank, e2)
        e3 = self.op("ve", lambda v: v.tensor_add(dst[0:nrows, sl], self.scr1[0:nrows, :],
                                                  self.scr2[0:nrows, :]))
        self.rdep("scr1", e3)
        self.rdep("scr2", e3)
        return e3

    def qkv_group(g, e_ag, wq_tiles, wk_tiles, wv_tiles, scale_q, rope,
                  bq_ap=None, bk_ap=None, bv_row=None, e_wdeps=()):
        e_wq = self.load_w(wq_tiles, 0, NI, deps=e_wdeps)
        e_wk = self.load_w(wk_tiles, 1, NI, deps=e_wdeps)
        e_rope = load_rope() if (rope and not self.rope_loaded) else ()
        if rope:
            self.rope_loaded = True
        evs = []
        for cc in range(NTC_S):
            buf = cc % 2
            e_rhs = self.load_rhs(buf, ag_rhs_ap(cc), deps=(e_ag,))
            itsq = [(self.wsb[:, 0, i, :], self.rhs[:, buf, i, :], (e_rhs, e_wq))
                    for i in range(NI)]
            eq = self.mmg(0, ps[0][:, :], itsq)
            itsk = [(self.wsb[:, 1, i, :], self.rhs[:, buf, i, :], (e_rhs, e_wk))
                    for i in range(NI)]
            ek = self.mmg(1, ps[1][:, :], itsk)
            for bb in (f"rhs{buf}", "wsb0"):
                self.rdep(bb, eq)
            self.rdep(f"rhs{buf}", ek)
            self.rdep("wsb1", ek)
            sl = slice(cc * TC, (cc + 1) * TC)
            kwq = dict(scale=scale_q) if bq_ap is None else dict(
                scale=scale_q, bias=bq_ap)
            eeq = self.op("se", lambda s, _sl=sl, _kw=kwq: s.activation(
                self.qf[:, _sl], ps[0][:, :],
                AF.Identity if "bias" in _kw else AF.Copy, **_kw), deps=(eq,))
            self.bank_read(0, eeq)
            kwk = dict() if bk_ap is None else dict(bias=bk_ap)
            eek = self.op("se", lambda s, _sl=sl, _kw=kwk: s.activation(
                self.kf[:, _sl], ps[1][:, :],
                AF.Identity if "bias" in _kw else AF.Copy, **_kw), deps=(ek,))
            self.bank_read(1, eek)
            if rope:
                war0 = tuple(self.bank[0]["r"])
                self.bank[0]["r"] = []
                mrq = self.mm(ps[0][:, :], self.ropeRt[:], self.qf[:, sl], True, True,
                              deps=war0 + (eeq,) + e_rope)
                self.bank[0]["w"] = mrq
                e3 = rope_inplace(self.qf, cc, 0, (eeq,) + e_rope)
                war1 = tuple(self.bank[1]["r"])
                self.bank[1]["r"] = []
                mrk = self.mm(ps[1][:, :], self.ropeRt[:], self.kf[:, sl], True, True,
                              deps=war1 + (eek,))
                self.bank[1]["w"] = mrk
                e4 = rope_inplace(self.kf, cc, 1, (eek,) + e_rope)
                evs += [e3, e4]
            else:
                evs += [eeq, eek]
        e_wv = self.load_w(wv_tiles, 0, NI, deps=(self.ev("te"),))
        for cc in range(NTC_S):
            buf = cc % 2
            e_rhs = self.load_rhs(buf, ag_rhs_ap(cc), deps=(e_ag,))
            for tt in range(TC // 128):
                its = [(self.rhs[:, buf, i, tt * 128:(tt + 1) * 128], self.wsb[:, 0, i, :],
                        (e_rhs, e_wv)) for i in range(NI)]
                if bv_row is not None:
                    its.append((self.onesb[0:1, 0:128], bv_row, ()))
                ev_ = self.mmg(2, ps[2][0:128, 0:128], its)
                self.rdep(f"rhs{buf}", ev_)
                self.rdep("wsb0", ev_)
                kt = cc * 4 + tt
                evv = self.op("ve", lambda v, _kt=kt: v.tensor_copy(
                    self.vtm[:, _kt, :, 0:64],
                    ps[2][:, 0:128].rearrange("p (h d) -> p h d", h=2)),
                    deps=(ev_, self.e_vones))
                self.bank_read(2, evv)
                evs.append(evv)
        return evs

    def head_bound(hh, qbase, kbase, e_ready):
        rows = self.rows
        for idx, (src, base) in enumerate(((self.qf, qbase), (self.kf, kbase))):
            for cc in range(NTC_S):
                sl = slice(cc * TC, (cc + 1) * TC)
                esq = self.op("se", lambda s, _s=src, _sl=sl, _b=base: s.activation(
                    self.scr1[0:64, :], _s[_b:_b + 64, _sl], AF.Square),
                    deps=tuple(e_ready) + self.wdeps("scr1"))
                em = self.mmg(4, ps[4][0:1, 0:512],
                              [(self.onesb[0:64, 0:1], self.scr1[0:64, :], (esq,))])
                self.rdep("scr1", em)
                ered = self.op("ve", lambda v: v.reduce_max(rows[0:1, 1536:1537],
                                                            ps[4][0:1, :], AX), deps=(em,))
                self.bank_read(4, ered)
                if cc == 0:
                    eacc = self.op("ve", lambda v: v.tensor_copy(
                        rows[0:1, 1537:1538], rows[0:1, 1536:1537]), deps=(ered,))
                else:
                    eacc = self.op("ve", lambda v: v.tensor_max(
                        rows[0:1, 1537:1538], rows[0:1, 1537:1538], rows[0:1, 1536:1537]),
                        deps=(ered,))
            eqs = self.op("se", lambda s: s.activation(rows[0:1, 1538 + idx:1539 + idx],
                                                       rows[0:1, 1537:1538], AF.Sqrt),
                          deps=(eacc,))
        ebn = self.op("ve", lambda v: v.tensor_mul(rows[0:1, 1540:1541],
                                                   rows[0:1, 1538:1539],
                                                   rows[0:1, 1539:1540]), deps=(eqs,))
        ebn2 = self.op("ve", lambda v: v.tensor_scalar(rows[0:1, 1540:1541],
                                                       rows[0:1, 1540:1541], -1.0, None,
                                                       ALU.mult), deps=(ebn,))
        return self.bscal(2 + hh, rows[0:1, 1540:1541], deps=(ebn2,))

    def attn_causal_group(g, kvsel, e_ready):
        for hh in range(2):
            kbase, _ = kvsel(hh)
            head_bound(hh, 64 * hh, kbase, e_ready)
        e_out = []
        for qc in range(NTC_S):
            nkt = 4 * qc + 4
            pv_started = [False, False]
            for kt in range(nkt):
                for hh in range(2):
                    slot = (kt % 2) * 2 + hh
                    qb = 64 * hh
                    kbase, vix = kvsel(hh)
                    bqk = hh
                    war = tuple(self.bank[bqk]["r"])
                    self.bank[bqk]["r"] = []
                    eqk = self.mm(ps[bqk][:, :],
                                  self.kf[kbase:kbase + 64, kt * 128:(kt + 1) * 128],
                                  self.qf[qb:qb + 64, qc * TC:(qc + 1) * TC],
                                  True, True, deps=war + tuple(e_ready))
                    self.bank[bqk]["w"] = eqk
                    if kt >= 4 * qc:
                        emk = self.op("ve", lambda v, _b=bqk, _r=kt - 4 * qc: v.tensor_add(
                            ps[_b][:, :], ps[_b][:, :], self.cmaskb[:, _r, :]), deps=(eqk,))
                        self.bank[bqk]["w"] = emk
                        eqk = emk
                    sname = f"expS{slot}"
                    eexp = self.op("se", lambda s, _b=bqk, _sl=slot, _hh=hh: s.activation(
                        self.expS[:, _sl, :],
                        ps[_b][:, :], AF.Exp, bias=self.scl[:, 2 + _hh:3 + _hh]),
                        deps=(eqk,) + self.wdeps(sname))
                    self.bank_read(bqk, eexp)
                    bb = 2 + hh
                    if not pv_started[hh]:
                        war2 = tuple(self.bank[bb]["r"])
                        self.bank[bb]["r"] = []
                    else:
                        war2 = ()
                    epv = self.mm(ps[bb][0:65, :], self.vtm[:, kt, vix, 0:65],
                                  self.expS[:, slot, :], not pv_started[hh],
                                  kt == nkt - 1, deps=war2 + (eexp,))
                    pv_started[hh] = True
                    self.rdep(sname, epv)
                    self.bank[bb]["w"] = epv
            for hh in range(2):
                bb = 2 + hh
                qb = 64 * hh
                erz = self.op("ve", lambda v, _b=bb: v.reciprocal(
                    self.rows[0:1, 1024:1536], ps[_b][64:65, :]),
                    deps=(self.bank[bb]["w"],))
                self.bank_read(bb, erz)
                ebz = self.bcast(4, self.rows[0:1, 1024:1536], 512, nparts=64, deps=(erz,))
                ecp = self.op("se", lambda s, _b=bb: s.activation(
                    self.oevac[0:64, :], ps[_b][0:64, :], AF.Copy),
                    deps=(self.bank[bb]["w"],) + self.wdeps("oevac"))
                self.bank_read(bb, ecp)
                emu = self.op("ve", lambda v, _qb=qb, _qc=qc: v.tensor_mul(
                    self.outTg[_qb:_qb + 64, _qc * TC:(_qc + 1) * TC],
                    self.oevac[0:64, :], ps[4][0:64, :]), deps=(ebz, ecp))
                self.bank_read(4, emu)
                self.rdep("oevac", emu)
                e_out.append(emu)
        return e_out

    def stage_outTg_to_hx(g, e_out):
        return self.dma("sy", self.hx_in.ap().rearrange("(g p) t -> g p t", p=128)[g],
                        self.outTg[:], deps=tuple(e_out))

    def hx_rhs_ap_factory(tck):
        def make(halfrow):
            def f():
                t = self.hx_out.ap()
                off = halfrow * 512 * S + self.h2048 + tck * TC
                return bass.AP(t.tensor, off, [[S, 128], [128 * S, 4], [1, TC]])
            return f
        return make

    def ensure_h(q):
        if self.ENG(q) and self.h2048 is None:
            pid = self.nc.partition_id(engines=[self.H[q].engine])
            self.h2048 = (pid % 2) * 2048

    def out_proj_lnout(wo_tiles_ap, bo_dram, e_hx, extra_bias_slot=None):
        if bo_dram is not None:
            e_bo = self.dma("gp", self.bias_sb[:, 0:NI], bo_dram.ap(),
                            deps=self.wdeps("bias0"))
        for tck in range(NTC_T):
            mk = hx_rhs_ap_factory(tck)
            buf = tck % 2
            d = self.wdeps(f"rhs{buf}")
            e_r0 = self.dma("sy", self.rhs[:, buf, 0:4, :], mk(0), deps=(e_hx,) + d)
            e_r1 = self.dma("sy", self.rhs[:, buf, 4:8, :], mk(1), deps=(e_hx,))
            y_evs = []
            for o in range(NI):
                e_w = self.load_w(wo_tiles_ap[o], o % 2, NI, deps=(self.ev("te"),))
                bnk = o % 2
                its = [(self.wsb[:, o % 2, i, :], self.rhs[:, buf, i, :],
                        (e_r0, e_r1, e_w)) for i in range(NI)]
                eo = self.mmg(bnk, ps[bnk][:, :], its)
                self.rdep(f"rhs{buf}", eo)
                self.rdep(f"wsb{o % 2}", eo)
                if bo_dram is not None:
                    kw = dict(bias=self.bias_sb[:, o:o + 1])
                    dp = (eo, e_bo)
                else:
                    kw = {}
                    dp = (eo,)
                dpc = dp + (self.wdeps("cast_c") if o == 0 else ())
                ec = self.op("se", lambda s, _o=o, _kw=kw: s.activation(
                    self.cast_c[:, _o, :], ps[_o % 2][:, :],
                    AF.Identity if "bias" in _kw else AF.Copy, **_kw),
                    deps=dpc)
                es2 = self.op("se", lambda s, _o=o, _kw=kw: s.activation(
                    self.sq_c[:, _o, :], ps[_o % 2][:, :], AF.Square, **_kw),
                    deps=(eo,) + (self.wdeps("sq_c") if o == 0 else ()))
                self.bank_read(bnk, ec)
                self.bank_read(bnk, es2)
                y_evs += [ec, es2]
            self.rdep("bias0", y_evs[-1])
            self.lnout_residual(tck, y_evs)

    def layer0(e_x, e_const):
        e_ag = stage_ln_to_ag((e_x, e_const))
        self.e_vones = self.op("ve", lambda v: v.memset(self.vtm[:, :, :, 64:66], 0.0))
        self.e_vones = self.op("ve", lambda v: v.memset(self.vtm[:, :, :, 64:65], 1.0))
        hx_stage = []
        for g in range(4):
            evs = qkv_group(g, e_ag, din["l0_wq"].ap()[g], din["l0_wk"].ap()[g],
                            din["l0_wv"].ap()[g], scale_q=0.125, rope=True)
            e_out = attn_causal_group(g, lambda hh: (64 * hh, hh), evs)
            hx_stage.append(stage_outTg_to_hx(g, e_out))
        e_hx = self.coll("AllGather", ALU.bypass, self.PAIRS, [self.hx_in.ap()],
                         [self.hx_out.ap()], deps=tuple(hx_stage))
        out_proj_lnout([din["l0_wo"].ap()[o] for o in range(NI)], None, e_hx)

    def ffn0():
        fe = self.fence("te", "ve", "se")
        for wname, dram_src, dq, NW in (("fc", din["l0_wfc_t"], self.wqd["fc"], 64),
                                        ("pr", din["l0_wpr_t"], self.wqd["pr"], 64)):
            KD = 1024 if wname == "fc" else 4096
            MD = 4096 if wname == "fc" else 1024
            src = dram_src.ap().rearrange("(i p) m -> p i m", p=128)
            for w in range(NW):
                it = (w * 512) // MD
                mo = (w * 512) % MD
                e_l = self.dma("gp", self.oevac[:], src[:, it, mo:mo + 512],
                               deps=self.wdeps("oevac"))
                e_r = self.op("ve", lambda v: v.reduce_max(
                    self.colA[:, 0:1], self.oevac[:], AX, apply_absolute_value=True),
                    deps=(e_l,))
                self.rdep("oevac", e_r)
                if w == 0:
                    e_acc = self.op("ve", lambda v: v.tensor_copy(self.colA[:, 1:2],
                                                                  self.colA[:, 0:1]),
                                    deps=(e_r,))
                else:
                    e_acc = self.op("ve", lambda v: v.tensor_max(
                        self.colA[:, 1:2], self.colA[:, 1:2], self.colA[:, 0:1]),
                        deps=(e_r,))
            e_c2r = self.dma("gp", self.rows[0:1, 1536 + 0:1536 + 128],
                             self.colA[:, 1:2], deps=(e_acc,))
            e_gm = self.op("ve", lambda v: v.reduce_max(self.rows[0:1, 1664:1665],
                                                        self.rows[0:1, 1536:1664], AX),
                           deps=(e_c2r,))
            e_s = self.op("ve", lambda v: v.tensor_scalar(self.rows[0:1, 1665:1666],
                                                          self.rows[0:1, 1664:1665],
                                                          1.0 / 127.0, None, ALU.mult),
                          deps=(e_gm,))
            e_rcp = self.op("ve", lambda v: v.reciprocal(self.rows[0:1, 1666:1667],
                                                         self.rows[0:1, 1665:1666]),
                            deps=(e_s,))
            e_ss = self.bscal(4, self.rows[0:1, 1665:1666], deps=(e_s,))
            e_rr = self.bscal(5, self.rows[0:1, 1666:1667], deps=(e_rcp,))
            NO = MD // 128
            NIW = KD // 128
            for o in range(NO):
                for wv in range(NIW // 4):
                    e_l = self.dma("gp", self.oevac[:].rearrange("p (i m) -> p i m", i=4),
                                   src[:, 4 * wv:4 * wv + 4, o * 128:(o + 1) * 128],
                                   deps=self.wdeps("oevac"))
                    e_q1 = self.op("ve", lambda v: v.tensor_scalar(
                        self.qscr[:], self.oevac[:], self.scl[:, 5:6], R2,
                        ALU.mult, ALU.add), deps=(e_l, e_rr) + self.wdeps("qscr"))
                    self.rdep("oevac", e_q1)
                    e_q2 = self.op("ve", lambda v: v.tensor_scalar(
                        self.scr1[:], self.qscr[:], R2, self.scl[:, 4:5],
                        ALU.subtract, ALU.mult), deps=(e_ss,) + self.wdeps("scr1"))
                    self.rdep("qscr", e_q2)
                    e_st = self.dma("gp", dq.ap()[o, 4 * wv:4 * wv + 4]
                                    .rearrange("i p m -> p i m"),
                                    self.scr1[:].rearrange("p (i m) -> p i m", i=4),
                                    deps=(e_q2,))
                    self.rdep("scr1", e_st)
            if wname == "fc":
                e_wqfc_done = self.ev("gp")
            else:
                e_wqpr_done = self.ev("gp")
        e_afc = self.dma("gp", self.afc[:], din["l0_afc_t"].ap()
                         .rearrange("(i p) r -> p i r", p=128), deps=fe)
        e_bfl = self.dma("gp", self.bfc_l[:], din["l0_bfc_l"].ap(), deps=fe)
        e_apr = self.dma("gp", self.apr[:], din["l0_apr_t"].ap()
                         .rearrange("(i p) r -> p i r", p=128))
        e_bpl = self.dma("gp", self.bpr_l[:], din["l0_bpr_l"].ap())
        e_bfc = self.dma("gp", self.bias_sb[:, 0:32], din["l0_bfc"].ap())
        e_bpr = self.dma("gp", self.bias_sb[:, 32:40], din["l0_bpr"].ap())

        for tck in range(NTC_T):
            src = self.x[:, :, tck * TC:(tck + 1) * TC]
            e_ln = self.ln_chunk(src, self.ln_c[:], self.wdeps("ln_c"))
            its = [(self.afc[:, i, :], self.ln_c[:, i, :], (e_ln, e_afc))
                   for i in range(NI)]
            em = self.mmg(3, ps[3][0:16, :], its)
            self.rdep("ln_c", em)
            e_mid = self.op("se", lambda s: s.activation(
                self.rowbf[0:16, 0:512], ps[3][0:16, :], AF.Copy, scale=2.0),
                deps=(em,) + self.wdeps("mid1"))
            self.bank_read(3, e_mid)
            h_evs = []
            for o in range(32):
                buf = o % 2
                e_w = self.load_w(self.wqd["fc"].ap()[o], buf, NI,
                                  deps=(e_wqfc_done, self.ev("te")))
                its = [(self.wsb[:, buf, i, :], self.ln_c[:, i, :], (e_ln, e_w))
                       for i in range(NI)]
                its.append((self.bfc_l[:, o * 128:(o + 1) * 128],
                            self.rowbf[0:16, 0:512], (e_mid, e_bfl)))
                bnk = o % 2
                eo = self.mmg(bnk, ps[bnk][:, :], its)
                self.rdep("ln_c", eo)
                self.rdep(f"wsb{buf}", eo)
                self.rdep("mid1", eo)
                eg = self.op("se", lambda s, _o=o: s.activation(
                    self.hid[:, _o, :], ps[_o % 2][:, :], AF.Gelu,
                    bias=self.bias_sb[:, _o:_o + 1]), deps=(eo, e_bfc) + fe)
                self.bank_read(bnk, eg)
                h_evs.append(eg)
            its = [(self.apr[:, f, :], self.hid[:, f, :], (h_evs[f], e_apr))
                   for f in range(32)]
            em2 = self.mmg(3, ps[3][0:16, :], its)
            e_mid2 = self.op("se", lambda s: s.activation(
                self.rowbf[0:16, 512:1024], ps[3][0:16, :], AF.Copy, scale=2.0),
                deps=(em2,) + self.wdeps("mid2"))
            self.bank_read(3, e_mid2)
            y_evs = []
            for o in range(NI):
                bnk = o % 2
                eo = None
                war = tuple(self.bank[bnk]["r"])
                self.bank[bnk]["r"] = []
                for wave in range(2):
                    e_w = self.load_w(self.wqd["pr"].ap()[o, 16 * wave:16 * wave + 16],
                                      bnk, 16, deps=(e_wqpr_done, self.ev("te")))
                    for i in range(16):
                        f = 16 * wave + i
                        eo = self.mm(ps[bnk][:, :], self.wsb[:, bnk, i, :],
                                     self.hid[:, f, :], (wave == 0 and i == 0), False,
                                     deps=(war + (e_w, h_evs[f])) if (wave == 0 and i == 0)
                                     else (e_w, h_evs[f]))
                    self.rdep(f"wsb{bnk}", eo)
                eo = self.mm(ps[bnk][:, :], self.bpr_l[:, o * 128:(o + 1) * 128],
                             self.rowbf[0:16, 512:1024], False, True,
                             deps=(e_mid2, e_bpl))
                self.bank[bnk]["w"] = eo
                self.rdep("mid2", eo)
                ec = self.op("se", lambda s, _o=o: s.activation(
                    self.cast_c[:, _o, :], ps[_o % 2][:, :], AF.Identity,
                    bias=self.bias_sb[:, 32 + _o:33 + _o]),
                    deps=(eo, e_bpr) + (self.wdeps("cast_c") if o == 0 else ()))
                es2 = self.op("se", lambda s, _o=o: s.activation(
                    self.sq_c[:, _o, :], ps[_o % 2][:, :], AF.Square,
                    bias=self.bias_sb[:, 32 + _o:33 + _o]),
                    deps=(eo,) + (self.wdeps("sq_c") if o == 0 else ()))
                self.bank_read(bnk, ec)
                self.bank_read(bnk, es2)
                y_evs += [ec, es2]
            self.lnout_residual(tck, y_evs)


    def layer1():
        fe = self.fence("te", "ve", "se")
        e_ag = stage_ln_to_ag(fe)
        e_gm = self.dma("gp", self.gmaskb[:], dco["l1_gmask"].ap(), deps=fe)
        e_bq = self.dma("gp", self.bias_sb[:, 0:4], din["l1_bq"].ap(), deps=self.wdeps("bias0"))
        e_bk = self.dma("gp", self.bias_sb[:, 4:8], din["l1_bk"].ap())
        e_bqs = self.op("ve", lambda v: v.tensor_scalar(
            self.bias_sb[:, 8:12], self.bias_sb[:, 0:4], 0.125, None, ALU.mult),
            deps=(e_bq,))
        e_bv = self.dma("gp", self.l1bv[:], din["l1_bv_row"].ap(), deps=fe)
        self.e_vones = self.op("ve", lambda v: v.memset(self.vtm[:, :, :, 64:66], 0.0),
                               deps=fe)
        self.e_vones = self.op("ve", lambda v: v.memset(self.vtm[:, :, :, 64:65], 1.0))
        e_vg1 = self.op("ve", lambda v: v.memset(self.vglob[:, 64:66], 0.0), deps=fe)
        e_vg1 = self.op("ve", lambda v: v.memset(self.vglob[:, 64:65], 1.0))
        hx_stage = []
        for g in range(4):
            evs = qkv_group(g, e_ag, din["l1_wq"].ap()[g], din["l1_wk"].ap()[g],
                            din["l1_wv"].ap()[g], scale_q=0.125, rope=False,
                            bq_ap=self.bias_sb[:, 8 + g:9 + g],
                            bk_ap=self.bias_sb[:, 4 + g:5 + g],
                            bv_row=self.l1bv[0:1, 128 * g:128 * g + 128],
                            e_wdeps=(e_bqs, e_bk, e_bv))
            e_out = attn_sparse_group(g, evs, (e_gm, e_vg1))
            hx_stage.append(stage_outTg_to_hx(g, e_out))
        e_hx = self.coll("AllGather", ALU.bypass, self.PAIRS, [self.hx_in.ap()],
                         [self.hx_out.ap()], deps=tuple(hx_stage))
        out_proj_lnout([din["l1_wo"].ap()[o] for o in range(NI)], din["l1_bo"], e_hx)

    def attn_sparse_group(g, e_ready, e_masks):
        e_vg1 = e_masks[-1]
        for hh in range(2):
            head_bound(hh, 64 * hh, 64 * hh, e_ready)
        egk = self.op("ve", lambda v: v.tensor_copy(
            self.kglob[:], bass.AP(self.kf.ap().tensor, GP0,
                                   [[S, 128], [GPSTEP, 8]])), deps=tuple(e_ready))
        evg = []
        for gi in range(8):
            t = int(L1_GP[gi])
            tile, part = t // 128, t % 128
            for hh in range(2):
                evg.append(self.dma("gp", self.vglob[32 * hh + gi:32 * hh + gi + 1, 0:64],
                                    self.vtm[part:part + 1, tile, hh, 0:64],
                                    deps=tuple(e_ready)))
        for hh in range(2):
            for cc in range(NTC_S):
                sl = slice(cc * TC, (cc + 1) * TC)
                war = tuple(self.bank[4]["r"]); self.bank[4]["r"] = []
                eqkg = self.mm(ps[4][0:8, :], self.kglob[64 * hh:64 * hh + 64, :],
                               self.qf[64 * hh:64 * hh + 64, sl], True, True,
                               deps=war + (egk,) + tuple(e_ready))
                self.bank[4]["w"] = eqkg
                emk = self.op("ve", lambda v, _sl=sl: v.tensor_add(
                    ps[4][0:8, :], ps[4][0:8, :], self.gmaskb[:, _sl]),
                    deps=(eqkg,) + e_masks)
                self.bank[4]["w"] = emk
                eex = self.op("se", lambda s, _hh=hh, _sl=sl: s.activation(
                    self.rowbf[32 * _hh:32 * _hh + 8, _sl], ps[4][0:8, :], AF.Exp,
                    bias=self.scl[0:8, 2 + _hh:3 + _hh]), deps=(emk,) + self.wdeps(f"expG{hh}"))
                self.bank_read(4, eex)
                self.rdep(f"expG{hh}", eex)
        e_out = []
        for qc in range(NTC_S):
            pv_started = [False, False]
            for sub in range(4):
                qt = 4 * qc + sub
                for hh in range(2):
                    qb = 64 * hh
                    bqk = hh
                    war = tuple(self.bank[bqk]["r"]); self.bank[bqk]["r"] = []
                    eqk = self.mm(ps[bqk][:, 0:128],
                                  self.kf[qb:qb + 64, qt * 128:(qt + 1) * 128],
                                  self.qf[qb:qb + 64, qt * 128:(qt + 1) * 128],
                                  True, True, deps=war + tuple(e_ready))
                    emk = self.op("ve", lambda v, _b=bqk: v.tensor_add(
                        ps[_b][:, 0:128], ps[_b][:, 0:128], self.dmaskb[:]),
                        deps=(eqk,))
                    self.bank[bqk]["w"] = emk
                    slot = (sub % 2) * 2 + hh
                    sname = f"expS{slot}"
                    eexp = self.op("se", lambda s, _b=bqk, _sl=slot, _hh=hh: s.activation(
                        self.expS[:, _sl, 0:128], ps[_b][:, 0:128], AF.Exp,
                        bias=self.scl[:, 2 + _hh:3 + _hh]),
                        deps=(emk,) + self.wdeps(sname))
                    self.bank_read(bqk, eexp)
                    bb = 2 + hh
                    if not pv_started[hh]:
                        war2 = tuple(self.bank[bb]["r"]); self.bank[bb]["r"] = []
                        pv_started[hh] = True
                    else:
                        war2 = ()
                    ed = self.mm(ps[bb][0:65, sub * 128:(sub + 1) * 128],
                                 self.vtm[:, qt, hh, 0:65], self.expS[:, slot, 0:128],
                                 True, False, deps=war2 + (eexp,))
                    self.rdep(sname, ed)
                    eg2 = self.mm(ps[bb][0:65, sub * 128:(sub + 1) * 128],
                                  self.vglob[32 * hh:32 * hh + 8, 0:65],
                                  self.rowbf[32 * hh:32 * hh + 8,
                                             qt * 128:(qt + 1) * 128],
                                  False, True, deps=tuple(evg) + (e_vg1,))
                    self.rdep(f"expG{hh}", eg2)
                    self.bank[bb]["w"] = eg2
            for hh in range(2):
                bb = 2 + hh
                qb = 64 * hh
                erz = self.op("ve", lambda v, _b=bb: v.reciprocal(
                    self.rows[0:1, 1024:1536], ps[_b][64:65, :]),
                    deps=(self.bank[bb]["w"],))
                self.bank_read(bb, erz)
                ebz = self.bcast(4, self.rows[0:1, 1024:1536], 512, nparts=64,
                                 deps=(erz,))
                ecp = self.op("se", lambda s, _b=bb: s.activation(
                    self.oevac[0:64, :], ps[_b][0:64, :], AF.Copy),
                    deps=(self.bank[bb]["w"],) + self.wdeps("oevac"))
                self.bank_read(bb, ecp)
                emu = self.op("ve", lambda v, _qb=qb, _qc=qc: v.tensor_mul(
                    self.outTg[_qb:_qb + 64, _qc * TC:(_qc + 1) * TC],
                    self.oevac[0:64, :], ps[4][0:64, :]), deps=(ebz, ecp))
                self.bank_read(4, emu)
                self.rdep("oevac", emu)
                e_out.append(emu)
        for hh in range(2):
            qb = 64 * hh
            eqg = self.op("ve", lambda v, _qb=qb: v.tensor_copy(
                self.qglob[_qb:_qb + 64, :],
                bass.AP(self.qf.ap().tensor, _qb * S + GP0,
                        [[S, 64], [GPSTEP, 8]])), deps=tuple(e_out))
            for kt in range(32):
                bqk = kt % 2
                war = tuple(self.bank[bqk]["r"]); self.bank[bqk]["r"] = []
                eqk = self.mm(ps[bqk][:, 0:8],
                              self.kf[qb:qb + 64, kt * 128:(kt + 1) * 128],
                              self.qglob[qb:qb + 64, :], True, True,
                              deps=war + (eqg,))
                self.bank[bqk]["w"] = eqk
                eex = self.op("se", lambda s, _b=bqk, _kt=kt: s.activation(
                    self.scr1[:, 8 * _kt:8 * _kt + 8], ps[_b][:, 0:8], AF.Exp,
                    bias=self.scl[:, 2 + hh:3 + hh]),
                    deps=(eqk,) + (self.wdeps("scr1") if kt == 0 else ()))
                self.bank_read(bqk, eex)
                if kt == 0:
                    war2 = tuple(self.bank[5]["r"]); self.bank[5]["r"] = []
                else:
                    war2 = ()
                epv = self.mm(ps[5][0:65, 0:8], self.vtm[:, kt, hh, 0:65],
                              self.scr1[:, 8 * kt:8 * kt + 8], kt == 0, kt == 31,
                              deps=war2 + (eex,))
                self.rdep("scr1", epv)
                self.bank[5]["w"] = epv
            erz = self.op("ve", lambda v: v.reciprocal(self.rows[0:1, 1024:1032],
                                                       ps[5][64:65, 0:8]),
                          deps=(self.bank[5]["w"],))
            self.bank_read(5, erz)
            ebz = self.bcast(4, self.rows[0:1, 1024:1032], 8, nparts=64, deps=(erz,))
            ecp = self.op("se", lambda s: s.activation(
                self.oevac[0:64, 0:8], ps[5][0:64, 0:8], AF.Copy),
                deps=(self.bank[5]["w"],) + self.wdeps("oevac"))
            self.bank_read(5, ecp)
            emu = self.op("ve", lambda v, _qb=qb: v.tensor_mul(
                bass.AP(self.outTg.ap().tensor, _qb * S + GP0, [[S, 64], [GPSTEP, 8]]),
                self.oevac[0:64, 0:8], ps[4][0:64, 0:8]), deps=(ebz, ecp))
            self.bank_read(4, emu)
            self.rdep("oevac", emu)
            e_out.append(emu)
        return e_out

    def ffn1():
        fe = self.fence("te", "ve", "se")
        e_gw = self.dma("gp", self.gwsb[:], din["l1_gw"].ap()
                        .rearrange("(i p) e -> p i e", p=128), deps=fe)
        e_gb = self.dma("gp", self.gbrow[:], din["l1_gb"].ap(), deps=fe)
        e_eb = self.dma("gp", self.ebs[:], din["l1_eb"].ap(), deps=fe)
        e_i8 = self.dma("gp", self.ind8[:], dco["ind8"].ap(), deps=fe)
        for tck in range(NTC_T):
            src = self.x[:, :, tck * TC:(tck + 1) * TC]
            e_ln = self.ln_chunk(src, self.ln_c[:], self.wdeps("ln_c") + fe)
            ew8 = []
            for tt in range(4):
                cb = tt * 8
                its = [(self.ln_c[:, i, tt * 128:(tt + 1) * 128], self.gwsb[:, i, :],
                        (e_ln, e_gw)) for i in range(NI)]
                its.append((self.onesb[0:1, 0:128], self.gbrow[:], (e_gb,)))
                eg = self.mmg(3, ps[3][0:128, 0:8], its)
                self.rdep("ln_c", eg)
                em = self.op("ve", lambda v, _c=cb: v.reduce_max(
                    self.colA[:, _c:_c + 1], ps[3][0:128, 0:8], AX), deps=(eg,))
                enm = self.op("ve", lambda v, _c=cb: v.tensor_scalar(
                    self.colA[:, _c + 1:_c + 2], self.colA[:, _c:_c + 1], -1.0, None,
                    ALU.mult), deps=(em,))
                eex = self.op("se", lambda s, _c=cb: s.activation(
                    self.gsm[:], ps[3][0:128, 0:8], AF.Exp,
                    bias=self.colA[:, _c + 1:_c + 2]),
                    deps=(enm, eg) + self.wdeps("gsm"))
                self.bank_read(3, eex)
                if tck == 0 and tt == 0:
                    _et = self.tap("l1gsm", self.gsm[:], deps=(eex,))
                    if _et:
                        self.rdep("gsm", _et)
                esum = self.op("ve", lambda v, _c=cb: v.reduce_sum(
                    self.colA[:, _c + 2:_c + 3], self.gsm[:], AX), deps=(eex,))
                ethr = self.op("ve", lambda v, _c=cb: v.tensor_scalar(
                    self.colA[:, _c + 3:_c + 4], self.colA[:, _c + 2:_c + 3], 0.1, None,
                    ALU.mult), deps=(esum,))
                if tck == 0 and tt == 0:
                    _et = self.tap("l1colA", self.colA[:], deps=(ethr,))
                emask = self.op("ve", lambda v, _c=cb: v.tensor_scalar(
                    self.gm2[:], self.gsm[:], self.colA[:, _c + 3:_c + 4], None,
                    ALU.is_ge), deps=(ethr,) + self.wdeps("gm2"))
                if tck == 0 and tt == 0:
                    _et = self.tap("l1sel", self.gm2[:], deps=(emask,))
                    if _et:
                        self.rdep("gm2", _et)
                ew = self.op("ve", lambda v: v.tensor_tensor(
                    self.wnf[:], self.gsm[:], self.gm2[:], ALU.mult),
                    deps=(emask,) + self.wdeps("wnf"))
                self.rdep("gsm", ew)
                if tck == 0 and tt == 0:
                    _et = self.tap("l1wnf", self.wnf[:], deps=(ew,))
                    if _et:
                        self.rdep("wnf", _et)
                ews = self.op("ve", lambda v, _c=cb: v.reduce_sum(
                    self.colA[:, _c + 4:_c + 5], self.wnf[:], AX), deps=(ew,))
                ercp2 = self.op("ve", lambda v, _c=cb: v.reciprocal(
                    self.colA[:, _c + 5:_c + 6], self.colA[:, _c + 4:_c + 5]),
                    deps=(ews,))
                ewn = self.op("ve", lambda v, _c=cb: v.tensor_scalar(
                    self.gm2[:], self.wnf[:], self.colA[:, _c + 5:_c + 6], None,
                    ALU.mult), deps=(ercp2,))
                self.rdep("wnf", ewn)
                if tck == 0 and tt == 0:
                    _et = self.tap("l1fin", self.gm2[:], deps=(ewn,))
                    if _et:
                        self.rdep("gm2", _et)
                ecst = self.op("ve", lambda v: v.tensor_copy(self.wnbf[:], self.gm2[:]),
                               deps=(ewn,) + self.wdeps("wnbf"))
                self.rdep("gm2", ecst)
                etp = self.transpose(self.pbf[0:8, 0:128], self.wnbf[:],
                                     deps=(ecst,) + self.wdeps("pbf"))
                self.rdep("wnbf", etp)
                ev8 = self.op("ve", lambda v, _tt=tt: v.tensor_copy(
                    self.w8[0:8, _tt * 128:(_tt + 1) * 128], self.pbf[0:8, 0:128]),
                    deps=(etp,) + (self.wdeps("w8") if tt == 0 else ()))
                self.rdep("pbf", ev8)
                ew8.append(ev8)
            if tck == 0:
                self.tap("l1ln", self.ln_c[:].rearrange("p i t -> p (i t)"),
                         deps=(e_ln,))
                self.tap("l1w8", self.w8[:], deps=tuple(ew8))
            for eg_ in range(2):
                e_rhs = []
                for ei in range(4):
                    e = 4 * eg_ + ei
                    war = tuple(self.bank[3]["r"]); self.bank[3]["r"] = []
                    ebc = self.mm(ps[3][:, :], self.ind8[0:8, e * 128:(e + 1) * 128],
                                  self.w8[0:8, :], True, True,
                                  deps=war + tuple(ew8) + (e_i8,))
                    self.bank[3]["w"] = ebc
                    er = self.op("ve", lambda v, _ei=ei: v.tensor_mul(
                        self.hid[:, 8 * _ei:8 * _ei + 8, :], self.ln_c[:],
                        ps[3][:, :].rearrange("p (u n) -> p u n", u=1)
                        .broadcast_to([128, NI, TC])),
                        deps=(ebc, e_ln) + self.wdeps(f"hid{ei}"))
                    self.bank_read(3, er)
                    self.rdep("ln_c", er)
                    e_rhs.append(er)
                for ot in range(NI):
                    bnk = ot % 2
                    war = tuple(self.bank[bnk]["r"]); self.bank[bnk]["r"] = []
                    first = True
                    em = None
                    for ei in range(4):
                        e = 4 * eg_ + ei
                        e_w = self.load_w(din["l1_ew"].ap()[e, ot], bnk, NI,
                                          deps=(self.ev("te"),))
                        for i in range(NI):
                            last = (eg_ == 1 and ei == 3 and i == NI - 1)
                            em = self.mm(ps[bnk][:, :], self.wsb[:, bnk, i, :],
                                         self.hid[:, 8 * ei + i, :],
                                         first, last,
                                         deps=(war + (e_w, e_rhs[ei])) if first
                                         else (e_w, e_rhs[ei]))
                            first = False
                        self.rdep(f"wsb{bnk}", em)
                        self.rdep(f"hid{ei}", em)
                    if eg_ == 0:
                        em = self.mm(ps[bnk][:, :], self.ebs[0:8, ot * 128:(ot + 1) * 128],
                                     self.w8[0:8, :], False, True,
                                     deps=(e_eb,) + tuple(ew8))
                        self.rdep("w8", em)
                    self.bank[bnk]["w"] = em
                    if eg_ == 0:
                        eo = self.op("ve", lambda v, _ot=ot, _b=bnk: v.tensor_copy(
                            self.oc32[:, _ot, :], ps[_b][:, :]), deps=(em,))
                    else:
                        eo = self.op("ve", lambda v, _ot=ot, _b=bnk: v.tensor_add(
                            self.oc32[:, _ot, :], self.oc32[:, _ot, :], ps[_b][:, :]),
                            deps=(em,))
                    self.bank_read(bnk, eo)
                    self.rdep("oc32", eo)
            if tck == 0:
                self.tap("l1y", self.oc32[:].rearrange("p i t -> p (i t)"),
                         deps=tuple(self.bufdep.get("oc32", ())))
            ecc = self.op("ve", lambda v: v.tensor_copy(self.cast_c[:], self.oc32[:]),
                          deps=self.wdeps("oc32x") + self.wdeps("cast_c") +
                          tuple(self.bufdep.get("oc32", ())))
            esq = self.op("se", lambda s: s.activation(self.sq_c[:], self.oc32[:],
                                                       AF.Square),
                          deps=self.wdeps("sq_c") + tuple(self.bufdep.get("oc32", ())))
            self.bufdep["oc32"] = []
            e1, e2 = self._ln_stats((ecc, esq))
            self._ln_tail(e1, e2, sub_src=self.oc32[:], resid_tck=tck)

    def layer2():
        fe = self.fence("te", "ve", "se")
        e_ag = stage_ln_to_ag(fe)
        e_wf = self.dma("gp", self.l2wf2[:], din["l2_wf2"].ap(), deps=fe)
        e_bc = self.dma("gp", self.l2bfc[:], din["l2_bf2c"].ap(), deps=fe)
        e_br = self.dma("gp", self.l2bfr[:], din["l2_bf2r"].ap(), deps=fe)
        self.e_vones = self.op("ve", lambda v: v.memset(self.vtm[:, :, :, 64:66], 0.0),
                               deps=fe)
        self.e_vones = self.op("ve", lambda v: v.memset(self.vtm[:, :, :, 64:65], 1.0))
        cut = L2CUT
        hx_stage = []
        for g in range(4):
            if not (cut & 1):
                break
            evs = qkv_group(g, e_ag, din["l2_wqkv"].ap()[g], din["l2_wqkv"].ap()[4 + g],
                            din["l2_wqkv"].ap()[8 + g], scale_q=1.0, rope=False,
                            e_wdeps=(e_wf, e_bc, e_br))
            ev_q = []
            for cc in range(NTC_S if (cut & 2) else 0):
                sl = slice(cc * TC, (cc + 1) * TC)
                war = tuple(self.bank[0]["r"]); self.bank[0]["r"] = []
                eq = self.mm(ps[0][:, :], self.l2wf2[:], self.qf[:, sl], True, True,
                             deps=war + tuple(evs) + (e_wf,))
                self.bank[0]["w"] = eq
                er = self.op("se", lambda s, _sl=sl: s.activation(
                    self.qf[:, _sl], ps[0][:, :], AF.Relu, bias=self.l2bfc[:, 0:1]),
                    deps=(eq, e_bc))
                self.bank_read(0, er)
                ev_q.append(er)
            ekv_done = [None, None]
            for kt in range(32 if (cut & 4) else 0):
                war = tuple(self.bank[1]["r"]); self.bank[1]["r"] = []
                ek = self.mm(ps[1][0:128, 0:128], self.kf[:, kt * 128:(kt + 1) * 128],
                             self.l2wf2[:], True, False,
                             deps=war + tuple(evs) + (e_wf,))
                ek = self.mm(ps[1][0:128, 0:128], self.onesb[0:1, 0:128], self.l2bfr[:],
                             False, True, deps=(e_br,))
                self.bank[1]["w"] = ek
                er = self.op("se", lambda s: s.activation(
                    self.scr1[:, 0:128], ps[1][0:128, 0:128], AF.Relu),
                    deps=(ek,) + self.wdeps("scr1"))
                self.bank_read(1, er)
                for hh in range(2):
                    bb = 2 + hh
                    if kt == 0:
                        war2 = tuple(self.bank[bb]["r"]); self.bank[bb]["r"] = []
                    else:
                        war2 = ()
                    ekv = self.mm(ps[bb][0:64, 0:65], self.scr1[:, 64 * hh:64 * hh + 64],
                                  self.vtm[:, kt, hh, 0:65], kt == 0, kt == 31,
                                  deps=war2 + (er, self.e_vones) + tuple(evs))
                    self.bank[bb]["w"] = ekv
                    self.rdep("scr1", ekv)
                    ekv_done[hh] = ekv
            e_kc = []
            for hh in range(2 if (cut & 8) else 0):
                bb = 2 + hh
                pb = 64 * hh
                ekc = self.op("se", lambda s, _hh=hh, _bb=bb, _pb=pb: s.activation(
                    self.expS[_pb:_pb + 64, _hh, 0:65], ps[_bb][0:64, 0:65], AF.Copy),
                    deps=(ekv_done[hh],) + self.wdeps(f"kvsb{hh}"))
                self.bank_read(bb, ekc)
                e_kc.append(ekc)
            e_out = []
            for qc in range(NTC_S if (cut & 8) else 0):
                for hh in range(2):
                    bb = hh
                    pb = 64 * hh
                    war = tuple(self.bank[bb]["r"]); self.bank[bb]["r"] = []
                    eo = self.mm(ps[bb][0:65, :], self.expS[pb:pb + 64, hh, 0:65],
                                 self.qf[pb:pb + 64, qc * TC:(qc + 1) * TC], True, True,
                                 deps=war + (e_kc[hh],) + tuple(ev_q))
                    self.bank[bb]["w"] = eo
                    self.rdep(f"kvsb{hh}", eo)
                    ea = self.op("ve", lambda v, _bb=bb: v.tensor_scalar(
                        self.rows[0:1, 0:512], ps[_bb][64:65, :], 1e-6, None, ALU.add),
                        deps=(eo,))
                    erz = self.op("ve", lambda v: v.reciprocal(
                        self.rows[0:1, 1024:1536], self.rows[0:1, 0:512]), deps=(ea,))
                    ebz = self.bcast(4, self.rows[0:1, 1024:1536], 512, nparts=64,
                                     deps=(erz,))
                    ecp = self.op("se", lambda s, _bb=bb: s.activation(
                        self.oevac[0:64, :], ps[_bb][0:64, :], AF.Copy),
                        deps=(eo,) + self.wdeps("oevac"))
                    self.bank_read(bb, ecp)
                    emu = self.op("ve", lambda v, _pb=pb, _qc=qc: v.tensor_mul(
                        self.outTg[_pb:_pb + 64, _qc * TC:(_qc + 1) * TC],
                        self.oevac[0:64, :], ps[4][0:64, :]), deps=(ebz, ecp))
                    self.bank_read(4, emu)
                    self.rdep("oevac", emu)
                    e_out.append(emu)
            if cut & 16:
                hx_stage.append(stage_outTg_to_hx(g, e_out))
        if cut & 16:
            e_hx = self.coll("AllGather", ALU.bypass, self.PAIRS, [self.hx_in.ap()],
                             [self.hx_out.ap()], deps=tuple(hx_stage))
            out_proj_lnout([din["l2_wout"].ap()[o] for o in range(NI)],
                           din["l2_bout"], e_hx)

    def _col_to_scalar(col_ap, dst_off, deps):
        e1 = self.dma("gp", self.rows[0:1, 1536:1664], col_ap, deps=deps)
        e2 = self.op("ve", lambda v: v.reduce_max(
            self.rows[0:1, dst_off:dst_off + 1], self.rows[0:1, 1536:1664], AX),
            deps=(e1,))
        return e2

    def _armx(base, k, deps):
        e1 = self.dma("gp", self.armx_in.ap()[0:1, 0:k], self.rows[0:1, base:base + k],
                      deps=deps)
        e2 = self.coll("AllReduce", ALU.max, self.ALL8, [self.armx_in.ap()],
                       [self.armx_out.ap()], deps=(e1,))
        e3 = self.dma("gp", self.rows[0:1, base:base + k],
                      self.armx_out.ap()[0:1, 0:k], deps=(e2,))
        return e3

    def _mk_scale(src_off, qmax, s_slot, rs_slot, deps):
        e1 = self.op("ve", lambda v: v.tensor_scalar(
            self.rows[0:1, 1700:1701], self.rows[0:1, src_off:src_off + 1],
            1.0 / qmax, 1e-12, ALU.mult, ALU.max), deps=deps)
        e2 = self.op("ve", lambda v: v.reciprocal(
            self.rows[0:1, 1701:1702], self.rows[0:1, 1700:1701]), deps=(e1,))
        es = self.bscal(s_slot, self.rows[0:1, 1700:1701], deps=(e1,))
        er = self.bscal(rs_slot, self.rows[0:1, 1701:1702], deps=(e2,))
        return (es, er)

    def ffn2():
        f2 = F2CUT
        fe = self.fence("te", "ve", "se")
        e_b4 = self.dma("gp", self.blk4[:], dco["blkind4"].ap(), deps=fe)
        e_z = self.op("ve", lambda v: v.memset(self.rows[0:1, 1696:1704], 0.0), deps=fe)
        e_az = self.dma("gp", self.armx_in.ap(), self.rows[0:1, 1696:1704], deps=(e_z,))

        dq_done = {}
        for wname, dsrc, dsrc_t, dq, O, I_ in () if not (f2 & 1) else (
                ("wu", din["l2_wu"], din["l2_wu_t"], self.wqd["wu"], 3072, 1024),
                ("wg", din["l2_wg"], din["l2_wg_t"], self.wqd["wg"], 3072, 1024),
                ("wd", din["l2_wd"], din["l2_wd_t"], self.wqd["wd"], 1024, 3072)):
            nbI = I_ // 32
            src = dsrc.ap().rearrange("(t p) i -> t p i", p=128)
            e_id = self.dma("gp", self.wt2[:, 0:128], dco["ident_f32"].ap(),
                            deps=self.wdeps("wt2"))
            for ot in range(O // 128):
                eh = []
                for half in range(I_ // 512):
                    wt = self.wt_a if half % 2 == 0 else self.wt_b
                    wn = "wt_a" if half % 2 == 0 else "wt_b"
                    e_l = self.dma("sy", wt[:], src[ot][:, half * 512:half * 512 + 512],
                                   deps=self.wdeps(wn) + fe)
                    e_r = self.op("ve", lambda v, _wt=wt, _h=half: v.reduce_max(
                        self.smx[:, _h * 16:_h * 16 + 16],
                        _wt[:].rearrange("p (b e) -> p b e", e=32), AX,
                        apply_absolute_value=True),
                        deps=(e_l,) + (self.wdeps("smx") if half == 0 else ()))
                    self.rdep(wn, e_r)
                    eh.append(e_r)
                e_s = self.op("ve", lambda v, _nb=nbI: v.tensor_scalar(
                    self.sS[:, 0:_nb], self.smx[:, 0:_nb], 1.0 / 7.0, 1e-12,
                    ALU.mult, ALU.max), deps=tuple(eh) + self.wdeps("sS"))
                self.rdep("smx", e_s)
                war = tuple(self.bank[6]["r"]); self.bank[6]["r"] = []
                e_t = self.transpose(ps[6][0:nbI, 0:128], self.sS[:, 0:nbI],
                                     deps=war + (e_s, e_id), ident=self.wt2[:, 0:128])
                self.bank[6]["w"] = e_t
                self.rdep("wt2", e_t)
                self.rdep("sS", e_t)
                e_st = self.op("ve", lambda v, _nb=nbI, _ot=ot: v.tensor_copy(
                    self.sT[0:_nb, _ot * 128:(_ot + 1) * 128], ps[6][0:_nb, 0:128]),
                    deps=(e_t,) + (self.wdeps("sT") if ot == 0 else ()))
                self.bank_read(6, e_st)
            e_sT = self.ev("ve")
            src_t = dsrc_t.ap().rearrange("(t p) o -> t p o", p=128)
            for it in range(I_ // 128):
                for oc in range(O // 512):
                    wt = self.wt_a if oc % 2 == 0 else self.wt_b
                    wn = "wt_a" if oc % 2 == 0 else "wt_b"
                    e_l = self.dma("sy", wt[:], src_t[it][:, oc * 512:oc * 512 + 512],
                                   deps=self.wdeps(wn))
                    e_s4 = self.dma("gp", self.stage4[:],
                                    self.sT[4 * it:4 * it + 4, oc * 512:oc * 512 + 512],
                                    deps=(e_sT,) + self.wdeps("stage4"))
                    self.rdep("sT", e_s4)
                    war = tuple(self.bank[5]["r"]); self.bank[5]["r"] = []
                    e_bc = self.mm(ps[5][:, :], self.blk4[:], self.stage4[:],
                                   True, True, deps=war + (e_b4, e_s4))
                    self.bank[5]["w"] = e_bc
                    self.rdep("stage4", e_bc)
                    e_rc = self.op("ve", lambda v: v.reciprocal(self.qscr[:], ps[5][:, :]),
                                   deps=(e_bc,) + self.wdeps("qscr"))
                    self.bank_read(5, e_rc)
                    e_m1 = self.op("ve", lambda v, _wt=wt: v.tensor_tensor(
                        self.wt2[:], _wt[:], self.qscr[:], ALU.mult),
                        deps=(e_l, e_rc) + self.wdeps("wt2"))
                    self.rdep(wn, e_m1)
                    self.rdep("qscr", e_m1)
                    e_m2 = self.op("ve", lambda v: v.tensor_scalar(
                        self.wt2[:], self.wt2[:], R2, None, ALU.add), deps=(e_m1,))
                    e_m2 = self.op("ve", lambda v: v.tensor_scalar(
                        self.wt2[:], self.wt2[:], R2, None, ALU.subtract), deps=(e_m2,))
                    e_m3 = self.op("ve", lambda v: v.tensor_tensor(
                        self.wdq[:], self.wt2[:], ps[5][:, :], ALU.mult),
                        deps=(e_m2,) + self.wdeps("wdq"))
                    self.bank_read(5, e_m3)
                    self.rdep("wt2", e_m3)
                    e_d = self.dma("gp", dq.ap()[4 * oc:4 * oc + 4, it]
                                   .rearrange("o p m -> p o m"),
                                   self.wdq[:].rearrange("p (o m) -> p o m", o=4),
                                   deps=(e_m3,))
                    self.rdep("wdq", e_d)
            dq_done[wname] = self.ev("gp")
            if wname == "wu":
                self.tap("f2wq", dq.ap()[0, 0], deps=(dq_done[wname],))

        if not (f2 & 2):
            return
        e_acc = None
        for tck in range(NTC_T):
            tsl = slice(tck * TC, (tck + 1) * TC)
            dst = self.lnbuf[:, :, tsl]
            e_ln = self.ln_chunk(self.x[:, :, tsl], dst, self.wdeps("lnbuf") + fe)
            e_r1 = self.op("ve", lambda v, _d=dst: v.reduce_max(
                self.colA[:, 8:16], _d, AX, apply_absolute_value=True),
                deps=(e_ln,))
            self.rdep("lnbuf", e_r1)
            e_r2 = self.op("ve", lambda v: v.reduce_max(
                self.colA[:, 0:1], self.colA[:, 8:16], AX), deps=(e_r1,))
            if tck == 0:
                e_acc = self.op("ve", lambda v: v.tensor_copy(
                    self.colA[:, 1:2], self.colA[:, 0:1]), deps=(e_r2,))
            else:
                e_acc = self.op("ve", lambda v: v.tensor_max(
                    self.colA[:, 1:2], self.colA[:, 1:2], self.colA[:, 0:1]),
                    deps=(e_r2,))
        e_sc = _col_to_scalar(self.colA[:, 1:2], 1704, (e_acc,))
        e_ar = _armx(1704, 1, (e_sc, e_az))
        e_sx = _mk_scale(1704, 32767.0, 4, 5, (e_ar,))
        self.tap("f2sx", self.rows[0:1, 1700:1702], deps=e_sx)
        e_xq = []
        for tck in range(NTC_T):
            tsl = slice(tck * TC, (tck + 1) * TC)
            for i in range(NI):
                sl_ap = self.lnbuf[:, i, tsl]
                e1 = self.op("ve", lambda v, _a=sl_ap: v.tensor_scalar(
                    self.qscr[:], _a, self.scl[:, 5:6], R2, ALU.mult, ALU.add),
                    deps=e_sx + self.wdeps("qscr"))
                e2 = self.op("ve", lambda v: v.tensor_scalar(
                    self.qscr[:], self.qscr[:], R2, None, ALU.subtract))
                e3 = self.op("ve", lambda v, _a=sl_ap: v.tensor_scalar(
                    _a, self.qscr[:], self.scl[:, 4:5], None, ALU.mult), deps=(e2,))
                self.rdep("qscr", e3)
                e_xq.append(e3)
                if tck == 0 and i == 0:
                    _et = self.tap("f2xq", sl_ap, deps=(e3,))
                    if _et:
                        self.rdep("lnbuf", _et)
        e_xq = tuple(e_xq)

        if not (f2 & 4):
            return
        first_mx = {"up": True, "gate": True}
        for tck in range(NTC_T):
            tsl = slice(tck * TC, (tck + 1) * TC)
            for which, which_w, dq, cacc in (("up", "wu", self.wqd["wu"], 2),
                                             ("gate", "wg", self.wqd["wg"], 4)):
                fd = self.fdr[which]
                for ot in range(24):
                    buf = ot % 2
                    bnk = ot % 2
                    e_w = self.load_w(dq.ap()[ot], buf, NI,
                                      deps=(self.ev("te"), dq_done[which_w]))
                    its = [(self.wsb[:, buf, i, :], self.lnbuf[:, i, tsl],
                            (e_w,) + e_xq) for i in range(NI)]
                    eo = self.mmg(bnk, ps[bnk][:, :], its)
                    self.rdep(f"wsb{buf}", eo)
                    self.rdep("lnbuf", eo)
                    sb = self.scr1 if ot % 2 == 0 else self.scr2
                    sn = "scr1" if ot % 2 == 0 else "scr2"
                    ee = self.op("se", lambda s, _b=bnk, _sb=sb: s.activation(
                        _sb[:], ps[_b][:, :], AF.Copy), deps=(eo,) + self.wdeps(sn))
                    self.bank_read(bnk, ee)
                    ed = self.dma("sy", fd.ap()[ot, :, tsl], sb[:], deps=(ee,))
                    self.rdep(sn, ed)
                    if tck == 0 and ot == 0:
                        _et = self.tap(f"f2{which}0", sb[:], deps=(ee,))
                        if _et:
                            self.rdep(sn, _et)
                    em = self.op("ve", lambda v, _b=bnk, _c=cacc: v.reduce_max(
                        self.colA[:, _c:_c + 1], ps[_b][:, :], AX,
                        apply_absolute_value=True), deps=(eo, ee))
                    self.bank_read(bnk, em)
                    if first_mx[which]:
                        ema = self.op("ve", lambda v, _c=cacc: v.tensor_copy(
                            self.colA[:, _c + 1:_c + 2], self.colA[:, _c:_c + 1]),
                            deps=(em,))
                        first_mx[which] = False
                    else:
                        ema = self.op("ve", lambda v, _c=cacc: v.tensor_max(
                            self.colA[:, _c + 1:_c + 2], self.colA[:, _c + 1:_c + 2],
                            self.colA[:, _c:_c + 1]), deps=(em,))
        if not (F2P1 & 8):
            return
        e_up = _col_to_scalar(self.colA[:, 3:4], 1705, (self.ev("ve"),))
        e_g = _col_to_scalar(self.colA[:, 5:6], 1706, (e_up,))
        e_ar = _armx(1705, 2, (e_g,))
        e_sup = _mk_scale(1705, 32767.0, 6, 7, (e_ar,))
        self.tap("f2sup", self.rows[0:1, 1700:1702], deps=e_sup)
        e_sg = _mk_scale(1706, 32767.0, 8, 9, e_sup + (self.ev("gp"),))
        self.tap("f2sg", self.rows[0:1, 1700:1702], deps=e_sg)

        if not (f2 & 8):
            return
        first = True
        for tck in range(NTC_T):
            tsl = slice(tck * TC, (tck + 1) * TC)
            for ot in range(24):
                sb = self.scr1 if ot % 2 == 0 else self.scr2
                sn = "scr1" if ot % 2 == 0 else "scr2"
                e_l = self.dma("sy", sb[:], self.fdr["gate"].ap()[ot, :, tsl],
                               deps=self.wdeps(sn))
                e1 = self.op("ve", lambda v, _sb=sb: v.tensor_scalar(
                    self.qscr[:], _sb[:], self.scl[:, 9:10], R2, ALU.mult, ALU.add),
                    deps=(e_l,) + e_sg + self.wdeps("qscr"))
                self.rdep(sn, e1)
                e2 = self.op("ve", lambda v: v.tensor_scalar(
                    self.qscr[:], self.qscr[:], R2, None, ALU.subtract))
                e3 = self.op("ve", lambda v: v.tensor_scalar(
                    self.oevac[:], self.qscr[:], self.scl[:, 8:9], None, ALU.mult),
                    deps=(e2,) + self.wdeps("oevac"))
                e4 = self.op("se", lambda s: s.activation(
                    self.qscr[:], self.oevac[:], AF.Sigmoid), deps=(e3,))
                self.rdep("qscr", e4)
                e5 = self.op("ve", lambda v: v.tensor_scalar(
                    self.qscr[:], self.qscr[:], 65536.0, R2, ALU.mult, ALU.add),
                    deps=(e4,))
                e6 = self.op("ve", lambda v: v.tensor_scalar(
                    self.qscr[:], self.qscr[:], R2, 65535.0, ALU.subtract, ALU.min))
                e7 = self.op("ve", lambda v: v.tensor_tensor(
                    self.oevac[:], self.oevac[:], self.qscr[:], ALU.mult), deps=(e6,))
                self.rdep("qscr", e7)
                e8 = self.op("ve", lambda v: v.tensor_scalar(
                    self.oevac[:], self.oevac[:], 2.0 ** -16, None, ALU.mult))
                if tck == 0 and ot == 0:
                    _et = self.tap("f2g20", self.oevac[:], deps=(e8,))
                    if _et:
                        self.rdep("oevac", _et)
                em = self.op("ve", lambda v: v.reduce_max(
                    self.colA[:, 6:7], self.oevac[:], AX, apply_absolute_value=True),
                    deps=(e8,))
                if first:
                    ema = self.op("ve", lambda v: v.tensor_copy(
                        self.colA[:, 7:8], self.colA[:, 6:7]), deps=(em,))
                    first = False
                else:
                    ema = self.op("ve", lambda v: v.tensor_max(
                        self.colA[:, 7:8], self.colA[:, 7:8], self.colA[:, 6:7]),
                        deps=(em,))
                e9 = self.op("ve", lambda v, _sb=sb: v.tensor_copy(_sb[:], self.oevac[:]),
                             deps=(em,))
                self.rdep("oevac", e9)
                ed = self.dma("sy", self.fdr["gate"].ap()[ot, :, tsl], sb[:], deps=(e9,))
                self.rdep(sn, ed)
        e_sc = _col_to_scalar(self.colA[:, 7:8], 1707, (self.ev("ve"),))
        e_ar = _armx(1707, 1, (e_sc,))
        e_sg2 = _mk_scale(1707, 32767.0, 10, 11, (e_ar,))

        if not (f2 & 16):
            return
        first = True
        for tck in range(NTC_T):
            tsl = slice(tck * TC, (tck + 1) * TC)
            for ot in range(24):
                e_l1 = self.dma("sy", self.scr1[:], self.fdr["gate"].ap()[ot, :, tsl],
                                deps=self.wdeps("scr1"))
                e_l2 = self.dma("sy", self.scr2[:], self.fdr["up"].ap()[ot, :, tsl],
                                deps=self.wdeps("scr2"))
                e1 = self.op("ve", lambda v: v.tensor_scalar(
                    self.qscr[:], self.scr1[:], self.scl[:, 11:12], R2, ALU.mult,
                    ALU.add), deps=(e_l1,) + e_sg2 + self.wdeps("qscr"))
                self.rdep("scr1", e1)
                e2 = self.op("ve", lambda v: v.tensor_scalar(
                    self.qscr[:], self.qscr[:], R2, None, ALU.subtract))
                e3 = self.op("ve", lambda v: v.tensor_scalar(
                    self.oevac[:], self.qscr[:], self.scl[:, 10:11], None, ALU.mult),
                    deps=(e2,) + self.wdeps("oevac"))
                self.rdep("qscr", e3)
                e4 = self.op("ve", lambda v: v.tensor_scalar(
                    self.qscr[:], self.scr2[:], self.scl[:, 7:8], R2, ALU.mult,
                    ALU.add), deps=(e_l2, e3))
                self.rdep("scr2", e4)
                e5 = self.op("ve", lambda v: v.tensor_scalar(
                    self.qscr[:], self.qscr[:], R2, None, ALU.subtract))
                e6 = self.op("ve", lambda v: v.tensor_scalar(
                    self.qscr[:], self.qscr[:], self.scl[:, 6:7], None, ALU.mult))
                e7 = self.op("ve", lambda v: v.tensor_tensor(
                    self.oevac[:], self.oevac[:], self.qscr[:], ALU.mult), deps=(e6,))
                self.rdep("qscr", e7)
                if tck == 0 and ot == 0:
                    _et = self.tap("f2o0", self.oevac[:], deps=(e7,))
                    if _et:
                        self.rdep("oevac", _et)
                em = self.op("ve", lambda v: v.reduce_max(
                    self.colA[:, 12:13], self.oevac[:], AX, apply_absolute_value=True),
                    deps=(e7,))
                if first:
                    ema = self.op("ve", lambda v: v.tensor_copy(
                        self.colA[:, 13:14], self.colA[:, 12:13]), deps=(em,))
                    first = False
                else:
                    ema = self.op("ve", lambda v: v.tensor_max(
                        self.colA[:, 13:14], self.colA[:, 13:14], self.colA[:, 12:13]),
                        deps=(em,))
                e9 = self.op("ve", lambda v: v.tensor_copy(self.scr1[:], self.oevac[:]),
                             deps=(em,) + self.wdeps("scr1"))
                self.rdep("oevac", e9)
                ed = self.dma("sy", self.fdr["up"].ap()[ot, :, tsl], self.scr1[:],
                              deps=(e9,))
                self.rdep("scr1", ed)
        e_sc = _col_to_scalar(self.colA[:, 13:14], 1708, (self.ev("ve"),))
        e_ar = _armx(1708, 1, (e_sc,))
        e_so = _mk_scale(1708, 32767.0, 12, 13, (e_ar,))

        if not (f2 & 32):
            return
        for tck in range(NTC_T):
            tsl = slice(tck * TC, (tck + 1) * TC)
            oq_evs = []
            for ot in range(24):
                e_l = self.dma("sy", self.scr1[:], self.fdr["up"].ap()[ot, :, tsl],
                               deps=self.wdeps("scr1"))
                e1 = self.op("ve", lambda v: v.tensor_scalar(
                    self.qscr[:], self.scr1[:], self.scl[:, 13:14], R2, ALU.mult,
                    ALU.add), deps=(e_l,) + e_so + self.wdeps("qscr"))
                self.rdep("scr1", e1)
                e2 = self.op("ve", lambda v: v.tensor_scalar(
                    self.qscr[:], self.qscr[:], R2, None, ALU.subtract))
                e3 = self.op("ve", lambda v, _ot=ot: v.tensor_scalar(
                    self.hid[:, _ot, :], self.qscr[:], self.scl[:, 12:13], None,
                    ALU.mult), deps=(e2,) + (self.wdeps("hid") if ot == 0 else ()))
                self.rdep("qscr", e3)
                oq_evs.append(e3)
            y_evs = []
            for oto in range(NI):
                bnk = oto % 2
                war = tuple(self.bank[bnk]["r"]); self.bank[bnk]["r"] = []
                eo = None
                for wave in range(2):
                    e_w = self.load_w(self.wqd["wd"].ap()[oto, 12 * wave:12 * wave + 12],
                                      bnk, 12, deps=(self.ev("te"), dq_done["wd"]))
                    for i in range(12):
                        f = 12 * wave + i
                        eo = self.mm(ps[bnk][:, :], self.wsb[:, bnk, i, :],
                                     self.hid[:, f, :], (wave == 0 and i == 0),
                                     (wave == 1 and i == 11),
                                     deps=(war + (e_w, oq_evs[f])) if (wave == 0 and i == 0)
                                     else (e_w, oq_evs[f]))
                    self.rdep(f"wsb{bnk}", eo)
                self.bank[bnk]["w"] = eo
                self.rdep("hid", eo)
                ec = self.op("se", lambda s, _o=oto, _b=bnk: s.activation(
                    self.cast_c[:, _o, :], ps[_b][:, :], AF.Copy),
                    deps=(eo,) + (self.wdeps("cast_c") if oto == 0 else ()))
                es2 = self.op("se", lambda s, _o=oto, _b=bnk: s.activation(
                    self.sq_c[:, _o, :], ps[_b][:, :], AF.Square),
                    deps=(eo,) + (self.wdeps("sq_c") if oto == 0 else ()))
                self.bank_read(bnk, ec)
                self.bank_read(bnk, es2)
                y_evs += [ec, es2]
            if tck == 0:
                self.tap("f2y0", self.cast_c[:].rearrange("p i t -> p (i t)"),
                         deps=tuple(y_evs))
            self.lnout_residual(tck, y_evs)

    def layer3():
        fe = self.fence("te", "ve", "se")
        e_ag = stage_ln_to_ag(fe)
        e_rope = load_rope(deps=fe)
        self.e_vones = self.op("ve", lambda v: v.memset(self.vtm[:, :, :, 64:66], 0.0),
                               deps=fe)
        self.e_vones = self.op("ve", lambda v: v.memset(self.vtm[:, :, :, 64:65], 1.0))
        e_wqa = self.dma("gp", self.wsb[:, 0, 0:8, 0:64],
                         din["l3_wqa"].ap().rearrange("(i p) r -> p i r", p=128),
                         deps=self.wdeps("wsb0") + fe)
        e_wkva = self.dma("gp", self.wsb[:, 0, 8:16, 0:32],
                          din["l3_wkva"].ap().rearrange("(i p) r -> p i r", p=128))
        e_wqb = self.dma("gp", self.wsb[0:64, 1, 0:4, :],
                         din["l3_wqb"].ap().rearrange("p (i o) -> p i o", o=128),
                         deps=self.wdeps("wsb1") + fe)
        e_wkvb = self.dma("gp", self.wsb[0:32, 1, 4:6, :],
                          din["l3_wkvb"].ap().rearrange("p (i o) -> p i o", o=128))
        e_pre = []
        for cc in range(NTC_S):
            buf = cc % 2
            e_rhs = self.load_rhs(buf, ag_rhs_ap(cc), deps=(e_ag,))
            its = [(self.wsb[:, 0, i, 0:64], self.rhs[:, buf, i, :], (e_rhs, e_wqa))
                   for i in range(NI)]
            ea = self.mmg(3, ps[3][0:64, :], its)
            self.rdep(f"rhs{buf}", ea)
            self.rdep("wsb0", ea)
            eaa = self.op("se", lambda s, _cc=cc: s.activation(
                self.aq[:, _cc * TC:(_cc + 1) * TC], ps[3][0:64, :], AF.Copy),
                deps=(ea,))
            self.bank_read(3, eaa)
            its = [(self.wsb[:, 0, 8 + i, 0:32], self.rhs[:, buf, i, :],
                    (e_rhs, e_wkva)) for i in range(NI)]
            ek = self.mmg(4, ps[4][0:32, :], its)
            self.rdep(f"rhs{buf}", ek)
            self.rdep("wsb0", ek)
            ekk = self.op("se", lambda s, _cc=cc: s.activation(
                self.rowbf[0:32, _cc * TC:(_cc + 1) * TC], ps[4][0:32, :], AF.Copy),
                deps=(ek,) + (self.wdeps("akv") if cc == 0 else ()))
            self.bank_read(4, ekk)
            e_pre += [eaa, ekk]
        e_pre = tuple(e_pre)
        hx_stage = []
        kv_evs = ()
        for g in range(4):
            kvh = g // 2
            evs = []
            for cc in range(NTC_S):
                sl = slice(cc * TC, (cc + 1) * TC)
                war = tuple(self.bank[0]["r"]); self.bank[0]["r"] = []
                eq = self.mm(ps[0][:, :], self.wsb[0:64, 1, g, :], self.aq[0:64, sl],
                             True, True, deps=war + e_pre + (e_wqb,))
                self.bank[0]["w"] = eq
                eeq = self.op("se", lambda s, _sl=sl: s.activation(
                    self.qf[:, _sl], ps[0][:, :], AF.Copy, scale=0.125), deps=(eq,))
                self.bank_read(0, eeq)
                war = tuple(self.bank[0]["r"]); self.bank[0]["r"] = []
                mrq = self.mm(ps[0][:, :], self.ropeRt[:], self.qf[:, sl], True, True,
                              deps=war + (eeq,) + e_rope)
                self.bank[0]["w"] = mrq
                e3 = rope_inplace(self.qf, cc, 0, (eeq,) + e_rope)
                evs.append(e3)
            if g % 2 == 0:
                for cc in range(NTC_S):
                    sl = slice(cc * TC, (cc + 1) * TC)
                    war = tuple(self.bank[1]["r"]); self.bank[1]["r"] = []
                    ek = self.mm(ps[1][0:64, :],
                                 self.wsb[0:32, 1, 4, 64 * kvh:64 * kvh + 64],
                                 self.rowbf[0:32, sl], True, True,
                                 deps=war + e_pre + (e_wkvb,))
                    self.bank[1]["w"] = ek
                    self.rdep("akv", ek)
                    eek = self.op("se", lambda s, _sl=sl: s.activation(
                        self.kf[0:64, _sl], ps[1][0:64, :], AF.Copy), deps=(ek,))
                    eek2 = self.op("se", lambda s, _sl=sl: s.activation(
                        self.kf[64:128, _sl], ps[1][0:64, :], AF.Copy))
                    self.bank_read(1, eek)
                    self.bank_read(1, eek2)
                    war = tuple(self.bank[1]["r"]); self.bank[1]["r"] = []
                    mrk = self.mm(ps[1][:, :], self.ropeRt[:],
                                  self.kf[:, sl], True, True, deps=war + (eek, eek2))
                    self.bank[1]["w"] = mrk
                    e4 = rope_inplace(self.kf, cc, 1, (eek, eek2) + e_rope)
                    evs.append(e4)
                for kt in range(32):
                    war = tuple(self.bank[2]["r"]); self.bank[2]["r"] = []
                    ev_ = self.mm(ps[2][0:128, 0:64],
                                  self.rowbf[0:32, kt * 128:(kt + 1) * 128],
                                  self.wsb[0:32, 1, 5, 64 * kvh:64 * kvh + 64],
                                  True, True, deps=war + e_pre + (e_wkvb,))
                    self.bank[2]["w"] = ev_
                    self.rdep("akv", ev_)
                    evv = self.op("ve", lambda v, _kt=kt: v.tensor_copy(
                        self.vtm[:, _kt, 0, 0:64], ps[2][0:128, 0:64]),
                        deps=(ev_, self.e_vones))
                    self.bank_read(2, evv)
                    evs.append(evv)
                kv_evs = tuple(evs)
            else:
                evs += list(kv_evs)
            e_out = attn_causal_group(g, lambda hh: (64 * hh, 0), tuple(evs))
            hx_stage.append(stage_outTg_to_hx(g, e_out))
        e_hx = self.coll("AllGather", ALU.bypass, self.PAIRS, [self.hx_in.ap()],
                         [self.hx_out.ap()], deps=tuple(hx_stage))
        out_proj_lnout([din["l3_wout"].ap()[o] for o in range(NI)], din["l3_bout"], e_hx)

    def ffn3():
        fe = self.fence("te", "ve", "se")
        e_cw = []
        for ot in range(8):
            e_cw.append(self.dma("gp", self.cnw1[:, ot],
                                 din["l3_cn_w1"].ap()[ot].rearrange("i p o -> p i o"),
                                 deps=fe if ot == 0 else ()))
        for ot in range(4):
            e_cw.append(self.dma("gp", self.cnw2[:, ot],
                                 din["l3_cn_w2"].ap()[ot].rearrange("i p o -> p i o"),
                                 deps=fe if ot == 0 else ()))
        for ot in range(2):
            e_cw.append(self.dma("gp", self.cnw3[:, ot],
                                 din["l3_cn_w3"].ap()[ot].rearrange("i p o -> p i o"),
                                 deps=fe if ot == 0 else ()))
        e_cw.append(self.dma("gp", self.cnw4[:], din["l3_cn_w4"].ap(), deps=fe))
        e_cw.append(self.dma("gp", self.gcw3[:], din["l3_gc_w3"].ap(), deps=fe))
        e_b = self.dma("gp", self.bias_sb[:, 0:8], din["l3_cn_b1"].ap(), deps=fe)
        e_b = self.dma("gp", self.bias_sb[:, 8:12], din["l3_cn_b2"].ap())
        e_b = self.dma("gp", self.bias_sb[:, 12:14], din["l3_cn_b3"].ap())
        e_b = self.dma("gp", self.bias_sb[0:1, 14:15], din["l3_cn_b4"].ap())
        e_b = self.dma("gp", self.bias_sb[:, 16:20], din["l3_gc_b1"].ap())
        e_b = self.dma("gp", self.bias_sb[:, 20:22], din["l3_gc_b2"].ap())
        e_b = self.dma("gp", self.bias_sb[0:1, 22:23], din["l3_gc_b3"].ap())
        e_b = self.dma("gp", self.bias_sb[0:1, 23:24], din["l3_ar"].ap())
        e_cw = tuple(e_cw) + (e_b,)
        e_A = self.op("ve", lambda v: v.tensor_scalar(
            self.colA[0:1, 0:1], self.bias_sb[0:1, 23:24], 2.0, None, ALU.mult),
            deps=(e_b,))
        e_B = self.op("ve", lambda v: v.tensor_scalar(
            self.colA[0:1, 1:2], self.bias_sb[0:1, 23:24], -1.0, 0.1,
            ALU.mult, ALU.add), deps=(e_b,))
        e_acc = None
        for tck in range(NTC_T):
            tsl = slice(tck * TC, (tck + 1) * TC)
            e_ln = self.ln_chunk(self.x[:, :, tsl], self.ln_c[:],
                                 self.wdeps("ln_c") + fe)
            e_r = self.op("ve", lambda v: v.reduce_sum(
                self.colA[:, 16:24], self.ln_c[:], AX), deps=(e_ln,))
            self.rdep("ln_c", e_r)
            if tck == 0:
                e_acc = self.op("ve", lambda v: v.tensor_copy(
                    self.colA[:, 24:32], self.colA[:, 16:24]), deps=(e_r,))
            else:
                e_acc = self.op("ve", lambda v: v.tensor_add(
                    self.colA[:, 24:32], self.colA[:, 24:32], self.colA[:, 16:24]),
                    deps=(e_r,))
        e_d1 = self.dma("gp", self.ctx_in.ap().rearrange("r (p i) -> p (r i)", p=128),
                        self.colA[:, 24:32], deps=(e_acc,))
        e_cc = self.coll("AllReduce", ALU.add, self.PAIRS, [self.ctx_in.ap()],
                         [self.ctx_out.ap()], deps=(e_d1,))
        e_d2 = self.dma("gp", self.colA[:, 24:32],
                        self.ctx_out.ap().rearrange("r (p i) -> p (r i)", p=128),
                        deps=(e_cc,))
        e_cm = self.op("ve", lambda v: v.tensor_scalar(
            self.colA[:, 24:32], self.colA[:, 24:32], 1.0 / S, None, ALU.mult),
            deps=(e_d2,))
        e_cb = self.op("ve", lambda v: v.tensor_copy(self.wnbf[:], self.colA[:, 24:32]),
                       deps=(e_cm,) + self.wdeps("wnbf"))
        gh_evs = []
        for ot in range(4):
            buf = ot % 2
            e_w = self.load_w(din["l3_gc_w1"].ap()[ot], buf, NI, deps=(self.ev("te"),))
            its = [(self.wsb[:, buf, i, :], self.wnbf[:, i:i + 1], (e_w, e_cb))
                   for i in range(NI)]
            eg = self.mmg(buf, ps[buf][0:128, 0:1], its)
            self.rdep(f"wsb{buf}", eg)
            self.rdep("wnbf", eg)
            ee = self.op("se", lambda s, _ot=ot, _b=buf: s.activation(
                self.gsm[:, _ot:_ot + 1], ps[_b][0:128, 0:1], AF.Gelu,
                bias=self.bias_sb[:, 16 + _ot:17 + _ot]),
                deps=(eg,) + (self.wdeps("gsm") if ot == 0 else ()))
            self.bank_read(buf, ee)
            gh_evs.append(ee)
        e_h1b = self.op("ve", lambda v: v.tensor_copy(self.wnf[:, 0:4], self.gsm[:, 0:4]),
                        deps=tuple(gh_evs) + self.wdeps("wnf"))
        self.rdep("gsm", e_h1b)
        e_h1c = self.op("ve", lambda v: v.tensor_copy(self.wnbf[:, 0:4], self.wnf[:, 0:4]),
                        deps=(e_h1b,) + self.wdeps("wnbf"))
        self.rdep("wnf", e_h1c)
        gh2_evs = []
        for ot in range(2):
            buf = ot % 2
            e_w = self.load_w(din["l3_gc_w2"].ap()[ot], buf, 4, deps=(self.ev("te"),))
            its = [(self.wsb[:, buf, i, :], self.wnbf[:, i:i + 1], (e_w, e_h1c))
                   for i in range(4)]
            eg = self.mmg(buf, ps[buf][0:128, 0:1], its)
            self.rdep(f"wsb{buf}", eg)
            self.rdep("wnbf", eg)
            ee = self.op("se", lambda s, _ot=ot, _b=buf: s.activation(
                self.gm2[:, _ot:_ot + 1], ps[_b][0:128, 0:1], AF.Gelu,
                bias=self.bias_sb[:, 20 + _ot:21 + _ot]),
                deps=(eg,) + (self.wdeps("gm2") if ot == 0 else ()))
            self.bank_read(buf, ee)
            gh2_evs.append(ee)
        e_h2b = self.op("ve", lambda v: v.tensor_copy(self.kglob[:, 0:2], self.gm2[:, 0:2]),
                        deps=tuple(gh2_evs) + self.wdeps("kglob"))
        self.rdep("gm2", e_h2b)
        its = [(self.gcw3[:, j:j + 1], self.kglob[:, j:j + 1], (e_h2b, e_cw))
               for j in range(2)]
        its = [(l, r, (e_h2b,) + e_cw) for (l, r, _) in its]
        eg3 = self.mmg(3, ps[3][0:1, 0:1], its)
        self.rdep("kglob", eg3)
        e_gf = self.op("se", lambda s: s.activation(
            self.colA[0:1, 2:3], ps[3][0:1, 0:1], AF.Sigmoid,
            bias=self.bias_sb[0:1, 22:23]), deps=(eg3,))
        self.bank_read(3, e_gf)
        e_gf7 = self.op("ve", lambda v: v.tensor_scalar(
            self.colA[0:1, 3:4], self.colA[0:1, 2:3], 0.7, None, ALU.mult),
            deps=(e_gf,))
        self.tap("f3gf", self.colA[0:1, 0:4], deps=(e_gf7,))
        self.tap("f3ctx", self.colA[:, 24:32], deps=(e_cm,))
        TW = ((0, 4, 1, -1), (1, 5, 1, -1), (2, 6, -1, 1), (3, 7, -1, 1))
        for tck in range(NTC_T):
            tsl = slice(tck * TC, (tck + 1) * TC)
            e_ln = self.ln_chunk(self.x[:, :, tsl], self.ln_c[:],
                                 self.wdeps("ln_c") + self.wdeps("oc32x") +
                                 tuple(self.bufdep.get("oc32", ())),
                                 dst_f32=self.oc32[:])
            self.bufdep["oc32"] = []
            e_lnb, e_lnf = e_ln
            last_evs = (e_lnb, e_lnf)
            for cyc in range(3):
                if cyc > 0:
                    e_mir = self.op("ve", lambda v: v.tensor_copy(
                        self.ln_c[:], self.oc32[:]),
                        deps=last_evs + self.wdeps("ln_c"))
                    e_src = (e_mir,)
                else:
                    e_src = (e_lnb,)
                h_evs = []
                for ot in range(8):
                    buf = ot % 2
                    its = [(self.cnw1[:, ot, i, :], self.ln_c[:, i, :],
                            e_src + e_cw) for i in range(NI)]
                    eo = self.mmg(buf, ps[buf][:, :], its)
                    self.rdep("ln_c", eo)
                    ee = self.op("se", lambda s, _ot=ot, _b=buf: s.activation(
                        self.h1[:, _ot, :], ps[_b][:, :], AF.Gelu,
                        bias=self.bias_sb[:, _ot:_ot + 1]),
                        deps=(eo,) + (self.wdeps("h1") if ot == 0 else ()))
                    self.bank_read(buf, ee)
                    h_evs.append(ee)
                h2_evs = []
                for ot in range(4):
                    buf = ot % 2
                    its = [(self.cnw2[:, ot, i, :], self.h1[:, i, :],
                            (h_evs[i],) + e_cw) for i in range(NI)]
                    eo = self.mmg(buf, ps[buf][:, :], its)
                    self.rdep("h1", eo)
                    ee = self.op("se", lambda s, _ot=ot, _b=buf: s.activation(
                        self.h2[:, _ot, :], ps[_b][:, :], AF.Gelu,
                        bias=self.bias_sb[:, 8 + _ot:9 + _ot]),
                        deps=(eo,) + (self.wdeps("h2") if ot == 0 else ()))
                    self.bank_read(buf, ee)
                    h2_evs.append(ee)
                h3_evs = []
                for ot in range(2):
                    buf = ot % 2
                    its = [(self.cnw3[:, ot, i, :], self.h2[:, i, :],
                            (h2_evs[i],) + e_cw) for i in range(4)]
                    eo = self.mmg(buf, ps[buf][:, :], its)
                    self.rdep("h2", eo)
                    ee = self.op("se", lambda s, _ot=ot, _b=buf: s.activation(
                        self.h3[:, _ot, :], ps[_b][:, :], AF.Gelu,
                        bias=self.bias_sb[:, 12 + _ot:13 + _ot]),
                        deps=(eo,) + (self.wdeps("h3") if ot == 0 else ()))
                    self.bank_read(buf, ee)
                    h3_evs.append(ee)
                its = [(self.cnw4[:, j:j + 1], self.h3[:, j, :],
                        (h3_evs[j],) + e_cw) for j in range(2)]
                etf = self.mmg(3, ps[3][0:1, :], its)
                self.rdep("h3", etf)
                e_tf = self.op("se", lambda s: s.activation(
                    self.rows[0:1, 0:512], ps[3][0:1, :], AF.Sigmoid,
                    bias=self.bias_sb[0:1, 14:15]), deps=(etf,))
                self.bank_read(3, e_tf)
                if tck == 0 and cyc == 0:
                    self.tap("f3tf", self.rows[0:1, 0:512], deps=(e_tf,))
                    self.tap("f3h1", self.h1[:].rearrange("p i t -> p (i t)"),
                             deps=tuple(h_evs))
                    self.tap("f3h2", self.h2[:].rearrange("p i t -> p (i t)"),
                             deps=tuple(h2_evs))
                    self.tap("f3h3", self.h3[:].rearrange("p i t -> p (i t)"),
                             deps=tuple(h3_evs))
                e_c = self.op("ve", lambda v: v.tensor_scalar(
                    self.rows[0:1, 512:1024], self.rows[0:1, 0:512], 0.3, None,
                    ALU.mult), deps=(e_tf, e_gf7))
                e_c = self.op("ve", lambda v: v.tensor_scalar(
                    self.rows[0:1, 512:1024], self.rows[0:1, 512:1024],
                    self.colA[0:1, 3:4], None, ALU.add), deps=(e_c,))
                e_c = self.op("ve", lambda v: v.tensor_scalar(
                    self.rows[0:1, 512:1024], self.rows[0:1, 512:1024],
                    self.colA[0:1, 0:1], None, ALU.mult), deps=(e_c, e_A))
                e_c = self.op("ve", lambda v: v.tensor_scalar(
                    self.rows[0:1, 512:1024], self.rows[0:1, 512:1024],
                    self.colA[0:1, 1:2], None, ALU.add), deps=(e_c, e_B))
                if tck == 0 and cyc == 0:
                    self.tap("f3c", self.rows[0:1, 512:1024], deps=(e_c,))
                ebc = self.bcast(6, self.rows[0:1, 512:1024], 512, deps=(e_c,))
                cyc_evs = []
                for (a, b, sa, sb) in TW:
                    e_t0 = self.op("ve", lambda v, _a=a: v.tensor_copy(
                        self.qscr[:], self.oc32[:, _a, :]),
                        deps=last_evs + self.wdeps("qscr"))
                    e_u1 = self.op("ve", lambda v, _b=b: v.tensor_tensor(
                        self.oevac[:], ps[6][:, :], self.oc32[:, _b, :], ALU.mult),
                        deps=(ebc, e_t0) + self.wdeps("oevac"))
                    self.bank_read(6, e_u1)
                    e_x1 = self.op("ve", lambda v, _a=a, _sa=sa: v.tensor_tensor(
                        self.oc32[:, _a, :], self.oc32[:, _a, :], self.oevac[:],
                        ALU.add if _sa > 0 else ALU.subtract), deps=(e_u1,))
                    self.rdep("oevac", e_x1)
                    e_u2 = self.op("ve", lambda v: v.tensor_tensor(
                        self.oevac[:], ps[6][:, :], self.qscr[:], ALU.mult),
                        deps=(e_x1,) + self.wdeps("oevac"))
                    self.bank_read(6, e_u2)
                    self.rdep("qscr", e_u2)
                    e_x2 = self.op("ve", lambda v, _b=b, _sb=sb: v.tensor_tensor(
                        self.oc32[:, _b, :], self.oc32[:, _b, :], self.oevac[:],
                        ALU.add if _sb > 0 else ALU.subtract), deps=(e_u2,))
                    self.rdep("oevac", e_x2)
                    cyc_evs += [e_x1, e_x2]
                last_evs = tuple(cyc_evs)
            if tck == 0:
                _et = self.tap("f3out", self.oc32[:].rearrange("p i t -> p (i t)"),
                               deps=last_evs)
                if _et:
                    self.rdep("oc32", _et)
            ecc = self.op("ve", lambda v: v.tensor_copy(self.cast_c[:], self.oc32[:]),
                          deps=last_evs + self.wdeps("cast_c"))
            esq = self.op("se", lambda s: s.activation(self.sq_c[:], self.oc32[:],
                                                       AF.Square),
                          deps=last_evs + self.wdeps("sq_c"))
            self.rdep("oc32", ecc)
            self.rdep("oc32", esq)
            e1, e2 = self._ln_stats((ecc, esq))
            self._ln_tail(e1, e2, sub_src=self.oc32[:], resid_tck=tck)

    def program():
        self.dma("gp", self.onesb[:], dco["ones_bf"].ap())
        self.dma("gp", self.onesf[:], dco["ones_f32"].ap())
        self.dma("gp", self.identb[:], dco["ident_bf"].ap())
        self.dma("gp", self.ropeRt[:], dco["ropeR_t"].ap())
        self.dma("gp", self.cmaskb[:], dco["cmask"].ap())
        self.dma("gp", self.dmaskb[:], dco["l1_dmask"].ap())
        e_const = self.ev("gp")
        e_x = self.dma("sy", self.x[:], din["x_in"].ap().rearrange("(i p) t -> p i t", p=128))
        self.rope_loaded = False
        self.h2048 = None
        self.x_last = None
        ensure_h("sy")

        ns = NSUBS
        layer0(e_x, e_const)
        if ns >= 2:
            ffn0()
        if n_layers >= 2 and ns >= 3:
            layer1()
        if n_layers >= 2 and ns >= 4:
            ffn1()
        if n_layers >= 3 and ns >= 5:
            layer2()
        if n_layers >= 3 and ns >= 6:
            ffn2()
        if n_layers >= 4 and ns >= 7:
            layer3()
        if n_layers >= 4 and ns >= 8:
            ffn3()

        e_out = self.dma("sy", self.dout.ap().rearrange("(i p) t -> p i t", p=128),
                         self.x[:], deps=(self.ev("ve"), self.ev("se"), self.ev("te")))
        self.wait_on("sy", e_out)
        self.wait_on("gp", e_out)
        self.wait_on("ve", e_out)

    return program


def kernel(**inputs):
    nc = bass.Bass("TRN2", target_bir_lowering=False)
    staged = [_stage_core(c, inputs) for c in range(8)]
    shapes = {k: v.shape for k, v in staged[0].items()}
    kb = KB(nc, shapes, TAPS)
    program = build_program(kb, N_LAYERS)

    def run_pass(name, handle):
        kb.active = name
        kb.H[name] = handle
        for k in kb.count:
            kb.count[k] = 0
        kb.bufdep = {}
        kb.bank = [dict(w=None, r=[]) for _ in range(8)]
        program()

    blk = kb.blk

    @blk.gpsimd
    def _(g):
        run_pass("gp", g)

    @blk.sync
    def _(s):
        run_pass("sy", s)

    @blk.tensor
    def _(t):
        run_pass("te", t)

    @blk.vector
    def _(v):
        run_pass("ve", v)

    @blk.scalar
    def _(s):
        run_pass("se", s)

    kb.blk.__exit__(None, None, None)
    kb.deadcheck()

    in_maps = [dict(stc) for stc in staged]
    try:
        res = run_bass_kernel_spmd(nc, in_maps, core_ids=list(range(8)))
    finally:
        kb.es.close()
    out = np.zeros((B, S, D), np.float32)
    for c in range(8):
        b, h = c // 2, c % 2
        out[b, 2048 * h:2048 * h + 2048, :] = res.results[c]["out"].T
    if TAPS:
        kernel.tap_results = [
            {n: r[f"tap_{n}"] for n in TAPS} for r in res.results]
    return out


# version 04 (final)
# speedup vs baseline: 1.0862x; 1.0862x over previous
import numpy as np
import ml_dtypes
from contextlib import ExitStack

import concourse.bass as bass
import concourse.mybir as mybir
from concourse.bass_utils import run_bass_kernel_spmd

F32 = mybir.dt.float32
BF16 = mybir.dt.bfloat16
U8 = mybir.dt.uint8
AF = mybir.ActivationFunctionType
ALU = mybir.AluOpType
AX = mybir.AxisListType.X

B, S, D = 4, 4096, 1024
T = 2048
NI = D // 128
TC = 512
NTC_T = T // TC
NTC_S = S // TC
R2 = float(1.5 * 2.0 ** 23)
NEG = -30000.0

N_LAYERS = 4
NSUBS = 99
L2CUT = 31
F2CUT = 63
F2P1 = 15
TAPS = {}


def lhsT_tiles(w):
    K, M = w.shape
    return np.ascontiguousarray(
        w.reshape(K // 128, 128, M // 128, 128).transpose(2, 0, 1, 3))


def _stage_core(c, I):
    h = c % 2
    hs = slice(512 * h, 512 * h + 512)
    st = {}

    def bt(v):
        return np.ascontiguousarray(v.reshape(-1, 128).T)

    st["l0_wq"] = lhsT_tiles(I["L0_attn_Wq"].T[:, hs].copy())
    st["l0_wk"] = lhsT_tiles(I["L0_attn_Wk"].T[:, hs].copy())
    st["l0_wv"] = lhsT_tiles(I["L0_attn_Wv"].T[:, hs].copy())
    st["l0_wo"] = lhsT_tiles(np.ascontiguousarray(I["L0_attn_Wo"].T))
    st["l0_wfc_t"] = np.ascontiguousarray(I["L0_ffn_W_fc"].T)
    st["l0_bfc"] = bt(I["L0_ffn_b_fc"])
    st["l0_afc_t"] = np.ascontiguousarray(I["L0_ffn_A_fc"].T)
    st["l0_bfc_l"] = np.ascontiguousarray(I["L0_ffn_B_fc"].T)
    st["l0_wpr_t"] = np.ascontiguousarray(I["L0_ffn_W_proj"].T)
    st["l0_bpr"] = bt(I["L0_ffn_b_proj"])
    st["l0_apr_t"] = np.ascontiguousarray(I["L0_ffn_A_proj"].T)
    st["l0_bpr_l"] = np.ascontiguousarray(I["L0_ffn_B_proj"].T)

    st["l1_wq"] = lhsT_tiles(I["L1_attn_Wq"][:, hs].copy())
    st["l1_wk"] = lhsT_tiles(I["L1_attn_Wk"][:, hs].copy())
    st["l1_wv"] = lhsT_tiles(I["L1_attn_Wv"][:, hs].copy())
    st["l1_wo"] = lhsT_tiles(I["L1_attn_Wo"].copy())
    st["l1_bq"] = bt(I["L1_attn_bq"][hs])
    st["l1_bk"] = bt(I["L1_attn_bk"][hs])
    st["l1_bv_row"] = np.ascontiguousarray(I["L1_attn_bv"][None, hs])
    st["l1_bo"] = bt(I["L1_attn_bo"])
    st["l1_gw"] = np.ascontiguousarray(I["L1_ffn_gate_w"])
    st["l1_gb"] = np.ascontiguousarray(I["L1_ffn_gate_b"][None, :])
    st["l1_ew"] = np.ascontiguousarray(
        np.stack([lhsT_tiles(I["L1_ffn_expert_w"][e]) for e in range(8)]))
    st["l1_eb"] = np.ascontiguousarray(I["L1_ffn_expert_b"])

    W = I["L2_attn_W_qkv"]
    st["l2_wqkv"] = lhsT_tiles(np.ascontiguousarray(
        np.concatenate([W[:, 512 * h:512 * h + 512],
                        W[:, 1024 + 512 * h:1024 + 512 * h + 512],
                        W[:, 2048 + 512 * h:2048 + 512 * h + 512]], axis=1)))
    wf2 = np.zeros((128, 128), np.float32)
    wf2[0:64, 0:64] = I["L2_attn_W_f"]
    wf2[64:128, 64:128] = I["L2_attn_W_f"]
    st["l2_wf2"] = wf2
    bf2 = np.concatenate([I["L2_attn_b_f"], I["L2_attn_b_f"]])
    st["l2_bf2c"] = np.ascontiguousarray(bf2[:, None])
    st["l2_bf2r"] = np.ascontiguousarray(bf2[None, :])
    st["l2_wout"] = lhsT_tiles(I["L2_attn_W_out"].copy())
    st["l2_bout"] = bt(I["L2_attn_b_out"])
    st["l2_wg"] = np.ascontiguousarray(I["L2_ffn_w_gate"])
    st["l2_wu"] = np.ascontiguousarray(I["L2_ffn_w_up"])
    st["l2_wd"] = np.ascontiguousarray(I["L2_ffn_w_down"])
    st["l2_wg_t"] = np.ascontiguousarray(I["L2_ffn_w_gate"].T)
    st["l2_wu_t"] = np.ascontiguousarray(I["L2_ffn_w_up"].T)
    st["l2_wd_t"] = np.ascontiguousarray(I["L2_ffn_w_down"].T)

    st["l3_wqa"] = np.ascontiguousarray(I["L3_attn_Wqa"])
    st["l3_wqb"] = np.ascontiguousarray(I["L3_attn_Wqb"][:, hs])
    st["l3_wkva"] = np.ascontiguousarray(I["L3_attn_Wkva"])
    kvb = I["L3_attn_Wkvb"]
    cols_k = [((kh * 64 + d) * 2 + 0) for kh in (2 * h, 2 * h + 1) for d in range(64)]
    cols_v = [((kh * 64 + d) * 2 + 1) for kh in (2 * h, 2 * h + 1) for d in range(64)]
    st["l3_wkvb"] = np.ascontiguousarray(kvb[:, cols_k + cols_v])
    st["l3_wout"] = lhsT_tiles(I["L3_attn_Wout"].copy())
    st["l3_bout"] = bt(I["L3_attn_bout"])
    st["l3_cn_w1"] = lhsT_tiles(I["L3_ffn_cn_w1"].copy())
    st["l3_cn_w2"] = lhsT_tiles(I["L3_ffn_cn_w2"].copy())
    st["l3_cn_w3"] = lhsT_tiles(I["L3_ffn_cn_w3"].copy())
    st["l3_cn_b1"] = bt(I["L3_ffn_cn_b1"])
    st["l3_cn_b2"] = bt(I["L3_ffn_cn_b2"])
    st["l3_cn_b3"] = bt(I["L3_ffn_cn_b3"])
    st["l3_cn_w4"] = np.ascontiguousarray(I["L3_ffn_cn_w4"].reshape(2, 128).T)
    st["l3_cn_b4"] = np.ascontiguousarray(I["L3_ffn_cn_b4"][None, :])
    st["l3_gc_w1"] = lhsT_tiles(I["L3_ffn_gc_w1"].copy())
    st["l3_gc_w2"] = lhsT_tiles(I["L3_ffn_gc_w2"].copy())
    st["l3_gc_w3"] = np.ascontiguousarray(I["L3_ffn_gc_w3"].reshape(2, 128).T)
    st["l3_gc_b1"] = bt(I["L3_ffn_gc_b1"])
    st["l3_gc_b2"] = bt(I["L3_ffn_gc_b2"])
    st["l3_gc_b3"] = np.ascontiguousarray(I["L3_ffn_gc_b3"][None, :])
    st["l3_ar"] = np.ascontiguousarray(I["L3_ffn_adaptive_range"][None, :])

    b = c // 2
    st["x_in"] = np.ascontiguousarray(I["L0_attn_x"][b, 2048 * h:2048 * h + 2048, :].T)
    return st


def _constants():
    co = {}
    co["ones_bf"] = np.ones((128, 128), ml_dtypes.bfloat16)
    co["ones_f32"] = np.ones((1, 128), np.float32)
    co["ident_bf"] = np.eye(128, dtype=ml_dtypes.bfloat16)
    co["ident_f32"] = np.eye(128, dtype=np.float32)
    pos = np.arange(S, dtype=np.float64)
    j = (np.arange(128) % 64) // 2
    inv = 10000.0 ** (-2.0 * j / 64.0)
    ang = pos[None, :] * inv[:, None]
    co["rope_cos"] = np.cos(ang).astype(ml_dtypes.bfloat16)
    co["rope_sin"] = np.sin(ang).astype(ml_dtypes.bfloat16)
    Rm = np.zeros((128, 128), np.float32)
    for blk in range(2):
        for i in range(32):
            a = blk * 64 + 2 * i
            Rm[a, a + 1] = -1.0
            Rm[a + 1, a] = 1.0
    co["ropeR_t"] = np.ascontiguousarray(Rm.T).astype(ml_dtypes.bfloat16)
    cm = np.zeros((128, 4, 512), np.float32)
    for ridx, r in enumerate((0, 128, 256, 384)):
        kk = np.arange(128)[:, None] + r
        qq = np.arange(512)[None, :]
        cm[:, ridx, :][kk > qq] = NEG
    co["cmask"] = cm.astype(ml_dtypes.bfloat16)
    m = np.zeros((128, 128), np.float32)
    kb_ = np.arange(128)[:, None] // 64
    qb_ = np.arange(128)[None, :] // 64
    m[kb_ != qb_] = NEG
    co["l1_dmask"] = m.astype(ml_dtypes.bfloat16)
    gp = np.linspace(0, S - 1, 8).astype(np.int64)
    gm = np.zeros((8, S), np.float32)
    qblk = np.arange(S) // 64
    for gi, g in enumerate(gp):
        gm[gi, qblk == (g // 64)] = NEG
    co["l1_gmask"] = gm.astype(ml_dtypes.bfloat16)
    ind8 = np.zeros((8, 8, 128), np.float32)
    for e in range(8):
        ind8[e, e, :] = 1.0
    co["ind8"] = ind8.reshape(8, 8 * 128).astype(ml_dtypes.bfloat16)
    b4 = np.zeros((4, 128), np.float32)
    for k in range(4):
        b4[k, 32 * k:32 * k + 32] = 1.0
    co["blkind4"] = b4
    return co


L1_GP = np.linspace(0, S - 1, 8).astype(np.int64)
GP0, GPSTEP = int(L1_GP[0]), int(L1_GP[1] - L1_GP[0])


class KB:
    def __init__(self, nc, st_shapes, taps):
        self.nc = nc
        self.es = ExitStack()
        es = self.es
        self.count = {}
        self.sems = {}
        self.waited = {}
        self.insts = []
        self.H = {}
        self.active = None
        for name in ("gp", "sy", "te", "ve", "se", "cc"):
            self.sems[name] = es.enter_context(nc.semaphore(name))
            self.count[name] = 0

        self.din = {n: nc.dram_tensor(n, list(s), F32, kind="ExternalInput")
                    for n, s in st_shapes.items()}
        self.dout = nc.dram_tensor("out", [D, T], F32, kind="ExternalOutput")
        co = _constants()
        self.dco = {n: nc.inline_tensor(np.ascontiguousarray(a), f"co_{n}")
                    for n, a in co.items()}
        self.tapd = {n: nc.dram_tensor(f"tap_{n}", list(s), F32, kind="ExternalOutput")
                     for n, s in taps.items()}

        self.ag_in = nc.dram_tensor("ag_in", [D, T], BF16, kind="Internal")
        self.ag_out = nc.dram_tensor("ag_out", [2, D, T], BF16, kind="Internal")
        self.hx_in = nc.dram_tensor("hx_in", [512, S], BF16, kind="Internal")
        self.hx_out = nc.dram_tensor("hx_out", [2, 512, S], BF16, kind="Internal")
        self.armx_in = nc.dram_tensor("armx_in", [1, 8], F32, kind="Internal")
        self.armx_out = nc.dram_tensor("armx_out", [1, 8], F32, kind="Internal",
                                       addr_space="Shared")
        self.ctx_in = nc.dram_tensor("ctx_in", [1, D], F32, kind="Internal")
        self.ctx_out = nc.dram_tensor("ctx_out", [1, D], F32, kind="Internal")
        self.fdr = {
            "up": nc.dram_tensor("fdr_up", [24, 128, T], BF16, kind="Internal"),
            "gate": nc.dram_tensor("fdr_gate", [24, 128, T], BF16, kind="Internal"),
        }
        self.wqd = {
            "wu": nc.dram_tensor("wq_wu", [24, 8, 128, 128], BF16, kind="Internal"),
            "wg": nc.dram_tensor("wq_wg", [24, 8, 128, 128], BF16, kind="Internal"),
            "wd": nc.dram_tensor("wq_wd", [8, 24, 128, 128], BF16, kind="Internal"),
            "fc": nc.dram_tensor("wq_fc", [32, 8, 128, 128], BF16, kind="Internal"),
            "pr": nc.dram_tensor("wq_pr", [8, 32, 128, 128], BF16, kind="Internal"),
        }
        self.PAIRS = [[0, 1], [2, 3], [4, 5], [6, 7]]
        self.ALL8 = [[0, 1, 2, 3, 4, 5, 6, 7]]

        sbt = lambda nm, shp, dt: es.enter_context(nc.sbuf_tensor(nm, shp, dt))
        self.x = sbt("x", [128, NI, T], F32)
        self.cast_c = sbt("cast_c", [128, NI, TC], BF16)
        self.sq_c = sbt("sq_c", [128, NI, TC], BF16)
        self.tmp_c = sbt("tmp_c", [128, NI, TC], BF16)
        self.ln_c = sbt("ln_c", [128, NI, TC], BF16)
        self.rows = sbt("rows", [1, 2048], F32)
        self.rhs = sbt("rhs", [128, 2, NI, TC], BF16)
        self.wsb = sbt("wsb", [128, 2, 16, 128], BF16)
        self.onesb = sbt("onesb", [128, 128], BF16)
        self.onesf = sbt("onesf", [1, 128], F32)
        self.identb = sbt("identb", [128, 128], BF16)
        self.ropeRt = sbt("ropeRt", [128, 128], BF16)
        self.cmaskb = sbt("cmaskb", [128, 4, 512], BF16)
        self.dmaskb = sbt("dmaskb", [128, 128], BF16)
        self.oevac = sbt("oevac", [128, 512], F32)
        self.scl = sbt("scl", [128, 16], F32)
        self.colA = sbt("colA", [128, 32], F32)
        self.bias_sb = sbt("bias_sb", [128, 40], F32)
        self.rowbf = sbt("rowbf", [64, 4096], BF16)
        AR = 64 * 1024
        self.arena = sbt("arena", [128, AR], U8)
        a0 = nc.lookup_mloc(self.arena).addr

        def at(nm, shp, dt, off):
            return nc.alloc_sbuf_tensor_at(nm, shp, dt, offset=a0 + off)

        self.cosb = at("cosb", [128, S], BF16, 0)
        self.sinb = at("sinb", [128, S], BF16, 8192)
        self.gmaskb = at("gmaskb", [8, S], BF16, 0)
        self.bfc_l = at("bfc_l", [16, 4096], BF16, 0)
        self.bpr_l = at("bpr_l", [16, 1024], BF16, 8192)
        self.afc = at("afc", [128, 8, 16], BF16, 10240)
        self.apr = at("apr", [128, 32, 16], BF16, 10496)
        self.ebs = at("ebs", [8, 1024], BF16, 0)
        self.ind8 = at("ind8", [8, 1024], BF16, 2048)
        self.gwsb = at("gwsb", [128, 8, 8], BF16, 4096)
        self.gbrow = at("gbrow", [1, 8], BF16, 4352)
        self.w8 = at("w8", [8, 512], BF16, 4608)
        self.wnbf = at("wnbf", [128, 8], BF16, 5632)
        self.l2wf2 = at("l2wf2", [128, 128], BF16, 0)
        self.l2bfc = at("l2bfc", [128, 1], F32, 256)
        self.l2bfr = at("l2bfr", [1, 128], BF16, 384)
        self.blk4 = at("blk4", [4, 128], F32, 640)
        self.stage4 = at("stage4", [4, 512], F32, 1152)
        self.kvw = at("kvw", [32, 256], BF16, 256)
        self.wqa = at("wqa", [128, 8, 32], BF16, 768)
        self.wqa2 = at("wqa2", [128, 8, 64], BF16, 1280)
        self.wqb = at("wqb", [64, 512], BF16, 2304)
        self.cnw4 = at("cnw4", [128, 2], BF16, 3328)
        self.qscr = at("qscr", [128, 512], F32, 13312)
        self.l1bv = at("l1bv", [1, 512], BF16, 5888)
        self.kglob = at("kglob", [128, 8], BF16, 6912)
        self.vglob = at("vglob", [40, 66], BF16, 7168)
        self.qglob = at("qglob", [128, 8], BF16, 7424)
        self.gsm = at("gsm", [128, 8], F32, 13312)
        self.gm2 = at("gm2", [128, 8], F32, 13824)
        self.wnf = at("wnf", [128, 8], F32, 14336)
        self.gcw3 = at("gcw3", [128, 2], BF16, 3392)
        self.wt_a = at("wt_a", [128, 512], F32, 16384)
        self.wt_b = at("wt_b", [128, 512], F32, 18432)
        self.wt2 = at("wt2", [128, 512], F32, 20480)
        self.wdq = at("wdq", [128, 512], BF16, 22528)
        self.smx = at("smx", [128, 96], F32, 23552)
        self.sS = at("sS", [128, 96], F32, 24064)
        self.sT = at("sT", [96, 3072], F32, 43008)
        self.lnbuf = at("lnbuf", [128, NI, T], BF16, 16384)
        self.aq = at("aq", [64, S], BF16, 55552)
        self.cnw1 = at("cnw1", [128, 8, 8, 128], BF16, 16384)
        self.cnw2 = at("cnw2", [128, 4, 8, 128], BF16, 32768)
        self.h1 = at("h1", [128, 8, TC], BF16, 40960)
        self.cnw3 = at("cnw3", [128, 2, 4, 128], BF16, 4096)
        self.h2 = at("h2", [128, 4, TC], BF16, 6144)
        self.h3 = at("h3", [128, 2, TC], BF16, 10240)
        self.qf = at("qf", [128, S], BF16, 16384)
        self.kf = at("kf", [128, S], BF16, 24576)
        self.vtm = at("vtm", [128, 32, 2, 66], BF16, 32768)
        self.outTg = at("outTg", [128, S], BF16, 41216)
        self.kva = at("kva", [32, S], BF16, 41216)
        self.expS = at("expS", [128, 4, 512], BF16, 49408)
        self.scr1 = at("scr1", [128, 512], BF16, 53504)
        self.scr2 = at("scr2", [128, 512], BF16, 54528)
        self.hid = at("hid", [128, 32, TC], BF16, 16384)
        self.oc32 = at("oc32", [128, NI, TC], F32, 49152)
        self.hu = at("hu", [128, 12, TC], BF16, 16384)
        self.hg = at("hg", [128, 12, TC], BF16, 28672)
        self.hs_ = at("hs_", [128, 12, TC], BF16, 40960)

        self.ps = [es.enter_context(nc.psum_tensor(f"ps{i}", [128, 512], F32))
                   for i in range(7)]
        self.pbf = es.enter_context(nc.psum_tensor("pbf", [128, 256], BF16))
        self.bank = [dict(w=None, r=[]) for _ in range(8)]
        self.bufdep = {}
        self.blk = nc.Block()
        self.blk.__enter__()
        self.h2048 = None
        self.x_last = None

    def ENG(self, n):
        return self.active == n

    def wait_on(self, name, *evs):
        if not self.ENG(name):
            return
        for e in evs:
            if e is None:
                continue
            s, v = e
            if v <= 0:
                continue
            key = (name, s)
            if self.waited.get(key, 0) >= v:
                continue
            self.waited[key] = v
            self.H[name].wait_ge(self.sems[s], v)
            self.insts.append((name, [(s, v)], []))

    def dma(self, q, out, in_, deps=()):
        self.wait_on(q, *deps)
        self.count[q] += 16
        if self.ENG(q):
            o = out() if callable(out) else out
            i = in_() if callable(in_) else in_
            self.H[q].dma_start(out=o, in_=i).then_inc(self.sems[q], 16)
            self.insts.append((q, [], [(q, 16)]))
        return (q, self.count[q])

    def coll(self, kind, aop, groups, ins, outs, deps=()):
        self.wait_on("gp", *deps)
        self.count["cc"] += 1
        if self.ENG("gp"):
            self.H["gp"].collective_compute(
                kind, aop, replica_groups=groups, ins=ins, outs=outs
            ).then_inc(self.sems["cc"], 1)
            self.insts.append(("gp", [], [("cc", 1)]))
        return ("cc", self.count["cc"])

    def op(self, engname, f, deps=()):
        self.wait_on(engname, *deps)
        self.count[engname] += 1
        if self.ENG(engname):
            f(self.H[engname]).then_inc(self.sems[engname], 1)
            self.insts.append((engname, [], [(engname, 1)]))
        return (engname, self.count[engname])

    def ev(self, name):
        return (name, self.count[name])

    def mm(self, psum_ap, lhsT, rhs_, start, stop, deps=(), tp=None):
        self.wait_on("te", *deps)
        self.count["te"] += 1
        if self.ENG("te"):
            self.H["te"].matmul(psum_ap, lhsT, rhs_, start=start, stop=stop,
                                tile_position=tp).then_inc(self.sems["te"], 1)
            self.insts.append(("te", [], [("te", 1)]))
        return ("te", self.count["te"])

    def mmg(self, b, region, items, extra_deps=()):
        war = tuple(self.bank[b]["r"]) + tuple(extra_deps)
        self.bank[b]["r"] = []
        e = None
        n = len(items)
        for idx, (l, r, dp) in enumerate(items):
            e = self.mm(region, l, r, idx == 0, idx == n - 1,
                        deps=(war + tuple(dp)) if idx == 0 else tuple(dp))
        self.bank[b]["w"] = e
        return e

    def bank_read(self, b, ev_):
        self.bank[b]["r"].append(ev_)
        return ev_

    def wdeps(self, buf):
        d = tuple(self.bufdep.get(buf, ()))
        self.bufdep[buf] = []
        return d

    def rdep(self, buf, ev_):
        self.bufdep.setdefault(buf, []).append(ev_)

    def tap(self, name, ap, deps=()):
        if name in self.tapd:
            return self.dma("gp", self.tapd[name].ap(), ap, deps=deps)

    def bcast(self, b, row_ap, n, nparts=128, deps=()):
        war = tuple(self.bank[b]["r"])
        self.bank[b]["r"] = []
        e = self.mm(self.ps[b][0:nparts, 0:n], self.onesf[:, 0:nparts], row_ap, True, True,
                    deps=war + tuple(deps))
        self.bank[b]["w"] = e
        return e

    def bscal(self, slot, sc_ap, deps=(), scale=1.0):
        e = self.bcast(6, sc_ap, 1, deps=deps)
        ee = self.op("ve", lambda v: v.tensor_scalar(
            self.scl[:, slot:slot + 1], self.ps[6][:, 0:1], scale, None, ALU.mult),
            deps=(e,))
        self.bank_read(6, ee)
        return ee

    def _ln_stats(self, y_evs):
        its = [(self.onesb[:, 0:1], self.cast_c[:, i, :], tuple(y_evs)) for i in range(NI)]
        e1 = self.mmg(5, self.ps[5][0:1, 0:512], its)
        its = [(self.onesb[:, 0:1], self.sq_c[:, i, :], tuple(y_evs)) for i in range(NI)]
        e2 = self.mmg(4, self.ps[4][0:1, 0:512], its)
        self.rdep("cast_c", e1)
        self.rdep("sq_c", e2)
        return e1, e2

    def _ln_tail(self, e_sum, e_sumsq, dst_bf=None, dst_f32=None, sub_src=None,
                 resid_tck=None):
        rows = self.rows
        em = self.op("ve", lambda v: v.tensor_scalar(rows[:, 0:512], self.ps[5][0:1, :],
                                                     1.0 / D, None, ALU.mult),
                     deps=(e_sum,))
        self.bank_read(5, em)
        es_ = self.op("ve", lambda v: v.tensor_scalar(rows[:, 512:1024], self.ps[4][0:1, :],
                                                      1.0 / D, None, ALU.mult),
                      deps=(e_sumsq,))
        self.bank_read(4, es_)
        self.op("ve", lambda v: v.tensor_mul(rows[:, 1024:1536], rows[:, 0:512],
                                             rows[:, 0:512]))
        self.op("ve", lambda v: v.tensor_sub(rows[:, 1024:1536], rows[:, 512:1024],
                                             rows[:, 1024:1536]))
        ev3 = self.op("ve", lambda v: v.tensor_scalar(rows[:, 1024:1536], rows[:, 1024:1536],
                                                      1e-5, None, ALU.add))
        esq2 = self.op("se", lambda s: s.activation(rows[:, 1536:2048], rows[:, 1024:1536],
                                                    AF.Sqrt), deps=(ev3,))
        er = self.op("ve", lambda v: v.reciprocal(rows[:, 1024:1536], rows[:, 1536:2048]),
                     deps=(esq2,))
        ebm = self.bcast(5, rows[:, 0:512], 512, deps=(er,))
        ebr = self.bcast(4, rows[:, 1024:1536], 512, deps=(er,))
        sub_in = sub_src if sub_src is not None else self.cast_c[:]
        bm = self.ps[5][:, :].rearrange("p (u n) -> p u n", u=1).broadcast_to([128, NI, TC])
        br = self.ps[4][:, :].rearrange("p (u n) -> p u n", u=1).broadcast_to([128, NI, TC])
        en1 = self.op("ve", lambda v: v.tensor_sub(self.tmp_c[:], sub_in, bm),
                      deps=(ebm,) + self.wdeps("tmp_c"))
        self.bank_read(5, en1)
        if resid_tck is not None:
            en2 = self.op("ve", lambda v: v.tensor_mul(self.tmp_c[:], self.tmp_c[:], br),
                          deps=(ebr,))
            self.bank_read(4, en2)
            xs = self.x[:, :, resid_tck * TC:(resid_tck + 1) * TC]
            eadd = self.op("ve", lambda v: v.tensor_add(xs, xs, self.tmp_c[:]),
                           deps=(en2,))
            self.rdep("tmp_c", eadd)
            self.x_last = eadd
            return eadd
        outs = []
        if dst_bf is not None:
            en2 = self.op("ve", lambda v: v.tensor_mul(dst_bf, self.tmp_c[:], br),
                          deps=(ebr, en1))
            self.bank_read(4, en2)
            outs.append(en2)
        if dst_f32 is not None:
            en3 = self.op("ve", lambda v: v.tensor_mul(dst_f32, self.tmp_c[:], br),
                          deps=(ebr, en1))
            self.bank_read(4, en3)
            outs.append(en3)
        self.rdep("tmp_c", outs[-1])
        return outs[0] if len(outs) == 1 else tuple(outs)

    def ln_chunk(self, src_ap, dst_bf, src_evs, dst_f32=None):
        src_evs = tuple(src_evs) + ((self.x_last,) if self.x_last else ())
        ecast = self.op("ve", lambda v: v.tensor_copy(self.cast_c[:], src_ap),
                        deps=tuple(src_evs) + self.wdeps("cast_c"))
        esq = self.op("se", lambda s: s.activation(self.sq_c[:], src_ap, AF.Square),
                      deps=tuple(src_evs) + self.wdeps("sq_c"))
        e1, e2 = self._ln_stats((ecast, esq))
        return self._ln_tail(e1, e2, dst_bf=dst_bf, dst_f32=dst_f32, sub_src=src_ap)

    def lnout_residual(self, tck, y_evs):
        e1, e2 = self._ln_stats(tuple(y_evs))
        return self._ln_tail(e1, e2, resid_tck=tck)

    def load_w(self, dram_ap, buf, ni, deps=()):
        d = self.wdeps(f"wsb{buf}") + tuple(deps)
        return self.dma("gp", self.wsb[:, buf, 0:ni, :],
                        dram_ap.rearrange("i p o -> p i o"), deps=d)

    def load_rhs(self, buf, src_ap, deps=()):
        d = self.wdeps(f"rhs{buf}") + tuple(deps)
        return self.dma("sy", self.rhs[:, buf], src_ap, deps=d)

    def transpose(self, out_ap, in_ap, deps=(), ident=None):
        self.wait_on("te", *deps)
        self.count["te"] += 1
        if self.ENG("te"):
            iap = self.identb[:] if ident is None else ident
            self.H["te"].transpose(out_ap, in_ap, iap).then_inc(self.sems["te"], 1)
            self.insts.append(("te", [], [("te", 1)]))
        return ("te", self.count["te"])

    def fence(self, *engs):
        return tuple((e, self.count[e]) for e in engs)

    def deadcheck(self):
        streams = {}
        for (e, w, inc) in self.insts:
            streams.setdefault(e, []).append((w, inc))
        ptr = {e: 0 for e in streams}
        cnt = {s: 0 for s in self.sems}
        prog = True
        while prog:
            prog = False
            for e, lst in streams.items():
                while ptr[e] < len(lst):
                    w, inc = lst[ptr[e]]
                    if all(cnt[s] >= v for (s, v) in w):
                        for (s, a) in inc:
                            cnt[s] += a
                        ptr[e] += 1
                        prog = True
                    else:
                        break
        stuck = {e: lst[ptr[e]][0] for e, lst in streams.items() if ptr[e] < len(lst)}
        if stuck:
            raise RuntimeError(f"deadlock sim: stuck {stuck} counts {cnt} "
                               f"ptrs { {e: (ptr[e], len(l)) for e, l in streams.items()} }")


def build_program(kb, n_layers):
    self = kb
    din, dco, ps = kb.din, kb.dco, kb.ps

    def load_rope(deps=()):
        e1 = self.dma("gp", self.cosb[:], dco["rope_cos"].ap(), deps=deps)
        e2 = self.dma("gp", self.sinb[:], dco["rope_sin"].ap(), deps=deps)
        return (e1, e2)

    def stage_ln_to_ag(first_evs):
        stage = []
        for tck in range(NTC_T):
            src = self.x[:, :, tck * TC:(tck + 1) * TC]
            e_ln = self.ln_chunk(src, self.ln_c[:], tuple(first_evs) + self.wdeps("ln_c"))
            ed = self.dma("sy", self.ag_in.ap().rearrange("(i p) t -> p i t", p=128)
                          [:, :, tck * TC:(tck + 1) * TC], self.ln_c[:], deps=(e_ln,))
            self.rdep("ln_c", ed)
            stage.append(ed)
        return self.coll("AllGather", ALU.bypass, self.PAIRS,
                         [self.ag_in.ap()], [self.ag_out.ap()], deps=tuple(stage))

    def ag_rhs_ap(cc):
        half = (cc * TC) // T
        toff = (cc * TC) % T
        return self.ag_out.ap()[half].rearrange("(i p) t -> p i t", p=128)[:, :, toff:toff + TC]

    def rope_inplace(dst, cc, psbank, deps, nrows=128):
        sl = slice(cc * TC, (cc + 1) * TC)
        e1 = self.op("ve", lambda v: v.tensor_mul(self.scr1[0:nrows, :], dst[0:nrows, sl],
                                                  self.cosb[0:nrows, sl]),
                     deps=tuple(deps) + self.wdeps("scr1"))
        e2 = self.op("ve", lambda v: v.tensor_mul(self.scr2[0:nrows, :],
                                                  self.ps[psbank][0:nrows, :],
                                                  self.sinb[0:nrows, sl]),
                     deps=(self.bank[psbank]["w"],) + self.wdeps("scr2"))
        self.bank_read(psbank, e2)
        e3 = self.op("ve", lambda v: v.tensor_add(dst[0:nrows, sl], self.scr1[0:nrows, :],
                                                  self.scr2[0:nrows, :]))
        self.rdep("scr1", e3)
        self.rdep("scr2", e3)
        return e3

    def qkv_group(g, e_ag, wq_tiles, wk_tiles, wv_tiles, scale_q, rope,
                  bq_ap=None, bk_ap=None, bv_row=None, e_wdeps=()):
        e_wq = self.load_w(wq_tiles, 0, NI, deps=e_wdeps)
        e_wk = self.load_w(wk_tiles, 1, NI, deps=e_wdeps)
        e_rope = load_rope() if (rope and not self.rope_loaded) else ()
        if rope:
            self.rope_loaded = True
        evs = []
        for cc in range(NTC_S):
            buf = cc % 2
            e_rhs = self.load_rhs(buf, ag_rhs_ap(cc), deps=(e_ag,))
            itsq = [(self.wsb[:, 0, i, :], self.rhs[:, buf, i, :], (e_rhs, e_wq))
                    for i in range(NI)]
            eq = self.mmg(0, ps[0][:, :], itsq)
            itsk = [(self.wsb[:, 1, i, :], self.rhs[:, buf, i, :], (e_rhs, e_wk))
                    for i in range(NI)]
            ek = self.mmg(1, ps[1][:, :], itsk)
            for bb in (f"rhs{buf}", "wsb0"):
                self.rdep(bb, eq)
            self.rdep(f"rhs{buf}", ek)
            self.rdep("wsb1", ek)
            sl = slice(cc * TC, (cc + 1) * TC)
            kwq = dict(scale=scale_q) if bq_ap is None else dict(
                scale=scale_q, bias=bq_ap)
            eeq = self.op("se", lambda s, _sl=sl, _kw=kwq: s.activation(
                self.qf[:, _sl], ps[0][:, :],
                AF.Identity if "bias" in _kw else AF.Copy, **_kw), deps=(eq,))
            self.bank_read(0, eeq)
            kwk = dict() if bk_ap is None else dict(bias=bk_ap)
            eek = self.op("se", lambda s, _sl=sl, _kw=kwk: s.activation(
                self.kf[:, _sl], ps[1][:, :],
                AF.Identity if "bias" in _kw else AF.Copy, **_kw), deps=(ek,))
            self.bank_read(1, eek)
            if rope:
                war0 = tuple(self.bank[0]["r"])
                self.bank[0]["r"] = []
                mrq = self.mm(ps[0][:, :], self.ropeRt[:], self.qf[:, sl], True, True,
                              deps=war0 + (eeq,) + e_rope)
                self.bank[0]["w"] = mrq
                e3 = rope_inplace(self.qf, cc, 0, (eeq,) + e_rope)
                war1 = tuple(self.bank[1]["r"])
                self.bank[1]["r"] = []
                mrk = self.mm(ps[1][:, :], self.ropeRt[:], self.kf[:, sl], True, True,
                              deps=war1 + (eek,))
                self.bank[1]["w"] = mrk
                e4 = rope_inplace(self.kf, cc, 1, (eek,) + e_rope)
                evs += [e3, e4]
            else:
                evs += [eeq, eek]
        e_wv = self.load_w(wv_tiles, 0, NI, deps=(self.ev("te"),))
        for cc in range(NTC_S):
            buf = cc % 2
            e_rhs = self.load_rhs(buf, ag_rhs_ap(cc), deps=(e_ag,))
            for tt in range(TC // 128):
                its = [(self.rhs[:, buf, i, tt * 128:(tt + 1) * 128], self.wsb[:, 0, i, :],
                        (e_rhs, e_wv)) for i in range(NI)]
                if bv_row is not None:
                    its.append((self.onesb[0:1, 0:128], bv_row, ()))
                ev_ = self.mmg(2, ps[2][0:128, 0:128], its)
                self.rdep(f"rhs{buf}", ev_)
                self.rdep("wsb0", ev_)
                kt = cc * 4 + tt
                evv = self.op("ve", lambda v, _kt=kt: v.tensor_copy(
                    self.vtm[:, _kt, :, 0:64],
                    ps[2][:, 0:128].rearrange("p (h d) -> p h d", h=2)),
                    deps=(ev_, self.e_vones))
                self.bank_read(2, evv)
                evs.append(evv)
        return evs

    def head_bound(hh, qbase, kbase, e_ready):
        rows = self.rows
        for idx, (src, base) in enumerate(((self.qf, qbase), (self.kf, kbase))):
            for cc in range(NTC_S):
                sl = slice(cc * TC, (cc + 1) * TC)
                esq = self.op("se", lambda s, _s=src, _sl=sl, _b=base: s.activation(
                    self.scr1[0:64, :], _s[_b:_b + 64, _sl], AF.Square),
                    deps=tuple(e_ready) + self.wdeps("scr1"))
                em = self.mmg(4, ps[4][0:1, 0:512],
                              [(self.onesb[0:64, 0:1], self.scr1[0:64, :], (esq,))])
                self.rdep("scr1", em)
                ered = self.op("ve", lambda v: v.reduce_max(rows[0:1, 1536:1537],
                                                            ps[4][0:1, :], AX), deps=(em,))
                self.bank_read(4, ered)
                if cc == 0:
                    eacc = self.op("ve", lambda v: v.tensor_copy(
                        rows[0:1, 1537:1538], rows[0:1, 1536:1537]), deps=(ered,))
                else:
                    eacc = self.op("ve", lambda v: v.tensor_max(
                        rows[0:1, 1537:1538], rows[0:1, 1537:1538], rows[0:1, 1536:1537]),
                        deps=(ered,))
            eqs = self.op("se", lambda s: s.activation(rows[0:1, 1538 + idx:1539 + idx],
                                                       rows[0:1, 1537:1538], AF.Sqrt),
                          deps=(eacc,))
        ebn = self.op("ve", lambda v: v.tensor_mul(rows[0:1, 1540:1541],
                                                   rows[0:1, 1538:1539],
                                                   rows[0:1, 1539:1540]), deps=(eqs,))
        ebn2 = self.op("ve", lambda v: v.tensor_scalar(rows[0:1, 1540:1541],
                                                       rows[0:1, 1540:1541], -1.0, None,
                                                       ALU.mult), deps=(ebn,))
        return self.bscal(2 + hh, rows[0:1, 1540:1541], deps=(ebn2,))

    def attn_causal_group(g, kvsel, e_ready):
        for hh in range(2):
            kbase, _ = kvsel(hh)
            head_bound(hh, 64 * hh, kbase, e_ready)
        e_out = []
        for qc in range(NTC_S):
            nkt = 4 * qc + 4
            pv_started = [False, False]
            for kt in range(nkt):
                for hh in range(2):
                    slot = (kt % 2) * 2 + hh
                    qb = 64 * hh
                    kbase, vix = kvsel(hh)
                    bqk = hh
                    war = tuple(self.bank[bqk]["r"])
                    self.bank[bqk]["r"] = []
                    eqk = self.mm(ps[bqk][:, :],
                                  self.kf[kbase:kbase + 64, kt * 128:(kt + 1) * 128],
                                  self.qf[qb:qb + 64, qc * TC:(qc + 1) * TC],
                                  True, True, deps=war + tuple(e_ready))
                    self.bank[bqk]["w"] = eqk
                    if kt >= 4 * qc:
                        emk = self.op("ve", lambda v, _b=bqk, _r=kt - 4 * qc: v.tensor_add(
                            ps[_b][:, :], ps[_b][:, :], self.cmaskb[:, _r, :]), deps=(eqk,))
                        self.bank[bqk]["w"] = emk
                        eqk = emk
                    sname = f"expS{slot}"
                    eexp = self.op("se", lambda s, _b=bqk, _sl=slot, _hh=hh: s.activation(
                        self.expS[:, _sl, :],
                        ps[_b][:, :], AF.Exp, bias=self.scl[:, 2 + _hh:3 + _hh]),
                        deps=(eqk,) + self.wdeps(sname))
                    self.bank_read(bqk, eexp)
                    bb = 2 + hh
                    if not pv_started[hh]:
                        war2 = tuple(self.bank[bb]["r"])
                        self.bank[bb]["r"] = []
                    else:
                        war2 = ()
                    epv = self.mm(ps[bb][0:65, :], self.vtm[:, kt, vix, 0:65],
                                  self.expS[:, slot, :], not pv_started[hh],
                                  kt == nkt - 1, deps=war2 + (eexp,))
                    pv_started[hh] = True
                    self.rdep(sname, epv)
                    self.bank[bb]["w"] = epv
            for hh in range(2):
                bb = 2 + hh
                qb = 64 * hh
                erz = self.op("ve", lambda v, _b=bb: v.reciprocal(
                    self.rows[0:1, 1024:1536], ps[_b][64:65, :]),
                    deps=(self.bank[bb]["w"],))
                self.bank_read(bb, erz)
                ebz = self.bcast(4, self.rows[0:1, 1024:1536], 512, nparts=64, deps=(erz,))
                ecp = self.op("se", lambda s, _b=bb: s.activation(
                    self.oevac[0:64, :], ps[_b][0:64, :], AF.Copy),
                    deps=(self.bank[bb]["w"],) + self.wdeps("oevac"))
                self.bank_read(bb, ecp)
                emu = self.op("ve", lambda v, _qb=qb, _qc=qc: v.tensor_mul(
                    self.outTg[_qb:_qb + 64, _qc * TC:(_qc + 1) * TC],
                    self.oevac[0:64, :], ps[4][0:64, :]), deps=(ebz, ecp))
                self.bank_read(4, emu)
                self.rdep("oevac", emu)
                e_out.append(emu)
        return e_out

    def stage_outTg_to_hx(g, e_out):
        return self.dma("sy", self.hx_in.ap().rearrange("(g p) t -> g p t", p=128)[g],
                        self.outTg[:], deps=tuple(e_out))

    def hx_rhs_ap_factory(tck):
        def make(halfrow):
            def f():
                t = self.hx_out.ap()
                off = halfrow * 512 * S + self.h2048 + tck * TC
                return bass.AP(t.tensor, off, [[S, 128], [128 * S, 4], [1, TC]])
            return f
        return make

    def ensure_h(q):
        if self.ENG(q) and self.h2048 is None:
            pid = self.nc.partition_id(engines=[self.H[q].engine])
            self.h2048 = (pid % 2) * 2048

    def out_proj_lnout(wo_tiles_ap, bo_dram, e_hx, extra_bias_slot=None):
        if bo_dram is not None:
            e_bo = self.dma("gp", self.bias_sb[:, 0:NI], bo_dram.ap(),
                            deps=self.wdeps("bias0"))
        for tck in range(NTC_T):
            mk = hx_rhs_ap_factory(tck)
            buf = tck % 2
            d = self.wdeps(f"rhs{buf}")
            e_r0 = self.dma("sy", self.rhs[:, buf, 0:4, :], mk(0), deps=(e_hx,) + d)
            e_r1 = self.dma("sy", self.rhs[:, buf, 4:8, :], mk(1), deps=(e_hx,))
            y_evs = []
            for o in range(NI):
                e_w = self.load_w(wo_tiles_ap[o], o % 2, NI, deps=(self.ev("te"),))
                bnk = o % 2
                its = [(self.wsb[:, o % 2, i, :], self.rhs[:, buf, i, :],
                        (e_r0, e_r1, e_w)) for i in range(NI)]
                eo = self.mmg(bnk, ps[bnk][:, :], its)
                self.rdep(f"rhs{buf}", eo)
                self.rdep(f"wsb{o % 2}", eo)
                if bo_dram is not None:
                    kw = dict(bias=self.bias_sb[:, o:o + 1])
                    dp = (eo, e_bo)
                else:
                    kw = {}
                    dp = (eo,)
                dpc = dp + (self.wdeps("cast_c") if o == 0 else ())
                ec = self.op("se", lambda s, _o=o, _kw=kw: s.activation(
                    self.cast_c[:, _o, :], ps[_o % 2][:, :],
                    AF.Identity if "bias" in _kw else AF.Copy, **_kw),
                    deps=dpc)
                es2 = self.op("se", lambda s, _o=o, _kw=kw: s.activation(
                    self.sq_c[:, _o, :], ps[_o % 2][:, :], AF.Square, **_kw),
                    deps=(eo,) + (self.wdeps("sq_c") if o == 0 else ()))
                self.bank_read(bnk, ec)
                self.bank_read(bnk, es2)
                y_evs += [ec, es2]
            self.rdep("bias0", y_evs[-1])
            self.lnout_residual(tck, y_evs)

    def layer0(e_x, e_const):
        e_ag = stage_ln_to_ag((e_x, e_const))
        self.e_vones = self.op("ve", lambda v: v.memset(self.vtm[:, :, :, 64:66], 0.0))
        self.e_vones = self.op("ve", lambda v: v.memset(self.vtm[:, :, :, 64:65], 1.0))
        hx_stage = []
        for g in range(4):
            evs = qkv_group(g, e_ag, din["l0_wq"].ap()[g], din["l0_wk"].ap()[g],
                            din["l0_wv"].ap()[g], scale_q=0.125, rope=True)
            e_out = attn_causal_group(g, lambda hh: (64 * hh, hh), evs)
            hx_stage.append(stage_outTg_to_hx(g, e_out))
        e_hx = self.coll("AllGather", ALU.bypass, self.PAIRS, [self.hx_in.ap()],
                         [self.hx_out.ap()], deps=tuple(hx_stage))
        out_proj_lnout([din["l0_wo"].ap()[o] for o in range(NI)], None, e_hx)

    def ffn0():
        fe = self.fence("te", "ve", "se")
        for wname, dram_src, dq, NW in (("fc", din["l0_wfc_t"], self.wqd["fc"], 64),
                                        ("pr", din["l0_wpr_t"], self.wqd["pr"], 64)):
            KD = 1024 if wname == "fc" else 4096
            MD = 4096 if wname == "fc" else 1024
            src = dram_src.ap().rearrange("(i p) m -> p i m", p=128)
            for w in range(NW):
                it = (w * 512) // MD
                mo = (w * 512) % MD
                e_l = self.dma("gp", self.oevac[:], src[:, it, mo:mo + 512],
                               deps=self.wdeps("oevac"))
                e_r = self.op("ve", lambda v: v.reduce_max(
                    self.colA[:, 0:1], self.oevac[:], AX, apply_absolute_value=True),
                    deps=(e_l,))
                self.rdep("oevac", e_r)
                if w == 0:
                    e_acc = self.op("ve", lambda v: v.tensor_copy(self.colA[:, 1:2],
                                                                  self.colA[:, 0:1]),
                                    deps=(e_r,))
                else:
                    e_acc = self.op("ve", lambda v: v.tensor_max(
                        self.colA[:, 1:2], self.colA[:, 1:2], self.colA[:, 0:1]),
                        deps=(e_r,))
            e_c2r = self.dma("gp", self.rows[0:1, 1536 + 0:1536 + 128],
                             self.colA[:, 1:2], deps=(e_acc,))
            e_gm = self.op("ve", lambda v: v.reduce_max(self.rows[0:1, 1664:1665],
                                                        self.rows[0:1, 1536:1664], AX),
                           deps=(e_c2r,))
            e_s = self.op("ve", lambda v: v.tensor_scalar(self.rows[0:1, 1665:1666],
                                                          self.rows[0:1, 1664:1665],
                                                          1.0 / 127.0, None, ALU.mult),
                          deps=(e_gm,))
            e_rcp = self.op("ve", lambda v: v.reciprocal(self.rows[0:1, 1666:1667],
                                                         self.rows[0:1, 1665:1666]),
                            deps=(e_s,))
            e_ss = self.bscal(4, self.rows[0:1, 1665:1666], deps=(e_s,))
            e_rr = self.bscal(5, self.rows[0:1, 1666:1667], deps=(e_rcp,))
            NO = MD // 128
            NIW = KD // 128
            for o in range(NO):
                for wv in range(NIW // 4):
                    e_l = self.dma("gp", self.oevac[:].rearrange("p (i m) -> p i m", i=4),
                                   src[:, 4 * wv:4 * wv + 4, o * 128:(o + 1) * 128],
                                   deps=self.wdeps("oevac"))
                    e_q1 = self.op("ve", lambda v: v.tensor_scalar(
                        self.qscr[:], self.oevac[:], self.scl[:, 5:6], R2,
                        ALU.mult, ALU.add), deps=(e_l, e_rr) + self.wdeps("qscr"))
                    self.rdep("oevac", e_q1)
                    e_q2 = self.op("ve", lambda v: v.tensor_scalar(
                        self.scr1[:], self.qscr[:], R2, self.scl[:, 4:5],
                        ALU.subtract, ALU.mult), deps=(e_ss,) + self.wdeps("scr1"))
                    self.rdep("qscr", e_q2)
                    e_st = self.dma("gp", dq.ap()[o, 4 * wv:4 * wv + 4]
                                    .rearrange("i p m -> p i m"),
                                    self.scr1[:].rearrange("p (i m) -> p i m", i=4),
                                    deps=(e_q2,))
                    self.rdep("scr1", e_st)
            if wname == "fc":
                e_wqfc_done = self.ev("gp")
            else:
                e_wqpr_done = self.ev("gp")
        e_afc = self.dma("gp", self.afc[:], din["l0_afc_t"].ap()
                         .rearrange("(i p) r -> p i r", p=128), deps=fe)
        e_bfl = self.dma("gp", self.bfc_l[:], din["l0_bfc_l"].ap(), deps=fe)
        e_apr = self.dma("gp", self.apr[:], din["l0_apr_t"].ap()
                         .rearrange("(i p) r -> p i r", p=128))
        e_bpl = self.dma("gp", self.bpr_l[:], din["l0_bpr_l"].ap())
        e_bfc = self.dma("gp", self.bias_sb[:, 0:32], din["l0_bfc"].ap())
        e_bpr = self.dma("gp", self.bias_sb[:, 32:40], din["l0_bpr"].ap())

        for tck in range(NTC_T):
            src = self.x[:, :, tck * TC:(tck + 1) * TC]
            e_ln = self.ln_chunk(src, self.ln_c[:], self.wdeps("ln_c"))
            its = [(self.afc[:, i, :], self.ln_c[:, i, :], (e_ln, e_afc))
                   for i in range(NI)]
            em = self.mmg(3, ps[3][0:16, :], its)
            self.rdep("ln_c", em)
            e_mid = self.op("se", lambda s: s.activation(
                self.rowbf[0:16, 0:512], ps[3][0:16, :], AF.Copy, scale=2.0),
                deps=(em,) + self.wdeps("mid1"))
            self.bank_read(3, e_mid)
            h_evs = []
            for o in range(32):
                buf = o % 2
                e_w = self.load_w(self.wqd["fc"].ap()[o], buf, NI,
                                  deps=(e_wqfc_done,))
                its = [(self.wsb[:, buf, i, :], self.ln_c[:, i, :], (e_ln, e_w))
                       for i in range(NI)]
                its.append((self.bfc_l[:, o * 128:(o + 1) * 128],
                            self.rowbf[0:16, 0:512], (e_mid, e_bfl)))
                bnk = o % 2
                eo = self.mmg(bnk, ps[bnk][:, :], its)
                self.rdep("ln_c", eo)
                self.rdep(f"wsb{buf}", eo)
                self.rdep("mid1", eo)
                eg = self.op("se", lambda s, _o=o: s.activation(
                    self.hid[:, _o, :], ps[_o % 2][:, :], AF.Gelu,
                    bias=self.bias_sb[:, _o:_o + 1]), deps=(eo, e_bfc) + fe)
                self.bank_read(bnk, eg)
                h_evs.append(eg)
            its = [(self.apr[:, f, :], self.hid[:, f, :], (h_evs[f], e_apr))
                   for f in range(32)]
            em2 = self.mmg(3, ps[3][0:16, :], its)
            e_mid2 = self.op("se", lambda s: s.activation(
                self.rowbf[0:16, 512:1024], ps[3][0:16, :], AF.Copy, scale=2.0),
                deps=(em2,) + self.wdeps("mid2"))
            self.bank_read(3, e_mid2)
            y_evs = []
            for o in range(NI):
                bnk = o % 2
                eo = None
                war = tuple(self.bank[bnk]["r"])
                self.bank[bnk]["r"] = []
                for wave in range(2):
                    e_w = self.load_w(self.wqd["pr"].ap()[o, 16 * wave:16 * wave + 16],
                                      bnk, 16, deps=(e_wqpr_done,))
                    for i in range(16):
                        f = 16 * wave + i
                        eo = self.mm(ps[bnk][:, :], self.wsb[:, bnk, i, :],
                                     self.hid[:, f, :], (wave == 0 and i == 0), False,
                                     deps=(war + (e_w, h_evs[f])) if (wave == 0 and i == 0)
                                     else (e_w, h_evs[f]))
                    self.rdep(f"wsb{bnk}", eo)
                eo = self.mm(ps[bnk][:, :], self.bpr_l[:, o * 128:(o + 1) * 128],
                             self.rowbf[0:16, 512:1024], False, True,
                             deps=(e_mid2, e_bpl))
                self.bank[bnk]["w"] = eo
                self.rdep("mid2", eo)
                ec = self.op("se", lambda s, _o=o: s.activation(
                    self.cast_c[:, _o, :], ps[_o % 2][:, :], AF.Identity,
                    bias=self.bias_sb[:, 32 + _o:33 + _o]),
                    deps=(eo, e_bpr) + (self.wdeps("cast_c") if o == 0 else ()))
                es2 = self.op("se", lambda s, _o=o: s.activation(
                    self.sq_c[:, _o, :], ps[_o % 2][:, :], AF.Square,
                    bias=self.bias_sb[:, 32 + _o:33 + _o]),
                    deps=(eo,) + (self.wdeps("sq_c") if o == 0 else ()))
                self.bank_read(bnk, ec)
                self.bank_read(bnk, es2)
                y_evs += [ec, es2]
            self.lnout_residual(tck, y_evs)


    def layer1():
        fe = self.fence("te", "ve", "se")
        e_ag = stage_ln_to_ag(fe)
        e_gm = self.dma("gp", self.gmaskb[:], dco["l1_gmask"].ap(), deps=fe)
        e_bq = self.dma("gp", self.bias_sb[:, 0:4], din["l1_bq"].ap(), deps=self.wdeps("bias0"))
        e_bk = self.dma("gp", self.bias_sb[:, 4:8], din["l1_bk"].ap())
        e_bqs = self.op("ve", lambda v: v.tensor_scalar(
            self.bias_sb[:, 8:12], self.bias_sb[:, 0:4], 0.125, None, ALU.mult),
            deps=(e_bq,))
        e_bv = self.dma("gp", self.l1bv[:], din["l1_bv_row"].ap(), deps=fe)
        self.e_vones = self.op("ve", lambda v: v.memset(self.vtm[:, :, :, 64:66], 0.0),
                               deps=fe)
        self.e_vones = self.op("ve", lambda v: v.memset(self.vtm[:, :, :, 64:65], 1.0))
        e_vg1 = self.op("ve", lambda v: v.memset(self.vglob[:, 64:66], 0.0), deps=fe)
        e_vg1 = self.op("ve", lambda v: v.memset(self.vglob[:, 64:65], 1.0))
        hx_stage = []
        for g in range(4):
            evs = qkv_group(g, e_ag, din["l1_wq"].ap()[g], din["l1_wk"].ap()[g],
                            din["l1_wv"].ap()[g], scale_q=0.125, rope=False,
                            bq_ap=self.bias_sb[:, 8 + g:9 + g],
                            bk_ap=self.bias_sb[:, 4 + g:5 + g],
                            bv_row=self.l1bv[0:1, 128 * g:128 * g + 128],
                            e_wdeps=(e_bqs, e_bk, e_bv))
            e_out = attn_sparse_group(g, evs, (e_gm, e_vg1))
            hx_stage.append(stage_outTg_to_hx(g, e_out))
        e_hx = self.coll("AllGather", ALU.bypass, self.PAIRS, [self.hx_in.ap()],
                         [self.hx_out.ap()], deps=tuple(hx_stage))
        out_proj_lnout([din["l1_wo"].ap()[o] for o in range(NI)], din["l1_bo"], e_hx)

    def attn_sparse_group(g, e_ready, e_masks):
        e_vg1 = e_masks[-1]
        for hh in range(2):
            head_bound(hh, 64 * hh, 64 * hh, e_ready)
        egk = self.op("ve", lambda v: v.tensor_copy(
            self.kglob[:], bass.AP(self.kf.ap().tensor, GP0,
                                   [[S, 128], [GPSTEP, 8]])), deps=tuple(e_ready))
        evg = []
        for gi in range(8):
            t = int(L1_GP[gi])
            tile, part = t // 128, t % 128
            for hh in range(2):
                evg.append(self.dma("gp", self.vglob[32 * hh + gi:32 * hh + gi + 1, 0:64],
                                    self.vtm[part:part + 1, tile, hh, 0:64],
                                    deps=tuple(e_ready)))
        for hh in range(2):
            for cc in range(NTC_S):
                sl = slice(cc * TC, (cc + 1) * TC)
                war = tuple(self.bank[4]["r"]); self.bank[4]["r"] = []
                eqkg = self.mm(ps[4][0:8, :], self.kglob[64 * hh:64 * hh + 64, :],
                               self.qf[64 * hh:64 * hh + 64, sl], True, True,
                               deps=war + (egk,) + tuple(e_ready))
                self.bank[4]["w"] = eqkg
                emk = self.op("ve", lambda v, _sl=sl: v.tensor_add(
                    ps[4][0:8, :], ps[4][0:8, :], self.gmaskb[:, _sl]),
                    deps=(eqkg,) + e_masks)
                self.bank[4]["w"] = emk
                eex = self.op("se", lambda s, _hh=hh, _sl=sl: s.activation(
                    self.rowbf[32 * _hh:32 * _hh + 8, _sl], ps[4][0:8, :], AF.Exp,
                    bias=self.scl[0:8, 2 + _hh:3 + _hh]), deps=(emk,) + self.wdeps(f"expG{hh}"))
                self.bank_read(4, eex)
                self.rdep(f"expG{hh}", eex)
        e_out = []
        for qc in range(NTC_S):
            pv_started = [False, False]
            for sub in range(4):
                qt = 4 * qc + sub
                for hh in range(2):
                    qb = 64 * hh
                    bqk = hh
                    war = tuple(self.bank[bqk]["r"]); self.bank[bqk]["r"] = []
                    eqk = self.mm(ps[bqk][:, 0:128],
                                  self.kf[qb:qb + 64, qt * 128:(qt + 1) * 128],
                                  self.qf[qb:qb + 64, qt * 128:(qt + 1) * 128],
                                  True, True, deps=war + tuple(e_ready))
                    emk = self.op("ve", lambda v, _b=bqk: v.tensor_add(
                        ps[_b][:, 0:128], ps[_b][:, 0:128], self.dmaskb[:]),
                        deps=(eqk,))
                    self.bank[bqk]["w"] = emk
                    slot = (sub % 2) * 2 + hh
                    sname = f"expS{slot}"
                    eexp = self.op("se", lambda s, _b=bqk, _sl=slot, _hh=hh: s.activation(
                        self.expS[:, _sl, 0:128], ps[_b][:, 0:128], AF.Exp,
                        bias=self.scl[:, 2 + _hh:3 + _hh]),
                        deps=(emk,) + self.wdeps(sname))
                    self.bank_read(bqk, eexp)
                    bb = 2 + hh
                    if not pv_started[hh]:
                        war2 = tuple(self.bank[bb]["r"]); self.bank[bb]["r"] = []
                        pv_started[hh] = True
                    else:
                        war2 = ()
                    ed = self.mm(ps[bb][0:65, sub * 128:(sub + 1) * 128],
                                 self.vtm[:, qt, hh, 0:65], self.expS[:, slot, 0:128],
                                 True, False, deps=war2 + (eexp,))
                    self.rdep(sname, ed)
                    eg2 = self.mm(ps[bb][0:65, sub * 128:(sub + 1) * 128],
                                  self.vglob[32 * hh:32 * hh + 8, 0:65],
                                  self.rowbf[32 * hh:32 * hh + 8,
                                             qt * 128:(qt + 1) * 128],
                                  False, True, deps=tuple(evg) + (e_vg1,))
                    self.rdep(f"expG{hh}", eg2)
                    self.bank[bb]["w"] = eg2
            for hh in range(2):
                bb = 2 + hh
                qb = 64 * hh
                erz = self.op("ve", lambda v, _b=bb: v.reciprocal(
                    self.rows[0:1, 1024:1536], ps[_b][64:65, :]),
                    deps=(self.bank[bb]["w"],))
                self.bank_read(bb, erz)
                ebz = self.bcast(4, self.rows[0:1, 1024:1536], 512, nparts=64,
                                 deps=(erz,))
                ecp = self.op("se", lambda s, _b=bb: s.activation(
                    self.oevac[0:64, :], ps[_b][0:64, :], AF.Copy),
                    deps=(self.bank[bb]["w"],) + self.wdeps("oevac"))
                self.bank_read(bb, ecp)
                emu = self.op("ve", lambda v, _qb=qb, _qc=qc: v.tensor_mul(
                    self.outTg[_qb:_qb + 64, _qc * TC:(_qc + 1) * TC],
                    self.oevac[0:64, :], ps[4][0:64, :]), deps=(ebz, ecp))
                self.bank_read(4, emu)
                self.rdep("oevac", emu)
                e_out.append(emu)
        for hh in range(2):
            qb = 64 * hh
            eqg = self.op("ve", lambda v, _qb=qb: v.tensor_copy(
                self.qglob[_qb:_qb + 64, :],
                bass.AP(self.qf.ap().tensor, _qb * S + GP0,
                        [[S, 64], [GPSTEP, 8]])), deps=tuple(e_out))
            for kt in range(32):
                bqk = kt % 2
                war = tuple(self.bank[bqk]["r"]); self.bank[bqk]["r"] = []
                eqk = self.mm(ps[bqk][:, 0:8],
                              self.kf[qb:qb + 64, kt * 128:(kt + 1) * 128],
                              self.qglob[qb:qb + 64, :], True, True,
                              deps=war + (eqg,))
                self.bank[bqk]["w"] = eqk
                eex = self.op("se", lambda s, _b=bqk, _kt=kt: s.activation(
                    self.scr1[:, 8 * _kt:8 * _kt + 8], ps[_b][:, 0:8], AF.Exp,
                    bias=self.scl[:, 2 + hh:3 + hh]),
                    deps=(eqk,) + (self.wdeps("scr1") if kt == 0 else ()))
                self.bank_read(bqk, eex)
                if kt == 0:
                    war2 = tuple(self.bank[5]["r"]); self.bank[5]["r"] = []
                else:
                    war2 = ()
                epv = self.mm(ps[5][0:65, 0:8], self.vtm[:, kt, hh, 0:65],
                              self.scr1[:, 8 * kt:8 * kt + 8], kt == 0, kt == 31,
                              deps=war2 + (eex,))
                self.rdep("scr1", epv)
                self.bank[5]["w"] = epv
            erz = self.op("ve", lambda v: v.reciprocal(self.rows[0:1, 1024:1032],
                                                       ps[5][64:65, 0:8]),
                          deps=(self.bank[5]["w"],))
            self.bank_read(5, erz)
            ebz = self.bcast(4, self.rows[0:1, 1024:1032], 8, nparts=64, deps=(erz,))
            ecp = self.op("se", lambda s: s.activation(
                self.oevac[0:64, 0:8], ps[5][0:64, 0:8], AF.Copy),
                deps=(self.bank[5]["w"],) + self.wdeps("oevac"))
            self.bank_read(5, ecp)
            emu = self.op("ve", lambda v, _qb=qb: v.tensor_mul(
                bass.AP(self.outTg.ap().tensor, _qb * S + GP0, [[S, 64], [GPSTEP, 8]]),
                self.oevac[0:64, 0:8], ps[4][0:64, 0:8]), deps=(ebz, ecp))
            self.bank_read(4, emu)
            self.rdep("oevac", emu)
            e_out.append(emu)
        return e_out

    def ffn1():
        fe = self.fence("te", "ve", "se")
        e_gw = self.dma("gp", self.gwsb[:], din["l1_gw"].ap()
                        .rearrange("(i p) e -> p i e", p=128), deps=fe)
        e_gb = self.dma("gp", self.gbrow[:], din["l1_gb"].ap(), deps=fe)
        e_eb = self.dma("gp", self.ebs[:], din["l1_eb"].ap(), deps=fe)
        e_i8 = self.dma("gp", self.ind8[:], dco["ind8"].ap(), deps=fe)
        for tck in range(NTC_T):
            src = self.x[:, :, tck * TC:(tck + 1) * TC]
            e_ln = self.ln_chunk(src, self.ln_c[:], self.wdeps("ln_c") + fe)
            ew8 = []
            for tt in range(4):
                cb = tt * 8
                its = [(self.ln_c[:, i, tt * 128:(tt + 1) * 128], self.gwsb[:, i, :],
                        (e_ln, e_gw)) for i in range(NI)]
                its.append((self.onesb[0:1, 0:128], self.gbrow[:], (e_gb,)))
                eg = self.mmg(3, ps[3][0:128, 0:8], its)
                self.rdep("ln_c", eg)
                em = self.op("ve", lambda v, _c=cb: v.reduce_max(
                    self.colA[:, _c:_c + 1], ps[3][0:128, 0:8], AX), deps=(eg,))
                enm = self.op("ve", lambda v, _c=cb: v.tensor_scalar(
                    self.colA[:, _c + 1:_c + 2], self.colA[:, _c:_c + 1], -1.0, None,
                    ALU.mult), deps=(em,))
                eex = self.op("se", lambda s, _c=cb: s.activation(
                    self.gsm[:], ps[3][0:128, 0:8], AF.Exp,
                    bias=self.colA[:, _c + 1:_c + 2]),
                    deps=(enm, eg) + self.wdeps("gsm"))
                self.bank_read(3, eex)
                if tck == 0 and tt == 0:
                    _et = self.tap("l1gsm", self.gsm[:], deps=(eex,))
                    if _et:
                        self.rdep("gsm", _et)
                esum = self.op("ve", lambda v, _c=cb: v.reduce_sum(
                    self.colA[:, _c + 2:_c + 3], self.gsm[:], AX), deps=(eex,))
                ethr = self.op("ve", lambda v, _c=cb: v.tensor_scalar(
                    self.colA[:, _c + 3:_c + 4], self.colA[:, _c + 2:_c + 3], 0.1, None,
                    ALU.mult), deps=(esum,))
                if tck == 0 and tt == 0:
                    _et = self.tap("l1colA", self.colA[:], deps=(ethr,))
                emask = self.op("ve", lambda v, _c=cb: v.tensor_scalar(
                    self.gm2[:], self.gsm[:], self.colA[:, _c + 3:_c + 4], None,
                    ALU.is_ge), deps=(ethr,) + self.wdeps("gm2"))
                if tck == 0 and tt == 0:
                    _et = self.tap("l1sel", self.gm2[:], deps=(emask,))
                    if _et:
                        self.rdep("gm2", _et)
                ew = self.op("ve", lambda v: v.tensor_tensor(
                    self.wnf[:], self.gsm[:], self.gm2[:], ALU.mult),
                    deps=(emask,) + self.wdeps("wnf"))
                self.rdep("gsm", ew)
                if tck == 0 and tt == 0:
                    _et = self.tap("l1wnf", self.wnf[:], deps=(ew,))
                    if _et:
                        self.rdep("wnf", _et)
                ews = self.op("ve", lambda v, _c=cb: v.reduce_sum(
                    self.colA[:, _c + 4:_c + 5], self.wnf[:], AX), deps=(ew,))
                ercp2 = self.op("ve", lambda v, _c=cb: v.reciprocal(
                    self.colA[:, _c + 5:_c + 6], self.colA[:, _c + 4:_c + 5]),
                    deps=(ews,))
                ewn = self.op("ve", lambda v, _c=cb: v.tensor_scalar(
                    self.gm2[:], self.wnf[:], self.colA[:, _c + 5:_c + 6], None,
                    ALU.mult), deps=(ercp2,))
                self.rdep("wnf", ewn)
                if tck == 0 and tt == 0:
                    _et = self.tap("l1fin", self.gm2[:], deps=(ewn,))
                    if _et:
                        self.rdep("gm2", _et)
                ecst = self.op("ve", lambda v: v.tensor_copy(self.wnbf[:], self.gm2[:]),
                               deps=(ewn,) + self.wdeps("wnbf"))
                self.rdep("gm2", ecst)
                etp = self.transpose(self.pbf[0:8, 0:128], self.wnbf[:],
                                     deps=(ecst,) + self.wdeps("pbf"))
                self.rdep("wnbf", etp)
                ev8 = self.op("ve", lambda v, _tt=tt: v.tensor_copy(
                    self.w8[0:8, _tt * 128:(_tt + 1) * 128], self.pbf[0:8, 0:128]),
                    deps=(etp,) + (self.wdeps("w8") if tt == 0 else ()))
                self.rdep("pbf", ev8)
                ew8.append(ev8)
            if tck == 0:
                self.tap("l1ln", self.ln_c[:].rearrange("p i t -> p (i t)"),
                         deps=(e_ln,))
                self.tap("l1w8", self.w8[:], deps=tuple(ew8))
            for eg_ in range(2):
                e_rhs = []
                for ei in range(4):
                    e = 4 * eg_ + ei
                    war = tuple(self.bank[3]["r"]); self.bank[3]["r"] = []
                    ebc = self.mm(ps[3][:, :], self.ind8[0:8, e * 128:(e + 1) * 128],
                                  self.w8[0:8, :], True, True,
                                  deps=war + tuple(ew8) + (e_i8,))
                    self.bank[3]["w"] = ebc
                    er = self.op("ve", lambda v, _ei=ei: v.tensor_mul(
                        self.hid[:, 8 * _ei:8 * _ei + 8, :], self.ln_c[:],
                        ps[3][:, :].rearrange("p (u n) -> p u n", u=1)
                        .broadcast_to([128, NI, TC])),
                        deps=(ebc, e_ln) + self.wdeps(f"hid{ei}"))
                    self.bank_read(3, er)
                    self.rdep("ln_c", er)
                    e_rhs.append(er)
                for ot in range(NI):
                    bnk = ot % 2
                    war = tuple(self.bank[bnk]["r"]); self.bank[bnk]["r"] = []
                    first = True
                    em = None
                    for ei in range(4):
                        e = 4 * eg_ + ei
                        e_w = self.load_w(din["l1_ew"].ap()[e, ot], bnk, NI)
                        for i in range(NI):
                            last = (eg_ == 1 and ei == 3 and i == NI - 1)
                            em = self.mm(ps[bnk][:, :], self.wsb[:, bnk, i, :],
                                         self.hid[:, 8 * ei + i, :],
                                         first, last,
                                         deps=(war + (e_w, e_rhs[ei])) if first
                                         else (e_w, e_rhs[ei]))
                            first = False
                        self.rdep(f"wsb{bnk}", em)
                        self.rdep(f"hid{ei}", em)
                    if eg_ == 0:
                        em = self.mm(ps[bnk][:, :], self.ebs[0:8, ot * 128:(ot + 1) * 128],
                                     self.w8[0:8, :], False, True,
                                     deps=(e_eb,) + tuple(ew8))
                        self.rdep("w8", em)
                    self.bank[bnk]["w"] = em
                    if eg_ == 0:
                        eo = self.op("ve", lambda v, _ot=ot, _b=bnk: v.tensor_copy(
                            self.oc32[:, _ot, :], ps[_b][:, :]), deps=(em,))
                    else:
                        eo = self.op("ve", lambda v, _ot=ot, _b=bnk: v.tensor_add(
                            self.oc32[:, _ot, :], self.oc32[:, _ot, :], ps[_b][:, :]),
                            deps=(em,))
                    self.bank_read(bnk, eo)
                    self.rdep("oc32", eo)
            if tck == 0:
                self.tap("l1y", self.oc32[:].rearrange("p i t -> p (i t)"),
                         deps=tuple(self.bufdep.get("oc32", ())))
            ecc = self.op("ve", lambda v: v.tensor_copy(self.cast_c[:], self.oc32[:]),
                          deps=self.wdeps("oc32x") + self.wdeps("cast_c") +
                          tuple(self.bufdep.get("oc32", ())))
            esq = self.op("se", lambda s: s.activation(self.sq_c[:], self.oc32[:],
                                                       AF.Square),
                          deps=self.wdeps("sq_c") + tuple(self.bufdep.get("oc32", ())))
            self.bufdep["oc32"] = []
            e1, e2 = self._ln_stats((ecc, esq))
            self._ln_tail(e1, e2, sub_src=self.oc32[:], resid_tck=tck)

    def layer2():
        fe = self.fence("te", "ve", "se")
        e_ag = stage_ln_to_ag(fe)
        e_wf = self.dma("gp", self.l2wf2[:], din["l2_wf2"].ap(), deps=fe)
        e_bc = self.dma("gp", self.l2bfc[:], din["l2_bf2c"].ap(), deps=fe)
        e_br = self.dma("gp", self.l2bfr[:], din["l2_bf2r"].ap(), deps=fe)
        self.e_vones = self.op("ve", lambda v: v.memset(self.vtm[:, :, :, 64:66], 0.0),
                               deps=fe)
        self.e_vones = self.op("ve", lambda v: v.memset(self.vtm[:, :, :, 64:65], 1.0))
        cut = L2CUT
        hx_stage = []
        for g in range(4):
            if not (cut & 1):
                break
            evs = qkv_group(g, e_ag, din["l2_wqkv"].ap()[g], din["l2_wqkv"].ap()[4 + g],
                            din["l2_wqkv"].ap()[8 + g], scale_q=1.0, rope=False,
                            e_wdeps=(e_wf, e_bc, e_br))
            ev_q = []
            for cc in range(NTC_S if (cut & 2) else 0):
                sl = slice(cc * TC, (cc + 1) * TC)
                war = tuple(self.bank[0]["r"]); self.bank[0]["r"] = []
                eq = self.mm(ps[0][:, :], self.l2wf2[:], self.qf[:, sl], True, True,
                             deps=war + tuple(evs) + (e_wf,))
                self.bank[0]["w"] = eq
                er = self.op("se", lambda s, _sl=sl: s.activation(
                    self.qf[:, _sl], ps[0][:, :], AF.Relu, bias=self.l2bfc[:, 0:1]),
                    deps=(eq, e_bc))
                self.bank_read(0, er)
                ev_q.append(er)
            ekv_done = [None, None]
            for kt in range(32 if (cut & 4) else 0):
                war = tuple(self.bank[1]["r"]); self.bank[1]["r"] = []
                ek = self.mm(ps[1][0:128, 0:128], self.kf[:, kt * 128:(kt + 1) * 128],
                             self.l2wf2[:], True, False,
                             deps=war + tuple(evs) + (e_wf,))
                ek = self.mm(ps[1][0:128, 0:128], self.onesb[0:1, 0:128], self.l2bfr[:],
                             False, True, deps=(e_br,))
                self.bank[1]["w"] = ek
                er = self.op("se", lambda s: s.activation(
                    self.scr1[:, 0:128], ps[1][0:128, 0:128], AF.Relu),
                    deps=(ek,) + self.wdeps("scr1"))
                self.bank_read(1, er)
                for hh in range(2):
                    bb = 2 + hh
                    if kt == 0:
                        war2 = tuple(self.bank[bb]["r"]); self.bank[bb]["r"] = []
                    else:
                        war2 = ()
                    ekv = self.mm(ps[bb][0:64, 0:65], self.scr1[:, 64 * hh:64 * hh + 64],
                                  self.vtm[:, kt, hh, 0:65], kt == 0, kt == 31,
                                  deps=war2 + (er, self.e_vones) + tuple(evs))
                    self.bank[bb]["w"] = ekv
                    self.rdep("scr1", ekv)
                    ekv_done[hh] = ekv
            e_kc = []
            for hh in range(2 if (cut & 8) else 0):
                bb = 2 + hh
                pb = 64 * hh
                ekc = self.op("se", lambda s, _hh=hh, _bb=bb, _pb=pb: s.activation(
                    self.expS[_pb:_pb + 64, _hh, 0:65], ps[_bb][0:64, 0:65], AF.Copy),
                    deps=(ekv_done[hh],) + self.wdeps(f"kvsb{hh}"))
                self.bank_read(bb, ekc)
                e_kc.append(ekc)
            e_out = []
            for qc in range(NTC_S if (cut & 8) else 0):
                for hh in range(2):
                    bb = hh
                    pb = 64 * hh
                    war = tuple(self.bank[bb]["r"]); self.bank[bb]["r"] = []
                    eo = self.mm(ps[bb][0:65, :], self.expS[pb:pb + 64, hh, 0:65],
                                 self.qf[pb:pb + 64, qc * TC:(qc + 1) * TC], True, True,
                                 deps=war + (e_kc[hh],) + tuple(ev_q))
                    self.bank[bb]["w"] = eo
                    self.rdep(f"kvsb{hh}", eo)
                    ea = self.op("ve", lambda v, _bb=bb: v.tensor_scalar(
                        self.rows[0:1, 0:512], ps[_bb][64:65, :], 1e-6, None, ALU.add),
                        deps=(eo,))
                    erz = self.op("ve", lambda v: v.reciprocal(
                        self.rows[0:1, 1024:1536], self.rows[0:1, 0:512]), deps=(ea,))
                    ebz = self.bcast(4, self.rows[0:1, 1024:1536], 512, nparts=64,
                                     deps=(erz,))
                    ecp = self.op("se", lambda s, _bb=bb: s.activation(
                        self.oevac[0:64, :], ps[_bb][0:64, :], AF.Copy),
                        deps=(eo,) + self.wdeps("oevac"))
                    self.bank_read(bb, ecp)
                    emu = self.op("ve", lambda v, _pb=pb, _qc=qc: v.tensor_mul(
                        self.outTg[_pb:_pb + 64, _qc * TC:(_qc + 1) * TC],
                        self.oevac[0:64, :], ps[4][0:64, :]), deps=(ebz, ecp))
                    self.bank_read(4, emu)
                    self.rdep("oevac", emu)
                    e_out.append(emu)
            if cut & 16:
                hx_stage.append(stage_outTg_to_hx(g, e_out))
        if cut & 16:
            e_hx = self.coll("AllGather", ALU.bypass, self.PAIRS, [self.hx_in.ap()],
                             [self.hx_out.ap()], deps=tuple(hx_stage))
            out_proj_lnout([din["l2_wout"].ap()[o] for o in range(NI)],
                           din["l2_bout"], e_hx)

    def _col_to_scalar(col_ap, dst_off, deps):
        e1 = self.dma("gp", self.rows[0:1, 1536:1664], col_ap, deps=deps)
        e2 = self.op("ve", lambda v: v.reduce_max(
            self.rows[0:1, dst_off:dst_off + 1], self.rows[0:1, 1536:1664], AX),
            deps=(e1,))
        return e2

    def _armx(base, k, deps):
        e1 = self.dma("gp", self.armx_in.ap()[0:1, 0:k], self.rows[0:1, base:base + k],
                      deps=deps)
        e2 = self.coll("AllReduce", ALU.max, self.ALL8, [self.armx_in.ap()],
                       [self.armx_out.ap()], deps=(e1,))
        e3 = self.dma("gp", self.rows[0:1, base:base + k],
                      self.armx_out.ap()[0:1, 0:k], deps=(e2,))
        return e3

    def _mk_scale(src_off, qmax, s_slot, rs_slot, deps):
        e1 = self.op("ve", lambda v: v.tensor_scalar(
            self.rows[0:1, 1700:1701], self.rows[0:1, src_off:src_off + 1],
            1.0 / qmax, 1e-12, ALU.mult, ALU.max), deps=deps)
        e2 = self.op("ve", lambda v: v.reciprocal(
            self.rows[0:1, 1701:1702], self.rows[0:1, 1700:1701]), deps=(e1,))
        es = self.bscal(s_slot, self.rows[0:1, 1700:1701], deps=(e1,))
        er = self.bscal(rs_slot, self.rows[0:1, 1701:1702], deps=(e2,))
        return (es, er)

    def ffn2():
        f2 = F2CUT
        fe = self.fence("te", "ve", "se")
        e_b4 = self.dma("gp", self.blk4[:], dco["blkind4"].ap(), deps=fe)
        e_z = self.op("ve", lambda v: v.memset(self.rows[0:1, 1696:1704], 0.0), deps=fe)
        e_az = self.dma("gp", self.armx_in.ap(), self.rows[0:1, 1696:1704], deps=(e_z,))

        dq_done = {}
        for wname, dsrc, dsrc_t, dq, O, I_ in () if not (f2 & 1) else (
                ("wu", din["l2_wu"], din["l2_wu_t"], self.wqd["wu"], 3072, 1024),
                ("wg", din["l2_wg"], din["l2_wg_t"], self.wqd["wg"], 3072, 1024),
                ("wd", din["l2_wd"], din["l2_wd_t"], self.wqd["wd"], 1024, 3072)):
            nbI = I_ // 32
            src = dsrc.ap().rearrange("(t p) i -> t p i", p=128)
            e_id = self.dma("gp", self.wt2[:, 0:128], dco["ident_f32"].ap(),
                            deps=self.wdeps("wt2"))
            for ot in range(O // 128):
                eh = []
                for half in range(I_ // 512):
                    wt = self.wt_a if half % 2 == 0 else self.wt_b
                    wn = "wt_a" if half % 2 == 0 else "wt_b"
                    e_l = self.dma("sy", wt[:], src[ot][:, half * 512:half * 512 + 512],
                                   deps=self.wdeps(wn) + fe)
                    e_r = self.op("ve", lambda v, _wt=wt, _h=half: v.reduce_max(
                        self.smx[:, _h * 16:_h * 16 + 16],
                        _wt[:].rearrange("p (b e) -> p b e", e=32), AX,
                        apply_absolute_value=True),
                        deps=(e_l,) + (self.wdeps("smx") if half == 0 else ()))
                    self.rdep(wn, e_r)
                    eh.append(e_r)
                e_s = self.op("ve", lambda v, _nb=nbI: v.tensor_scalar(
                    self.sS[:, 0:_nb], self.smx[:, 0:_nb], 1.0 / 7.0, 1e-12,
                    ALU.mult, ALU.max), deps=tuple(eh) + self.wdeps("sS"))
                self.rdep("smx", e_s)
                war = tuple(self.bank[6]["r"]); self.bank[6]["r"] = []
                e_t = self.transpose(ps[6][0:nbI, 0:128], self.sS[:, 0:nbI],
                                     deps=war + (e_s, e_id), ident=self.wt2[:, 0:128])
                self.bank[6]["w"] = e_t
                self.rdep("wt2", e_t)
                self.rdep("sS", e_t)
                e_st = self.op("ve", lambda v, _nb=nbI, _ot=ot: v.tensor_copy(
                    self.sT[0:_nb, _ot * 128:(_ot + 1) * 128], ps[6][0:_nb, 0:128]),
                    deps=(e_t,) + (self.wdeps("sT") if ot == 0 else ()))
                self.bank_read(6, e_st)
            e_sT = self.ev("ve")
            src_t = dsrc_t.ap().rearrange("(t p) o -> t p o", p=128)
            for it in range(I_ // 128):
                for oc in range(O // 512):
                    wt = self.wt_a if oc % 2 == 0 else self.wt_b
                    wn = "wt_a" if oc % 2 == 0 else "wt_b"
                    e_l = self.dma("sy", wt[:], src_t[it][:, oc * 512:oc * 512 + 512],
                                   deps=self.wdeps(wn))
                    e_s4 = self.dma("gp", self.stage4[:],
                                    self.sT[4 * it:4 * it + 4, oc * 512:oc * 512 + 512],
                                    deps=(e_sT,) + self.wdeps("stage4"))
                    self.rdep("sT", e_s4)
                    war = tuple(self.bank[5]["r"]); self.bank[5]["r"] = []
                    e_bc = self.mm(ps[5][:, :], self.blk4[:], self.stage4[:],
                                   True, True, deps=war + (e_b4, e_s4))
                    self.bank[5]["w"] = e_bc
                    self.rdep("stage4", e_bc)
                    e_rc = self.op("ve", lambda v: v.reciprocal(self.qscr[:], ps[5][:, :]),
                                   deps=(e_bc,) + self.wdeps("qscr"))
                    self.bank_read(5, e_rc)
                    e_m1 = self.op("ve", lambda v, _wt=wt: v.tensor_tensor(
                        self.wt2[:], _wt[:], self.qscr[:], ALU.mult),
                        deps=(e_l, e_rc) + self.wdeps("wt2"))
                    self.rdep(wn, e_m1)
                    self.rdep("qscr", e_m1)
                    e_m2 = self.op("ve", lambda v: v.tensor_scalar(
                        self.wt2[:], self.wt2[:], R2, None, ALU.add), deps=(e_m1,))
                    e_m2 = self.op("ve", lambda v: v.tensor_scalar(
                        self.wt2[:], self.wt2[:], R2, None, ALU.subtract), deps=(e_m2,))
                    e_m3 = self.op("ve", lambda v: v.tensor_tensor(
                        self.wdq[:], self.wt2[:], ps[5][:, :], ALU.mult),
                        deps=(e_m2,) + self.wdeps("wdq"))
                    self.bank_read(5, e_m3)
                    self.rdep("wt2", e_m3)
                    e_d = self.dma("gp", dq.ap()[4 * oc:4 * oc + 4, it]
                                   .rearrange("o p m -> p o m"),
                                   self.wdq[:].rearrange("p (o m) -> p o m", o=4),
                                   deps=(e_m3,))
                    self.rdep("wdq", e_d)
            dq_done[wname] = self.ev("gp")
            if wname == "wu":
                self.tap("f2wq", dq.ap()[0, 0], deps=(dq_done[wname],))

        if not (f2 & 2):
            return
        e_acc = None
        for tck in range(NTC_T):
            tsl = slice(tck * TC, (tck + 1) * TC)
            dst = self.lnbuf[:, :, tsl]
            e_ln = self.ln_chunk(self.x[:, :, tsl], dst, self.wdeps("lnbuf") + fe)
            e_r1 = self.op("ve", lambda v, _d=dst: v.reduce_max(
                self.colA[:, 8:16], _d, AX, apply_absolute_value=True),
                deps=(e_ln,))
            self.rdep("lnbuf", e_r1)
            e_r2 = self.op("ve", lambda v: v.reduce_max(
                self.colA[:, 0:1], self.colA[:, 8:16], AX), deps=(e_r1,))
            if tck == 0:
                e_acc = self.op("ve", lambda v: v.tensor_copy(
                    self.colA[:, 1:2], self.colA[:, 0:1]), deps=(e_r2,))
            else:
                e_acc = self.op("ve", lambda v: v.tensor_max(
                    self.colA[:, 1:2], self.colA[:, 1:2], self.colA[:, 0:1]),
                    deps=(e_r2,))
        e_sc = _col_to_scalar(self.colA[:, 1:2], 1704, (e_acc,))
        e_ar = _armx(1704, 1, (e_sc, e_az))
        e_sx = _mk_scale(1704, 32767.0, 4, 5, (e_ar,))
        self.tap("f2sx", self.rows[0:1, 1700:1702], deps=e_sx)
        e_xq = []
        for tck in range(NTC_T):
            tsl = slice(tck * TC, (tck + 1) * TC)
            for i in range(NI):
                sl_ap = self.lnbuf[:, i, tsl]
                e1 = self.op("ve", lambda v, _a=sl_ap: v.tensor_scalar(
                    self.qscr[:], _a, self.scl[:, 5:6], R2, ALU.mult, ALU.add),
                    deps=e_sx + self.wdeps("qscr"))
                e2 = self.op("ve", lambda v: v.tensor_scalar(
                    self.qscr[:], self.qscr[:], R2, None, ALU.subtract))
                e3 = self.op("ve", lambda v, _a=sl_ap: v.tensor_scalar(
                    _a, self.qscr[:], self.scl[:, 4:5], None, ALU.mult), deps=(e2,))
                self.rdep("qscr", e3)
                e_xq.append(e3)
                if tck == 0 and i == 0:
                    _et = self.tap("f2xq", sl_ap, deps=(e3,))
                    if _et:
                        self.rdep("lnbuf", _et)
        e_xq = tuple(e_xq)

        if not (f2 & 4):
            return
        first_mx = {"up": True, "gate": True}
        for tck in range(NTC_T):
            tsl = slice(tck * TC, (tck + 1) * TC)
            for which, which_w, dq, cacc in (("up", "wu", self.wqd["wu"], 2),
                                             ("gate", "wg", self.wqd["wg"], 4)):
                fd = self.fdr[which]
                for ot in range(24):
                    buf = ot % 2
                    bnk = ot % 2
                    e_w = self.load_w(dq.ap()[ot], buf, NI,
                                      deps=(dq_done[which_w],))
                    its = [(self.wsb[:, buf, i, :], self.lnbuf[:, i, tsl],
                            (e_w,) + e_xq) for i in range(NI)]
                    eo = self.mmg(bnk, ps[bnk][:, :], its)
                    self.rdep(f"wsb{buf}", eo)
                    self.rdep("lnbuf", eo)
                    sb = self.scr1 if ot % 2 == 0 else self.scr2
                    sn = "scr1" if ot % 2 == 0 else "scr2"
                    ee = self.op("se", lambda s, _b=bnk, _sb=sb: s.activation(
                        _sb[:], ps[_b][:, :], AF.Copy), deps=(eo,) + self.wdeps(sn))
                    self.bank_read(bnk, ee)
                    ed = self.dma("sy", fd.ap()[ot, :, tsl], sb[:], deps=(ee,))
                    self.rdep(sn, ed)
                    if tck == 0 and ot == 0:
                        _et = self.tap(f"f2{which}0", sb[:], deps=(ee,))
                        if _et:
                            self.rdep(sn, _et)
                    em = self.op("ve", lambda v, _b=bnk, _c=cacc: v.reduce_max(
                        self.colA[:, _c:_c + 1], ps[_b][:, :], AX,
                        apply_absolute_value=True), deps=(eo, ee))
                    self.bank_read(bnk, em)
                    if first_mx[which]:
                        ema = self.op("ve", lambda v, _c=cacc: v.tensor_copy(
                            self.colA[:, _c + 1:_c + 2], self.colA[:, _c:_c + 1]),
                            deps=(em,))
                        first_mx[which] = False
                    else:
                        ema = self.op("ve", lambda v, _c=cacc: v.tensor_max(
                            self.colA[:, _c + 1:_c + 2], self.colA[:, _c + 1:_c + 2],
                            self.colA[:, _c:_c + 1]), deps=(em,))
        if not (F2P1 & 8):
            return
        e_up = _col_to_scalar(self.colA[:, 3:4], 1705, (self.ev("ve"),))
        e_g = _col_to_scalar(self.colA[:, 5:6], 1706, (e_up,))
        e_ar = _armx(1705, 2, (e_g,))
        e_sup = _mk_scale(1705, 32767.0, 6, 7, (e_ar,))
        self.tap("f2sup", self.rows[0:1, 1700:1702], deps=e_sup)
        e_sg = _mk_scale(1706, 32767.0, 8, 9, e_sup + (self.ev("gp"),))
        self.tap("f2sg", self.rows[0:1, 1700:1702], deps=e_sg)

        if not (f2 & 8):
            return
        first = True
        for tck in range(NTC_T):
            tsl = slice(tck * TC, (tck + 1) * TC)
            for ot in range(24):
                sb = self.scr1 if ot % 2 == 0 else self.scr2
                sn = "scr1" if ot % 2 == 0 else "scr2"
                e_l = self.dma("sy", sb[:], self.fdr["gate"].ap()[ot, :, tsl],
                               deps=self.wdeps(sn))
                e1 = self.op("ve", lambda v, _sb=sb: v.tensor_scalar(
                    self.qscr[:], _sb[:], self.scl[:, 9:10], R2, ALU.mult, ALU.add),
                    deps=(e_l,) + e_sg + self.wdeps("qscr"))
                self.rdep(sn, e1)
                e2 = self.op("ve", lambda v: v.tensor_scalar(
                    self.qscr[:], self.qscr[:], R2, None, ALU.subtract))
                e3 = self.op("ve", lambda v: v.tensor_scalar(
                    self.oevac[:], self.qscr[:], self.scl[:, 8:9], None, ALU.mult),
                    deps=(e2,) + self.wdeps("oevac"))
                e4 = self.op("se", lambda s: s.activation(
                    self.qscr[:], self.oevac[:], AF.Sigmoid), deps=(e3,))
                self.rdep("qscr", e4)
                e5 = self.op("ve", lambda v: v.tensor_scalar(
                    self.qscr[:], self.qscr[:], 65536.0, R2, ALU.mult, ALU.add),
                    deps=(e4,))
                e6 = self.op("ve", lambda v: v.tensor_scalar(
                    self.qscr[:], self.qscr[:], R2, 65535.0, ALU.subtract, ALU.min))
                e7 = self.op("ve", lambda v: v.tensor_tensor(
                    self.oevac[:], self.oevac[:], self.qscr[:], ALU.mult), deps=(e6,))
                self.rdep("qscr", e7)
                e8 = self.op("ve", lambda v: v.tensor_scalar(
                    self.oevac[:], self.oevac[:], 2.0 ** -16, None, ALU.mult))
                if tck == 0 and ot == 0:
                    _et = self.tap("f2g20", self.oevac[:], deps=(e8,))
                    if _et:
                        self.rdep("oevac", _et)
                em = self.op("ve", lambda v: v.reduce_max(
                    self.colA[:, 6:7], self.oevac[:], AX, apply_absolute_value=True),
                    deps=(e8,))
                if first:
                    ema = self.op("ve", lambda v: v.tensor_copy(
                        self.colA[:, 7:8], self.colA[:, 6:7]), deps=(em,))
                    first = False
                else:
                    ema = self.op("ve", lambda v: v.tensor_max(
                        self.colA[:, 7:8], self.colA[:, 7:8], self.colA[:, 6:7]),
                        deps=(em,))
                e9 = self.op("ve", lambda v, _sb=sb: v.tensor_copy(_sb[:], self.oevac[:]),
                             deps=(em,))
                self.rdep("oevac", e9)
                ed = self.dma("sy", self.fdr["gate"].ap()[ot, :, tsl], sb[:], deps=(e9,))
                self.rdep(sn, ed)
        e_sc = _col_to_scalar(self.colA[:, 7:8], 1707, (self.ev("ve"),))
        e_ar = _armx(1707, 1, (e_sc,))
        e_sg2 = _mk_scale(1707, 32767.0, 10, 11, (e_ar,))

        if not (f2 & 16):
            return
        first = True
        for tck in range(NTC_T):
            tsl = slice(tck * TC, (tck + 1) * TC)
            for ot in range(24):
                e_l1 = self.dma("sy", self.scr1[:], self.fdr["gate"].ap()[ot, :, tsl],
                                deps=self.wdeps("scr1"))
                e_l2 = self.dma("sy", self.scr2[:], self.fdr["up"].ap()[ot, :, tsl],
                                deps=self.wdeps("scr2"))
                e1 = self.op("ve", lambda v: v.tensor_scalar(
                    self.qscr[:], self.scr1[:], self.scl[:, 11:12], R2, ALU.mult,
                    ALU.add), deps=(e_l1,) + e_sg2 + self.wdeps("qscr"))
                self.rdep("scr1", e1)
                e2 = self.op("ve", lambda v: v.tensor_scalar(
                    self.qscr[:], self.qscr[:], R2, None, ALU.subtract))
                e3 = self.op("ve", lambda v: v.tensor_scalar(
                    self.oevac[:], self.qscr[:], self.scl[:, 10:11], None, ALU.mult),
                    deps=(e2,) + self.wdeps("oevac"))
                self.rdep("qscr", e3)
                e4 = self.op("ve", lambda v: v.tensor_scalar(
                    self.qscr[:], self.scr2[:], self.scl[:, 7:8], R2, ALU.mult,
                    ALU.add), deps=(e_l2, e3))
                self.rdep("scr2", e4)
                e5 = self.op("ve", lambda v: v.tensor_scalar(
                    self.qscr[:], self.qscr[:], R2, None, ALU.subtract))
                e6 = self.op("ve", lambda v: v.tensor_scalar(
                    self.qscr[:], self.qscr[:], self.scl[:, 6:7], None, ALU.mult))
                e7 = self.op("ve", lambda v: v.tensor_tensor(
                    self.oevac[:], self.oevac[:], self.qscr[:], ALU.mult), deps=(e6,))
                self.rdep("qscr", e7)
                if tck == 0 and ot == 0:
                    _et = self.tap("f2o0", self.oevac[:], deps=(e7,))
                    if _et:
                        self.rdep("oevac", _et)
                em = self.op("ve", lambda v: v.reduce_max(
                    self.colA[:, 12:13], self.oevac[:], AX, apply_absolute_value=True),
                    deps=(e7,))
                if first:
                    ema = self.op("ve", lambda v: v.tensor_copy(
                        self.colA[:, 13:14], self.colA[:, 12:13]), deps=(em,))
                    first = False
                else:
                    ema = self.op("ve", lambda v: v.tensor_max(
                        self.colA[:, 13:14], self.colA[:, 13:14], self.colA[:, 12:13]),
                        deps=(em,))
                e9 = self.op("ve", lambda v: v.tensor_copy(self.scr1[:], self.oevac[:]),
                             deps=(em,) + self.wdeps("scr1"))
                self.rdep("oevac", e9)
                ed = self.dma("sy", self.fdr["up"].ap()[ot, :, tsl], self.scr1[:],
                              deps=(e9,))
                self.rdep("scr1", ed)
        e_sc = _col_to_scalar(self.colA[:, 13:14], 1708, (self.ev("ve"),))
        e_ar = _armx(1708, 1, (e_sc,))
        e_so = _mk_scale(1708, 32767.0, 12, 13, (e_ar,))

        if not (f2 & 32):
            return
        for tck in range(NTC_T):
            tsl = slice(tck * TC, (tck + 1) * TC)
            oq_evs = []
            for ot in range(24):
                e_l = self.dma("sy", self.scr1[:], self.fdr["up"].ap()[ot, :, tsl],
                               deps=self.wdeps("scr1"))
                e1 = self.op("ve", lambda v: v.tensor_scalar(
                    self.qscr[:], self.scr1[:], self.scl[:, 13:14], R2, ALU.mult,
                    ALU.add), deps=(e_l,) + e_so + self.wdeps("qscr"))
                self.rdep("scr1", e1)
                e2 = self.op("ve", lambda v: v.tensor_scalar(
                    self.qscr[:], self.qscr[:], R2, None, ALU.subtract))
                e3 = self.op("ve", lambda v, _ot=ot: v.tensor_scalar(
                    self.hid[:, _ot, :], self.qscr[:], self.scl[:, 12:13], None,
                    ALU.mult), deps=(e2,) + (self.wdeps("hid") if ot == 0 else ()))
                self.rdep("qscr", e3)
                oq_evs.append(e3)
            y_evs = []
            for oto in range(NI):
                bnk = oto % 2
                war = tuple(self.bank[bnk]["r"]); self.bank[bnk]["r"] = []
                eo = None
                for wave in range(2):
                    e_w = self.load_w(self.wqd["wd"].ap()[oto, 12 * wave:12 * wave + 12],
                                      bnk, 12, deps=(dq_done["wd"],))
                    for i in range(12):
                        f = 12 * wave + i
                        eo = self.mm(ps[bnk][:, :], self.wsb[:, bnk, i, :],
                                     self.hid[:, f, :], (wave == 0 and i == 0),
                                     (wave == 1 and i == 11),
                                     deps=(war + (e_w, oq_evs[f])) if (wave == 0 and i == 0)
                                     else (e_w, oq_evs[f]))
                    self.rdep(f"wsb{bnk}", eo)
                self.bank[bnk]["w"] = eo
                self.rdep("hid", eo)
                ec = self.op("se", lambda s, _o=oto, _b=bnk: s.activation(
                    self.cast_c[:, _o, :], ps[_b][:, :], AF.Copy),
                    deps=(eo,) + (self.wdeps("cast_c") if oto == 0 else ()))
                es2 = self.op("se", lambda s, _o=oto, _b=bnk: s.activation(
                    self.sq_c[:, _o, :], ps[_b][:, :], AF.Square),
                    deps=(eo,) + (self.wdeps("sq_c") if oto == 0 else ()))
                self.bank_read(bnk, ec)
                self.bank_read(bnk, es2)
                y_evs += [ec, es2]
            if tck == 0:
                self.tap("f2y0", self.cast_c[:].rearrange("p i t -> p (i t)"),
                         deps=tuple(y_evs))
            self.lnout_residual(tck, y_evs)

    def layer3():
        fe = self.fence("te", "ve", "se")
        e_ag = stage_ln_to_ag(fe)
        e_rope = load_rope(deps=fe)
        self.e_vones = self.op("ve", lambda v: v.memset(self.vtm[:, :, :, 64:66], 0.0),
                               deps=fe)
        self.e_vones = self.op("ve", lambda v: v.memset(self.vtm[:, :, :, 64:65], 1.0))
        e_wqa = self.dma("gp", self.wsb[:, 0, 0:8, 0:64],
                         din["l3_wqa"].ap().rearrange("(i p) r -> p i r", p=128),
                         deps=self.wdeps("wsb0") + fe)
        e_wkva = self.dma("gp", self.wsb[:, 0, 8:16, 0:32],
                          din["l3_wkva"].ap().rearrange("(i p) r -> p i r", p=128))
        e_wqb = self.dma("gp", self.wsb[0:64, 1, 0:4, :],
                         din["l3_wqb"].ap().rearrange("p (i o) -> p i o", o=128),
                         deps=self.wdeps("wsb1") + fe)
        e_wkvb = self.dma("gp", self.wsb[0:32, 1, 4:6, :],
                          din["l3_wkvb"].ap().rearrange("p (i o) -> p i o", o=128))
        e_pre = []
        for cc in range(NTC_S):
            buf = cc % 2
            e_rhs = self.load_rhs(buf, ag_rhs_ap(cc), deps=(e_ag,))
            its = [(self.wsb[:, 0, i, 0:64], self.rhs[:, buf, i, :], (e_rhs, e_wqa))
                   for i in range(NI)]
            ea = self.mmg(3, ps[3][0:64, :], its)
            self.rdep(f"rhs{buf}", ea)
            self.rdep("wsb0", ea)
            eaa = self.op("se", lambda s, _cc=cc: s.activation(
                self.aq[:, _cc * TC:(_cc + 1) * TC], ps[3][0:64, :], AF.Copy),
                deps=(ea,))
            self.bank_read(3, eaa)
            its = [(self.wsb[:, 0, 8 + i, 0:32], self.rhs[:, buf, i, :],
                    (e_rhs, e_wkva)) for i in range(NI)]
            ek = self.mmg(4, ps[4][0:32, :], its)
            self.rdep(f"rhs{buf}", ek)
            self.rdep("wsb0", ek)
            ekk = self.op("se", lambda s, _cc=cc: s.activation(
                self.rowbf[0:32, _cc * TC:(_cc + 1) * TC], ps[4][0:32, :], AF.Copy),
                deps=(ek,) + (self.wdeps("akv") if cc == 0 else ()))
            self.bank_read(4, ekk)
            e_pre += [eaa, ekk]
        e_pre = tuple(e_pre)
        hx_stage = []
        kv_evs = ()
        for g in range(4):
            kvh = g // 2
            evs = []
            for cc in range(NTC_S):
                sl = slice(cc * TC, (cc + 1) * TC)
                war = tuple(self.bank[0]["r"]); self.bank[0]["r"] = []
                eq = self.mm(ps[0][:, :], self.wsb[0:64, 1, g, :], self.aq[0:64, sl],
                             True, True, deps=war + e_pre + (e_wqb,))
                self.bank[0]["w"] = eq
                eeq = self.op("se", lambda s, _sl=sl: s.activation(
                    self.qf[:, _sl], ps[0][:, :], AF.Copy, scale=0.125), deps=(eq,))
                self.bank_read(0, eeq)
                war = tuple(self.bank[0]["r"]); self.bank[0]["r"] = []
                mrq = self.mm(ps[0][:, :], self.ropeRt[:], self.qf[:, sl], True, True,
                              deps=war + (eeq,) + e_rope)
                self.bank[0]["w"] = mrq
                e3 = rope_inplace(self.qf, cc, 0, (eeq,) + e_rope)
                evs.append(e3)
            if g % 2 == 0:
                for cc in range(NTC_S):
                    sl = slice(cc * TC, (cc + 1) * TC)
                    war = tuple(self.bank[1]["r"]); self.bank[1]["r"] = []
                    ek = self.mm(ps[1][0:64, :],
                                 self.wsb[0:32, 1, 4, 64 * kvh:64 * kvh + 64],
                                 self.rowbf[0:32, sl], True, True,
                                 deps=war + e_pre + (e_wkvb,))
                    self.bank[1]["w"] = ek
                    self.rdep("akv", ek)
                    eek = self.op("se", lambda s, _sl=sl: s.activation(
                        self.kf[0:64, _sl], ps[1][0:64, :], AF.Copy), deps=(ek,))
                    eek2 = self.op("se", lambda s, _sl=sl: s.activation(
                        self.kf[64:128, _sl], ps[1][0:64, :], AF.Copy))
                    self.bank_read(1, eek)
                    self.bank_read(1, eek2)
                    war = tuple(self.bank[1]["r"]); self.bank[1]["r"] = []
                    mrk = self.mm(ps[1][:, :], self.ropeRt[:],
                                  self.kf[:, sl], True, True, deps=war + (eek, eek2))
                    self.bank[1]["w"] = mrk
                    e4 = rope_inplace(self.kf, cc, 1, (eek, eek2) + e_rope)
                    evs.append(e4)
                for kt in range(32):
                    war = tuple(self.bank[2]["r"]); self.bank[2]["r"] = []
                    ev_ = self.mm(ps[2][0:128, 0:64],
                                  self.rowbf[0:32, kt * 128:(kt + 1) * 128],
                                  self.wsb[0:32, 1, 5, 64 * kvh:64 * kvh + 64],
                                  True, True, deps=war + e_pre + (e_wkvb,))
                    self.bank[2]["w"] = ev_
                    self.rdep("akv", ev_)
                    evv = self.op("ve", lambda v, _kt=kt: v.tensor_copy(
                        self.vtm[:, _kt, 0, 0:64], ps[2][0:128, 0:64]),
                        deps=(ev_, self.e_vones))
                    self.bank_read(2, evv)
                    evs.append(evv)
                kv_evs = tuple(evs)
            else:
                evs += list(kv_evs)
            e_out = attn_causal_group(g, lambda hh: (64 * hh, 0), tuple(evs))
            hx_stage.append(stage_outTg_to_hx(g, e_out))
        e_hx = self.coll("AllGather", ALU.bypass, self.PAIRS, [self.hx_in.ap()],
                         [self.hx_out.ap()], deps=tuple(hx_stage))
        out_proj_lnout([din["l3_wout"].ap()[o] for o in range(NI)], din["l3_bout"], e_hx)

    def ffn3():
        fe = self.fence("te", "ve", "se")
        e_cw = []
        for ot in range(8):
            e_cw.append(self.dma("gp", self.cnw1[:, ot],
                                 din["l3_cn_w1"].ap()[ot].rearrange("i p o -> p i o"),
                                 deps=fe if ot == 0 else ()))
        for ot in range(4):
            e_cw.append(self.dma("gp", self.cnw2[:, ot],
                                 din["l3_cn_w2"].ap()[ot].rearrange("i p o -> p i o"),
                                 deps=fe if ot == 0 else ()))
        for ot in range(2):
            e_cw.append(self.dma("gp", self.cnw3[:, ot],
                                 din["l3_cn_w3"].ap()[ot].rearrange("i p o -> p i o"),
                                 deps=fe if ot == 0 else ()))
        e_cw.append(self.dma("gp", self.cnw4[:], din["l3_cn_w4"].ap(), deps=fe))
        e_cw.append(self.dma("gp", self.gcw3[:], din["l3_gc_w3"].ap(), deps=fe))
        e_b = self.dma("gp", self.bias_sb[:, 0:8], din["l3_cn_b1"].ap(), deps=fe)
        e_b = self.dma("gp", self.bias_sb[:, 8:12], din["l3_cn_b2"].ap())
        e_b = self.dma("gp", self.bias_sb[:, 12:14], din["l3_cn_b3"].ap())
        e_b = self.dma("gp", self.bias_sb[0:1, 14:15], din["l3_cn_b4"].ap())
        e_b = self.dma("gp", self.bias_sb[:, 16:20], din["l3_gc_b1"].ap())
        e_b = self.dma("gp", self.bias_sb[:, 20:22], din["l3_gc_b2"].ap())
        e_b = self.dma("gp", self.bias_sb[0:1, 22:23], din["l3_gc_b3"].ap())
        e_b = self.dma("gp", self.bias_sb[0:1, 23:24], din["l3_ar"].ap())
        e_cw = tuple(e_cw) + (e_b,)
        e_A = self.op("ve", lambda v: v.tensor_scalar(
            self.colA[0:1, 0:1], self.bias_sb[0:1, 23:24], 2.0, None, ALU.mult),
            deps=(e_b,))
        e_B = self.op("ve", lambda v: v.tensor_scalar(
            self.colA[0:1, 1:2], self.bias_sb[0:1, 23:24], -1.0, 0.1,
            ALU.mult, ALU.add), deps=(e_b,))
        e_acc = None
        for tck in range(NTC_T):
            tsl = slice(tck * TC, (tck + 1) * TC)
            e_ln = self.ln_chunk(self.x[:, :, tsl], self.ln_c[:],
                                 self.wdeps("ln_c") + fe)
            e_r = self.op("ve", lambda v: v.reduce_sum(
                self.colA[:, 16:24], self.ln_c[:], AX), deps=(e_ln,))
            self.rdep("ln_c", e_r)
            if tck == 0:
                e_acc = self.op("ve", lambda v: v.tensor_copy(
                    self.colA[:, 24:32], self.colA[:, 16:24]), deps=(e_r,))
            else:
                e_acc = self.op("ve", lambda v: v.tensor_add(
                    self.colA[:, 24:32], self.colA[:, 24:32], self.colA[:, 16:24]),
                    deps=(e_r,))
        e_d1 = self.dma("gp", self.ctx_in.ap().rearrange("r (p i) -> p (r i)", p=128),
                        self.colA[:, 24:32], deps=(e_acc,))
        e_cc = self.coll("AllReduce", ALU.add, self.PAIRS, [self.ctx_in.ap()],
                         [self.ctx_out.ap()], deps=(e_d1,))
        e_d2 = self.dma("gp", self.colA[:, 24:32],
                        self.ctx_out.ap().rearrange("r (p i) -> p (r i)", p=128),
                        deps=(e_cc,))
        e_cm = self.op("ve", lambda v: v.tensor_scalar(
            self.colA[:, 24:32], self.colA[:, 24:32], 1.0 / S, None, ALU.mult),
            deps=(e_d2,))
        e_cb = self.op("ve", lambda v: v.tensor_copy(self.wnbf[:], self.colA[:, 24:32]),
                       deps=(e_cm,) + self.wdeps("wnbf"))
        gh_evs = []
        for ot in range(4):
            buf = ot % 2
            e_w = self.load_w(din["l3_gc_w1"].ap()[ot], buf, NI, deps=(self.ev("te"),))
            its = [(self.wsb[:, buf, i, :], self.wnbf[:, i:i + 1], (e_w, e_cb))
                   for i in range(NI)]
            eg = self.mmg(buf, ps[buf][0:128, 0:1], its)
            self.rdep(f"wsb{buf}", eg)
            self.rdep("wnbf", eg)
            ee = self.op("se", lambda s, _ot=ot, _b=buf: s.activation(
                self.gsm[:, _ot:_ot + 1], ps[_b][0:128, 0:1], AF.Gelu,
                bias=self.bias_sb[:, 16 + _ot:17 + _ot]),
                deps=(eg,) + (self.wdeps("gsm") if ot == 0 else ()))
            self.bank_read(buf, ee)
            gh_evs.append(ee)
        e_h1b = self.op("ve", lambda v: v.tensor_copy(self.wnf[:, 0:4], self.gsm[:, 0:4]),
                        deps=tuple(gh_evs) + self.wdeps("wnf"))
        self.rdep("gsm", e_h1b)
        e_h1c = self.op("ve", lambda v: v.tensor_copy(self.wnbf[:, 0:4], self.wnf[:, 0:4]),
                        deps=(e_h1b,) + self.wdeps("wnbf"))
        self.rdep("wnf", e_h1c)
        gh2_evs = []
        for ot in range(2):
            buf = ot % 2
            e_w = self.load_w(din["l3_gc_w2"].ap()[ot], buf, 4, deps=(self.ev("te"),))
            its = [(self.wsb[:, buf, i, :], self.wnbf[:, i:i + 1], (e_w, e_h1c))
                   for i in range(4)]
            eg = self.mmg(buf, ps[buf][0:128, 0:1], its)
            self.rdep(f"wsb{buf}", eg)
            self.rdep("wnbf", eg)
            ee = self.op("se", lambda s, _ot=ot, _b=buf: s.activation(
                self.gm2[:, _ot:_ot + 1], ps[_b][0:128, 0:1], AF.Gelu,
                bias=self.bias_sb[:, 20 + _ot:21 + _ot]),
                deps=(eg,) + (self.wdeps("gm2") if ot == 0 else ()))
            self.bank_read(buf, ee)
            gh2_evs.append(ee)
        e_h2b = self.op("ve", lambda v: v.tensor_copy(self.kglob[:, 0:2], self.gm2[:, 0:2]),
                        deps=tuple(gh2_evs) + self.wdeps("kglob"))
        self.rdep("gm2", e_h2b)
        its = [(self.gcw3[:, j:j + 1], self.kglob[:, j:j + 1], (e_h2b, e_cw))
               for j in range(2)]
        its = [(l, r, (e_h2b,) + e_cw) for (l, r, _) in its]
        eg3 = self.mmg(3, ps[3][0:1, 0:1], its)
        self.rdep("kglob", eg3)
        e_gf = self.op("se", lambda s: s.activation(
            self.colA[0:1, 2:3], ps[3][0:1, 0:1], AF.Sigmoid,
            bias=self.bias_sb[0:1, 22:23]), deps=(eg3,))
        self.bank_read(3, e_gf)
        e_gf7 = self.op("ve", lambda v: v.tensor_scalar(
            self.colA[0:1, 3:4], self.colA[0:1, 2:3], 0.7, None, ALU.mult),
            deps=(e_gf,))
        self.tap("f3gf", self.colA[0:1, 0:4], deps=(e_gf7,))
        self.tap("f3ctx", self.colA[:, 24:32], deps=(e_cm,))
        TW = ((0, 4, 1, -1), (1, 5, 1, -1), (2, 6, -1, 1), (3, 7, -1, 1))
        for tck in range(NTC_T):
            tsl = slice(tck * TC, (tck + 1) * TC)
            e_ln = self.ln_chunk(self.x[:, :, tsl], self.ln_c[:],
                                 self.wdeps("ln_c") + self.wdeps("oc32x") +
                                 tuple(self.bufdep.get("oc32", ())),
                                 dst_f32=self.oc32[:])
            self.bufdep["oc32"] = []
            e_lnb, e_lnf = e_ln
            last_evs = (e_lnb, e_lnf)
            for cyc in range(3):
                if cyc > 0:
                    e_mir = self.op("ve", lambda v: v.tensor_copy(
                        self.ln_c[:], self.oc32[:]),
                        deps=last_evs + self.wdeps("ln_c"))
                    e_src = (e_mir,)
                else:
                    e_src = (e_lnb,)
                h_evs = []
                for ot in range(8):
                    buf = ot % 2
                    its = [(self.cnw1[:, ot, i, :], self.ln_c[:, i, :],
                            e_src + e_cw) for i in range(NI)]
                    eo = self.mmg(buf, ps[buf][:, :], its)
                    self.rdep("ln_c", eo)
                    ee = self.op("se", lambda s, _ot=ot, _b=buf: s.activation(
                        self.h1[:, _ot, :], ps[_b][:, :], AF.Gelu,
                        bias=self.bias_sb[:, _ot:_ot + 1]),
                        deps=(eo,) + (self.wdeps("h1") if ot == 0 else ()))
                    self.bank_read(buf, ee)
                    h_evs.append(ee)
                h2_evs = []
                for ot in range(4):
                    buf = ot % 2
                    its = [(self.cnw2[:, ot, i, :], self.h1[:, i, :],
                            (h_evs[i],) + e_cw) for i in range(NI)]
                    eo = self.mmg(buf, ps[buf][:, :], its)
                    self.rdep("h1", eo)
                    ee = self.op("se", lambda s, _ot=ot, _b=buf: s.activation(
                        self.h2[:, _ot, :], ps[_b][:, :], AF.Gelu,
                        bias=self.bias_sb[:, 8 + _ot:9 + _ot]),
                        deps=(eo,) + (self.wdeps("h2") if ot == 0 else ()))
                    self.bank_read(buf, ee)
                    h2_evs.append(ee)
                h3_evs = []
                for ot in range(2):
                    buf = ot % 2
                    its = [(self.cnw3[:, ot, i, :], self.h2[:, i, :],
                            (h2_evs[i],) + e_cw) for i in range(4)]
                    eo = self.mmg(buf, ps[buf][:, :], its)
                    self.rdep("h2", eo)
                    ee = self.op("se", lambda s, _ot=ot, _b=buf: s.activation(
                        self.h3[:, _ot, :], ps[_b][:, :], AF.Gelu,
                        bias=self.bias_sb[:, 12 + _ot:13 + _ot]),
                        deps=(eo,) + (self.wdeps("h3") if ot == 0 else ()))
                    self.bank_read(buf, ee)
                    h3_evs.append(ee)
                its = [(self.cnw4[:, j:j + 1], self.h3[:, j, :],
                        (h3_evs[j],) + e_cw) for j in range(2)]
                etf = self.mmg(3, ps[3][0:1, :], its)
                self.rdep("h3", etf)
                e_tf = self.op("se", lambda s: s.activation(
                    self.rows[0:1, 0:512], ps[3][0:1, :], AF.Sigmoid,
                    bias=self.bias_sb[0:1, 14:15]), deps=(etf,))
                self.bank_read(3, e_tf)
                if tck == 0 and cyc == 0:
                    self.tap("f3tf", self.rows[0:1, 0:512], deps=(e_tf,))
                    self.tap("f3h1", self.h1[:].rearrange("p i t -> p (i t)"),
                             deps=tuple(h_evs))
                    self.tap("f3h2", self.h2[:].rearrange("p i t -> p (i t)"),
                             deps=tuple(h2_evs))
                    self.tap("f3h3", self.h3[:].rearrange("p i t -> p (i t)"),
                             deps=tuple(h3_evs))
                e_c = self.op("ve", lambda v: v.tensor_scalar(
                    self.rows[0:1, 512:1024], self.rows[0:1, 0:512], 0.3, None,
                    ALU.mult), deps=(e_tf, e_gf7))
                e_c = self.op("ve", lambda v: v.tensor_scalar(
                    self.rows[0:1, 512:1024], self.rows[0:1, 512:1024],
                    self.colA[0:1, 3:4], None, ALU.add), deps=(e_c,))
                e_c = self.op("ve", lambda v: v.tensor_scalar(
                    self.rows[0:1, 512:1024], self.rows[0:1, 512:1024],
                    self.colA[0:1, 0:1], None, ALU.mult), deps=(e_c, e_A))
                e_c = self.op("ve", lambda v: v.tensor_scalar(
                    self.rows[0:1, 512:1024], self.rows[0:1, 512:1024],
                    self.colA[0:1, 1:2], None, ALU.add), deps=(e_c, e_B))
                if tck == 0 and cyc == 0:
                    self.tap("f3c", self.rows[0:1, 512:1024], deps=(e_c,))
                ebc = self.bcast(6, self.rows[0:1, 512:1024], 512, deps=(e_c,))
                cyc_evs = []
                for (a, b, sa, sb) in TW:
                    e_t0 = self.op("ve", lambda v, _a=a: v.tensor_copy(
                        self.qscr[:], self.oc32[:, _a, :]),
                        deps=last_evs + self.wdeps("qscr"))
                    e_u1 = self.op("ve", lambda v, _b=b: v.tensor_tensor(
                        self.oevac[:], ps[6][:, :], self.oc32[:, _b, :], ALU.mult),
                        deps=(ebc, e_t0) + self.wdeps("oevac"))
                    self.bank_read(6, e_u1)
                    e_x1 = self.op("ve", lambda v, _a=a, _sa=sa: v.tensor_tensor(
                        self.oc32[:, _a, :], self.oc32[:, _a, :], self.oevac[:],
                        ALU.add if _sa > 0 else ALU.subtract), deps=(e_u1,))
                    self.rdep("oevac", e_x1)
                    e_u2 = self.op("ve", lambda v: v.tensor_tensor(
                        self.oevac[:], ps[6][:, :], self.qscr[:], ALU.mult),
                        deps=(e_x1,) + self.wdeps("oevac"))
                    self.bank_read(6, e_u2)
                    self.rdep("qscr", e_u2)
                    e_x2 = self.op("ve", lambda v, _b=b, _sb=sb: v.tensor_tensor(
                        self.oc32[:, _b, :], self.oc32[:, _b, :], self.oevac[:],
                        ALU.add if _sb > 0 else ALU.subtract), deps=(e_u2,))
                    self.rdep("oevac", e_x2)
                    cyc_evs += [e_x1, e_x2]
                last_evs = tuple(cyc_evs)
            if tck == 0:
                _et = self.tap("f3out", self.oc32[:].rearrange("p i t -> p (i t)"),
                               deps=last_evs)
                if _et:
                    self.rdep("oc32", _et)
            ecc = self.op("ve", lambda v: v.tensor_copy(self.cast_c[:], self.oc32[:]),
                          deps=last_evs + self.wdeps("cast_c"))
            esq = self.op("se", lambda s: s.activation(self.sq_c[:], self.oc32[:],
                                                       AF.Square),
                          deps=last_evs + self.wdeps("sq_c"))
            self.rdep("oc32", ecc)
            self.rdep("oc32", esq)
            e1, e2 = self._ln_stats((ecc, esq))
            self._ln_tail(e1, e2, sub_src=self.oc32[:], resid_tck=tck)

    def program():
        self.dma("gp", self.onesb[:], dco["ones_bf"].ap())
        self.dma("gp", self.onesf[:], dco["ones_f32"].ap())
        self.dma("gp", self.identb[:], dco["ident_bf"].ap())
        self.dma("gp", self.ropeRt[:], dco["ropeR_t"].ap())
        self.dma("gp", self.cmaskb[:], dco["cmask"].ap())
        self.dma("gp", self.dmaskb[:], dco["l1_dmask"].ap())
        e_const = self.ev("gp")
        e_x = self.dma("sy", self.x[:], din["x_in"].ap().rearrange("(i p) t -> p i t", p=128))
        self.rope_loaded = False
        self.h2048 = None
        self.x_last = None
        ensure_h("sy")

        ns = NSUBS
        layer0(e_x, e_const)
        if ns >= 2:
            ffn0()
        if n_layers >= 2 and ns >= 3:
            layer1()
        if n_layers >= 2 and ns >= 4:
            ffn1()
        if n_layers >= 3 and ns >= 5:
            layer2()
        if n_layers >= 3 and ns >= 6:
            ffn2()
        if n_layers >= 4 and ns >= 7:
            layer3()
        if n_layers >= 4 and ns >= 8:
            ffn3()

        e_out = self.dma("sy", self.dout.ap().rearrange("(i p) t -> p i t", p=128),
                         self.x[:], deps=(self.ev("ve"), self.ev("se"), self.ev("te")))
        self.wait_on("sy", e_out)
        self.wait_on("gp", e_out)
        self.wait_on("ve", e_out)

    return program


def kernel(**inputs):
    nc = bass.Bass("TRN2", target_bir_lowering=False)
    staged = [_stage_core(c, inputs) for c in range(8)]
    shapes = {k: v.shape for k, v in staged[0].items()}
    kb = KB(nc, shapes, TAPS)
    program = build_program(kb, N_LAYERS)

    def run_pass(name, handle):
        kb.active = name
        kb.H[name] = handle
        for k in kb.count:
            kb.count[k] = 0
        kb.bufdep = {}
        kb.bank = [dict(w=None, r=[]) for _ in range(8)]
        program()

    blk = kb.blk

    @blk.gpsimd
    def _(g):
        run_pass("gp", g)

    @blk.sync
    def _(s):
        run_pass("sy", s)

    @blk.tensor
    def _(t):
        run_pass("te", t)

    @blk.vector
    def _(v):
        run_pass("ve", v)

    @blk.scalar
    def _(s):
        run_pass("se", s)

    kb.blk.__exit__(None, None, None)
    kb.deadcheck()

    in_maps = [dict(stc) for stc in staged]
    try:
        res = run_bass_kernel_spmd(nc, in_maps, core_ids=list(range(8)))
    finally:
        kb.es.close()
    out = np.zeros((B, S, D), np.float32)
    for c in range(8):
        b, h = c // 2, c % 2
        out[b, 2048 * h:2048 * h + 2048, :] = res.results[c]["out"].T
    if TAPS:
        kernel.tap_results = [
            {n: r[f"tap_{n}"] for n in TAPS} for r in res.results]
    return out
